# version 09
# baseline (speedup 1.0000x reference)
import numpy as np
import ml_dtypes
import concourse.bass as bass
import concourse.mybir as mybir
from concourse.bass_utils import run_bass_kernel_spmd

F32 = mybir.dt.float32
BF16 = mybir.dt.bfloat16
AF = mybir.ActivationFunctionType
OP = mybir.AluOpType
AX = mybir.AxisListType

B, T, E = 4, 2048, 2048
HKV, NG, D = 8, 4, 64
KVD = 512
FH_E, NEXP = 1408, 8
FH_S = 2816
NQ = 1024
NBLK = 8
FT = 16
EPS = 1e-5

OWN_A = [0, 4, 8, 12, 3, 7, 11, 15]
OWN_B = [1, 5, 9, 13, 2, 6, 10, 14]
NKEYS = [256, 768, 1280, 1792, 512, 1024, 1536, 2048]

DEBUG = ()
TENSOR_DRAINS = False
NODRAIN_CATS = set("ABCDEF")
PRUNE_WAITS = True


ENGINES = ("sync", "gpsimd", "scalar", "vector", "tensor")


class Sched:
    def __init__(self):
        self.ops = []

    def op(self, engine, fn, reads=(), writes=(), dma=None, nodrain=False):
        assert engine in ENGINES
        self.ops.append(dict(engine=engine, fn=fn, reads=tuple(reads),
                             writes=tuple(writes), dma=dma, barrier=False,
                             nodrain=nodrain))

    def barrier(self):
        self.ops.append(dict(engine=None, fn=None, reads=(), writes=(),
                             dma=None, barrier=True))

    def finalize(self):
        last_writer = {}
        readers = {}
        last_barrier = -1
        n = len(self.ops)
        deps = [set() for _ in range(n)]
        barrier_ids = []
        since_barrier = []
        for i, o in enumerate(self.ops):
            if o["barrier"]:
                deps[i] = set(since_barrier)
                barrier_ids.append(i)
                since_barrier = []
                last_barrier = i
                continue
            dd = set()
            for r in o["reads"]:
                if r in last_writer:
                    dd.add(last_writer[r])
            for w in o["writes"]:
                if w in last_writer:
                    dd.add(last_writer[w])
                for rd in readers.get(w, ()):
                    dd.add(rd)
            if last_barrier >= 0:
                dd.add(last_barrier)
            deps[i] = {d for d in dd if d >= last_barrier}
            for w in o["writes"]:
                last_writer[w] = i
                readers[w] = []
            for r in o["reads"]:
                readers.setdefault(r, []).append(i)
            since_barrier.append(i)

        emit_ops = []
        for i, o in enumerate(self.ops):
            if o["barrier"]:
                emit_ops.append(dict(kind="barrier", idx=i))
            else:
                emit_ops.append(dict(kind="op", idx=i, **o))

        ord_of = {}
        eng_count = {e: 0 for e in ENGINES}
        for eo in emit_ops:
            i = eo["idx"]
            if eo["kind"] == "barrier":
                for e in ENGINES:
                    eng_count[e] += 1
                    ord_of[(i, e)] = eng_count[e]
            else:
                e = eo["engine"]
                eng_count[e] += 1
                ord_of[(i, e)] = eng_count[e]

        self.inc_needed = set()
        for i, o in enumerate(self.ops):
            if not o["barrier"] and o["dma"] is not None:
                self.inc_needed.add(i)
        for i, o in enumerate(self.ops):
            my_engines = set(ENGINES) if o["barrier"] else {o["engine"]}
            for d in deps[i]:
                od = self.ops[d]
                if od["barrier"]:
                    continue
                if od["dma"] is not None:
                    self.inc_needed.add(d)
                elif od["engine"] not in my_engines or o["barrier"]:
                    self.inc_needed.add(d)
        for b in barrier_ids:
            self.inc_needed.add(b)
        self.dma_issuer = {}
        for o in self.ops:
            if not o["barrier"] and o["dma"] is not None:
                prev = self.dma_issuer.setdefault(o["dma"], o["engine"])
                assert prev == o["engine"], f"dma key {o['dma']} on two engines"

        inc_ord = {}
        eng_inc = {e: 0 for e in ENGINES}
        dma_inc = {}
        self.dma_keys = []
        self.barrier_dma_snap = {}
        for eo in emit_ops:
            i = eo["idx"]
            if eo["kind"] == "barrier":
                self.barrier_dma_snap[i] = dict(dma_inc)
                for e in ENGINES:
                    eng_inc[e] += 1
                inc_ord[i] = ("barrier", None, {e: eng_inc[e] for e in ENGINES})
            else:
                o = self.ops[i]
                if i in self.inc_needed:
                    if o["dma"] is not None:
                        k = o["dma"]
                        if k not in dma_inc:
                            dma_inc[k] = 0
                            self.dma_keys.append(k)
                        dma_inc[k] += 16
                        inc_ord[i] = ("dma", k, dma_inc[k])
                    else:
                        e = o["engine"]
                        eng_inc[e] += 1
                        inc_ord[i] = ("eng", e, eng_inc[e])
        self.deps = deps
        self.inc_ord = inc_ord
        self.eng_inc_total = eng_inc
        self.barrier_set = set(barrier_ids)

    def plan(self):
        n = len(self.ops)
        ordv = [None] * n
        cnt = {e: 0 for e in ENGINES}
        for i, o in enumerate(self.ops):
            if o["barrier"]:
                for e in ENGINES:
                    cnt[e] += 1
                ordv[i] = dict(cnt)
            else:
                e = o["engine"]
                cnt[e] += 1
                ordv[i] = cnt[e]
        clock = [None] * n
        self.plan_waits = [[] for _ in range(n)]
        self.plan_drain = [False] * n
        last_op_on = {e: None for e in ENGINES}
        last_dma_on_key = {}
        drained_to = {e: 0 for e in ENGINES}
        ZERO = {e: 0 for e in ENGINES}

        def merge(a, b):
            return {e: max(a[e], b[e]) for e in ENGINES}

        for i, o in enumerate(self.ops):
            if o["barrier"]:
                c = dict(ordv[i])
                clock[i] = c
                for e in ENGINES:
                    last_op_on[e] = i
                    drained_to[e] = ordv[i][e]
                continue
            eng = o["engine"]
            know = dict(ZERO)
            waits = []
            dset = sorted(self.deps[i])
            for d in dset:
                if self.ops[d]["barrier"]:
                    know = merge(know, clock[d])
            for d in dset:
                od = self.ops[d]
                if od["barrier"]:
                    continue
                kind, key, cval = self.inc_ord[d]
                if kind == "dma":
                    if know.get(("dma", key), 0) if False else False:
                        pass
                    waits.append(("dma", key, cval, d))
                    know = merge(know, clock[d])
                elif key == eng:
                    continue
                else:
                    if PRUNE_WAITS and know[key] >= ordv[d]:
                        continue
                    waits.append(("eng", key, cval, d))
                    know = merge(know, clock[d])
            need_drain = False
            nd = o.get("nodrain", False)
            if isinstance(nd, str):
                nd = nd in NODRAIN_CATS
            if not nd:
                for d in dset:
                    od = self.ops[d]
                    if od["barrier"] or od["engine"] != eng or od["dma"] is not None:
                        continue
                    if know[eng] < ordv[d] and drained_to[eng] < ordv[d]:
                        need_drain = True
            if need_drain:
                p = last_op_on[eng]
                if p is not None:
                    know = merge(know, clock[p])
                know[eng] = max(know[eng], ordv[i] - 1)
                drained_to[eng] = ordv[i] - 1
            self.plan_waits[i] = waits
            self.plan_drain[i] = need_drain
            c = dict(know)
            if o["dma"] is not None:
                p = last_dma_on_key.get(o["dma"])
                if p is not None:
                    c = merge(c, clock[p])
                last_dma_on_key[o["dma"]] = i
            else:
                p = last_op_on[eng]
                if p is not None:
                    c = merge(c, clock[p])
                c[eng] = max(c[eng], ordv[i])
                last_op_on[eng] = i
            clock[i] = c

    def emit(self, engine, eng_obj, eng_sems, dma_sems):
        waited = {}

        def wait(kind, key, val):
            cur = waited.get((kind, key), 0)
            if val <= cur:
                return
            waited[(kind, key)] = val
            sem = eng_sems[key] if kind == "eng" else dma_sems[key]
            eng_obj.wait_ge(sem, val)

        for i, o in enumerate(self.ops):
            if o["barrier"]:
                kind, _, totals = self.inc_ord[i]
                for k, tot in self.barrier_dma_snap[i].items():
                    if self.dma_issuer.get(k) == engine and tot > 0:
                        wait("dma", k, tot)
                eng_obj.drain().then_inc(eng_sems[engine], 1)
                for e in ENGINES:
                    if e == engine:
                        continue
                    wait("eng", e, totals[e])
                continue
            if o["engine"] != engine:
                continue
            for kind, key, cval, d in self.plan_waits[i]:
                wait(kind, key, cval)
            if self.plan_drain[i]:
                eng_obj.drain()
            instr = o["fn"]()
            info = self.inc_ord.get(i)
            if info is not None:
                kind, key, cnt = info
                if kind == "dma":
                    instr.then_inc(dma_sems[key], 16)
                else:
                    instr.then_inc(eng_sems[key], 1)


class Arena:

    def __init__(self, handle, nbytes):
        self.h = handle
        self.nbytes = nbytes
        self.free_list = [(0, nbytes)]
        self.marks = []
        self.peak = 0

    def alloc(self, nbytes, align=32):
        nbytes = (int(nbytes) + align - 1) // align * align
        if not hasattr(self, "live"):
            self.live = {}
        for i, (off, size) in enumerate(self.free_list):
            aoff = (off + align - 1) // align * align
            pad = aoff - off
            if size - pad >= nbytes:
                rest = size - pad - nbytes
                new = []
                if pad:
                    new.append((off, pad))
                if rest:
                    new.append((aoff + nbytes, rest))
                self.free_list[i:i + 1] = new
                used = self.nbytes - sum(s for _, s in self.free_list)
                self.peak = max(self.peak, used)
                self.live[aoff] = nbytes
                return aoff
        raise AssertionError(
            f"SBUF arena overflow: want {nbytes}, free={self.free_list}")

    def free(self, off, nbytes, align=32):
        nbytes = (int(nbytes) + align - 1) // align * align
        assert self.live.get(off) == nbytes, (
            f"bad free: off={off} size={nbytes} live={self.live.get(off)}")
        del self.live[off]
        self.free_list.append((off, nbytes))
        self.free_list.sort()
        merged = []
        for o, sz in self.free_list:
            if merged and merged[-1][0] + merged[-1][1] == o:
                merged[-1] = (merged[-1][0], merged[-1][1] + sz)
            else:
                merged.append((o, sz))
        self.free_list = [tuple(x) for x in merged]

    def mark(self):
        pass

    def release(self):
        self.cur = self.marks.pop()

    def view(self, off, p0, p1, cols, dtype):
        esz = mybir.dt.size(dtype)
        return self.h[p0:p1, off:off + cols * esz].bitcast(dtype)


def vslice(arena, off, dtype, p0, p1, c0, c1):
    esz = mybir.dt.size(dtype)
    return arena.h[p0:p1, off + c0 * esz: off + c1 * esz].bitcast(dtype)


def build_program():
    nc = bass.Bass("TRN2", target_bir_lowering=False)

    hx = nc.dram_tensor("x_fm", [E, T], F32, kind="ExternalInput")
    dx = hx.ap()
    hxo = nc.dram_tensor("x_own_fm", [E, NQ], F32, kind="ExternalInput")
    dxo = hxo.ap()
    dWq = nc.dram_tensor("Wq", [E, E], F32, kind="ExternalInput").ap()
    dWk = nc.dram_tensor("Wk", [KVD, E], F32, kind="ExternalInput").ap()
    dWv = nc.dram_tensor("Wv", [KVD, E], F32, kind="ExternalInput").ap()
    dWo = nc.dram_tensor("Wo", [E, E], F32, kind="ExternalInput").ap()
    hWg_r = nc.dram_tensor("W_g", [E, NEXP], F32, kind="ExternalInput")
    hWg = nc.dram_tensor("Wg", [NEXP, E, FH_E], F32, kind="ExternalInput")
    hWu = nc.dram_tensor("Wu", [NEXP, E, FH_E], F32, kind="ExternalInput")
    hWd = nc.dram_tensor("Wd", [NEXP, FH_E, E], F32, kind="ExternalInput")
    hWsg = nc.dram_tensor("Ws_gate", [E, FH_S], F32, kind="ExternalInput")
    hWsu = nc.dram_tensor("Ws_up", [E, FH_S], F32, kind="ExternalInput")
    hWsd = nc.dram_tensor("Ws_down", [FH_S, E], F32, kind="ExternalInput")
    dbq = nc.dram_tensor("bq_t", [128, FT], F32, kind="ExternalInput").ap()
    dbk = nc.dram_tensor("bk_t", [128, 4], F32, kind="ExternalInput").ap()
    dbv = nc.dram_tensor("bv_t", [128, 4], F32, kind="ExternalInput").ap()
    dbo = nc.dram_tensor("bo_t", [128, FT], F32, kind="ExternalInput").ap()
    dcqc = nc.dram_tensor("cq_cos", [128, NQ], F32, kind="ExternalInput").ap()
    dcqs = nc.dram_tensor("cq_sin", [128, NQ], F32, kind="ExternalInput").ap()
    dckc = nc.dram_tensor("ck_cos", [128, T], F32, kind="ExternalInput").ap()
    dcks = nc.dram_tensor("ck_sin", [128, T], F32, kind="ExternalInput").ap()
    dmaskS = nc.dram_tensor("maskS", [128, NBLK * 256], BF16, kind="ExternalInput").ap()
    dmaskT = nc.dram_tensor("maskT", [128, NBLK * 1024], BF16, kind="ExternalInput").ap()
    dconst = nc.dram_tensor("consts", [128, 1024], F32, kind="ExternalInput").ap()
    dconst2 = nc.dram_tensor("consts2", [8, 1024], F32, kind="ExternalInput").ap()

    hout = nc.dram_tensor("out_fm", [E, NQ], F32, kind="ExternalOutput")
    dout = hout.ap()
    ddbg = {}
    if "ln1" in DEBUG:
        ddbg["ln1"] = nc.dram_tensor("dbg_ln1", [E, T], F32, kind="ExternalOutput").ap()
    if "kpack" in DEBUG:
        ddbg["kpack"] = nc.dram_tensor("dbg_kpack", [128, 4 * T], BF16, kind="ExternalOutput").ap()
        ddbg["vtm"] = nc.dram_tensor("dbg_vtm", [128, 16 * KVD], BF16, kind="ExternalOutput").ap()
    if "qpack" in DEBUG or "qpk" in DEBUG:
        ddbg["qpack"] = nc.dram_tensor("dbg_qpack", [128, 4 * 4096], BF16, kind="ExternalOutput").ap()
    if "qpk" in DEBUG:
        ddbg["qpack2"] = nc.dram_tensor("dbg_qpack2", [128, 4 * 4096], BF16, kind="ExternalOutput").ap()
    if "lnown" in DEBUG:
        ddbg["lnown"] = nc.dram_tensor("dbg_lnown", [128, FT * NQ], BF16, kind="ExternalOutput").ap()
    if "y" in DEBUG:
        ddbg["y"] = nc.dram_tensor("dbg_y", [128, FT * NQ], BF16, kind="ExternalOutput").ap()
    if "xattn" in DEBUG:
        ddbg["xattn"] = nc.dram_tensor("dbg_xattn", [128, FT * NQ], F32, kind="ExternalOutput").ap()
    if "ln2" in DEBUG:
        ddbg["ln2"] = nc.dram_tensor("dbg_ln2", [128, FT * NQ], BF16, kind="ExternalOutput").ap()
    if "comb" in DEBUG:
        ddbg["comb"] = nc.dram_tensor("dbg_comb", [NEXP, NQ], F32, kind="ExternalOutput").ap()

    sch = Sched()

    ARENA_BYTES = 204 * 1024
    arena_cm = nc.sbuf_tensor("arena", [128, ARENA_BYTES], mybir.dt.uint8)
    arena_h = arena_cm.__enter__()
    A = Arena(arena_h, ARENA_BYTES)

    psum_cm = nc.psum_tensor("psum", [128, 4096], F32)
    psum_h = psum_cm.__enter__()

    def PS(bank, c0=0, c1=512, p0=0, p1=128):
        return psum_h[p0:p1, bank * 512 + c0: bank * 512 + c1]

    o_I128b = A.alloc(128 * 2)
    o_I128f = A.alloc(128 * 4)
    o_ones1b = A.alloc(128 * 2)
    o_ones1f = A.alloc(128 * 4)
    o_onescol = A.alloc(32)
    o_bq = A.alloc(FT * 4)
    o_bk = A.alloc(4 * 4)
    o_bv = A.alloc(4 * 4)
    o_bo = A.alloc(FT * 4)

    I128b = A.view(o_I128b, 0, 128, 128, BF16)
    I128f = A.view(o_I128f, 0, 128, 128, F32)
    ones1b = A.view(o_ones1b, 0, 1, 128, BF16)
    ones1f = A.view(o_ones1f, 0, 1, 128, F32)
    onescol16 = A.view(o_onescol, 0, 128, 16, BF16)
    onescol = onescol16[:, 0:1]
    bq_t = A.view(o_bq, 0, 128, FT, F32)
    bk_t = A.view(o_bk, 0, 128, 4, F32)
    bv_t = A.view(o_bv, 0, 128, 4, F32)
    bo_t = A.view(o_bo, 0, 128, FT, F32)

    o_fc = A.alloc(8 * 4)
    fc = A.view(o_fc, 0, 128, 8, F32)
    const_load_later = []
    nc.const_aps.register(F32, 0.0, fc[:, 0:1])
    nc.const_aps.register(F32, EPS, fc[:, 1:2])

    def const_load(name, view, src):
        sch.op("sync", lambda v=view, s=src: nc.sync.dma_start(out=v, in_=s),
               writes=(name,), dma=name)

    const_load("c_I128b", I128b, dconst[:, 0:64].bitcast(BF16))
    const_load("c_I128f", I128f, dconst[:, 64:192])
    const_load("c_ones1b", ones1b, dconst[0:1, 192:256].bitcast(BF16))
    const_load("c_ones1f", ones1f, dconst[0:1, 256:384])
    const_load("c_onescol", onescol16, dconst[:, 384:392].bitcast(BF16))
    const_load("c_fc", fc, dconst[:, 392:400])
    const_load("c_bq", bq_t, dbq)
    const_load("c_bk", bk_t, dbk)
    const_load("c_bv", bv_t, dbv)
    const_load("c_bo", bo_t, dbo)

    o_warm = A.alloc(2048 * 2)
    warm = A.view(o_warm, 0, 128, 2048, BF16)

    def load_act_set(idx):
        return nc.scalar.add_instruction(
            mybir.InstLoadActFuncSet(
                name=nc.get_next_instruction_name(),
                act_func_set_id=idx, ins=[], outs=[]))

    sch.op("gpsimd", lambda: nc.gpsimd.memset(warm, 0.0), writes=("warm",))
    sch.op("scalar", lambda: load_act_set(6), writes=("acttab",))
    for _ in range(4):
        sch.op("scalar", lambda: nc.scalar.activation(
            out=warm, in_=warm, func=AF.Exp),
            reads=("acttab", "c_fc"), writes=("warm",))
    sch.barrier()

    o_BIG = A.alloc(FT * T * 2)
    ln1 = A.view(o_BIG, 0, 128, FT * T, BF16)

    def emit_ln(tag, src_dram, TL, out_view, out_tok_prefix):
        nch = TL // 512
        o_xt = [A.alloc(TL * 4), A.alloc(TL * 4)]
        o_xbf = [A.alloc(TL * 2), A.alloc(TL * 2)]
        o_x2bf = [A.alloc(TL * 2), A.alloc(TL * 2)]
        o_nm = A.alloc(TL * 4); o_rs = A.alloc(TL * 4)
        o_sa = A.alloc(TL * 4); o_sb2 = A.alloc(TL * 4)
        nm = A.view(o_nm, 0, 1, TL, F32); rs = A.view(o_rs, 0, 1, TL, F32)
        sa = A.view(o_sa, 0, 1, TL, F32); sb2 = A.view(o_sb2, 0, 1, TL, F32)
        from_sbuf = (src_dram[0] == "sbuf")
        for ft in range(FT):
            r = ft % 2
            xbf = A.view(o_xbf[r], 0, 128, TL, BF16)
            x2bf = A.view(o_x2bf[r], 0, 128, TL, BF16)
            if from_sbuf:
                xt = src_dram[1][:, ft * TL:(ft + 1) * TL]
                xtok = src_dram[2]
            else:
                xt = A.view(o_xt[r], 0, 128, TL, F32)
                xtok = f"{tag}xt{r}"
                sch.op("sync", lambda v=xt, f=ft, sd=src_dram[1]:
                       nc.sync.dma_start(out=v, in_=sd[f * 128:(f + 1) * 128, :]),
                       writes=(xtok,), dma=xtok)
            sch.op("scalar", lambda o=xbf, i=xt: nc.scalar.copy(out=o, in_=i),
                   reads=(xtok,), writes=(f"{tag}xbf{r}",))
            sch.op("scalar", lambda o=x2bf, i=xt:
                   nc.scalar.activation(out=o, in_=i, func=AF.Square),
                   reads=(xtok, "acttab"), writes=(f"{tag}x2bf{r}",))
            for cb in range(nch):
                for sti, (srcv, tok) in enumerate(
                        ((xbf, f"{tag}xbf{r}"), (x2bf, f"{tag}x2bf{r}"))):
                    bank = sti * nch + cb
                    sch.op("tensor",
                           lambda b=bank, sv=srcv, c=cb, f=ft:
                           nc.tensor.matmul(PS(b, 0, 512, 0, 1),
                                            onescol, sv[:, c * 512:(c + 1) * 512],
                                            start=(f == 0), stop=(f == FT - 1)),
                           reads=(tok, "c_onescol"), writes=(f"ps{bank}",),
                           nodrain=('A' if ft != 0 else False))
        for cb in range(nch):
            sch.op("vector", lambda c=cb, v=nm: nc.vector.tensor_scalar(
                out=v[:, c * 512:(c + 1) * 512], in0=PS(c, 0, 512, 0, 1),
                scalar1=-1.0 / E, scalar2=None, op0=OP.mult),
                reads=(f"ps{cb}",), writes=(f"{tag}nm",))
            sch.op("vector", lambda c=cb, v=sa, n=nch: nc.vector.tensor_scalar(
                out=v[:, c * 512:(c + 1) * 512], in0=PS(n + c, 0, 512, 0, 1),
                scalar1=1.0 / E, scalar2=None, op0=OP.mult),
                reads=(f"ps{nch+cb}",), writes=(f"{tag}sa",))
        sch.op("vector", lambda a=sb2, b=nm: nc.vector.tensor_tensor(
            out=a, in0=b, in1=b, op=OP.mult),
            reads=(f"{tag}nm",), writes=(f"{tag}sb",))
        sch.op("vector", lambda a=sa, b=sb2: nc.vector.tensor_tensor(
            out=a, in0=a, in1=b, op=OP.subtract),
            reads=(f"{tag}sa", f"{tag}sb"), writes=(f"{tag}sa",))
        sch.op("scalar", lambda a=sb2, b=sa: nc.scalar.activation(
            out=a, in_=b, func=AF.Ln, bias=EPS),
            reads=(f"{tag}sa", "acttab"), writes=(f"{tag}sb",))
        sch.op("scalar", lambda a=rs, b=sb2: nc.scalar.activation(
            out=a, in_=b, func=AF.Exp, scale=-0.5),
            reads=(f"{tag}sb",), writes=(f"{tag}rs",))
        nhalf = TL // 1024
        for half in range(nhalf):
            for cb in range(2):
                c0 = half * 1024 + cb * 512
                bnm = half * 2 + cb
                brs = 2 * nhalf + half * 2 + cb
                sch.op("tensor", lambda b=bnm, c=c0, v=nm: nc.tensor.matmul(
                    PS(b), ones1f, v[:, c:c + 512], start=True, stop=True),
                    reads=(f"{tag}nm", "c_ones1f"), writes=(f"ps{bnm}",),
                    nodrain="F")
                sch.op("tensor", lambda b=brs, c=c0, v=rs: nc.tensor.matmul(
                    PS(b), ones1f, v[:, c:c + 512], start=True, stop=True),
                    reads=(f"{tag}rs", "c_ones1f"), writes=(f"ps{brs}",),
                    nodrain="F")
        o_tmpf = [A.alloc(1024 * 4), A.alloc(1024 * 4)]
        for ft in range(FT):
            r = ft % 2
            if from_sbuf:
                xt = src_dram[1][:, ft * TL:(ft + 1) * TL]
                xtok = src_dram[2]
            else:
                xt = A.view(o_xt[r], 0, 128, TL, F32)
                xtok = f"{tag}xt{r}"
                sch.op("sync", lambda v=xt, f=ft, sd=src_dram[1]:
                       nc.sync.dma_start(out=v, in_=sd[f * 128:(f + 1) * 128, :]),
                       writes=(xtok,), dma=xtok)
            for half in range(nhalf):
                tmpf = A.view(o_tmpf[r], 0, 128, 1024, F32)
                hb = half * 2
                rb = 2 * nhalf + hb
                sch.op("vector", lambda v=xt, h=half, t=tmpf, b=hb:
                       nc.vector.tensor_tensor(
                           out=t, in0=v[:, h * 1024:(h + 1) * 1024],
                           in1=psum_h[:, b * 512:(b + 2) * 512], op=OP.add),
                       reads=(xtok, f"ps{hb}", f"ps{hb+1}"),
                       writes=(f"{tag}tmp{r}",))
                sch.op("vector", lambda t=tmpf, f=ft, h=half, b=rb, ov=out_view:
                       nc.vector.tensor_tensor(
                           out=ov[:, f * TL + h * 1024: f * TL + (h + 1) * 1024],
                           in0=t, in1=psum_h[:, b * 512:(b + 2) * 512], op=OP.mult),
                       reads=(f"{tag}tmp{r}", f"ps{rb}", f"ps{rb+1}"),
                       writes=(f"{out_tok_prefix}{ft}",))
        sch.barrier()
        for off in o_xt:
            A.free(off, TL * 4)
        for off in o_xbf + o_x2bf:
            A.free(off, TL * 2)
        A.free(o_tmpf[0], 1024 * 4); A.free(o_tmpf[1], 1024 * 4)
        A.free(o_nm, TL * 4); A.free(o_rs, TL * 4)
        A.free(o_sa, TL * 4); A.free(o_sb2, TL * 4)

    emit_ln("l1", ("dram", dx), T, ln1, "ln1_")

    if "ln1" in DEBUG:
        o_dbg = A.alloc(T * 4)
        dbgt = A.view(o_dbg, 0, 128, T, F32)
        for ft in range(FT):
            sch.op("vector", lambda f=ft: nc.vector.tensor_copy(
                dbgt, ln1[:, f * T:(f + 1) * T]),
                reads=(f"ln1_{ft}",), writes=("dbgt",))
            sch.op("sync", lambda f=ft: nc.sync.dma_start(
                out=ddbg["ln1"][f * 128:(f + 1) * 128, :], in_=dbgt),
                reads=("dbgt",), writes=(f"dbgo{ft}",), dma="dbg")
            sch.barrier()
        A.free(o_dbg, T * 4)

    o_wb = [A.alloc(T * 2), A.alloc(T * 2)]
    o_strip = [A.alloc(T * 2), A.alloc(T * 2)]

    def emit_strip(r, src_rows_ap, tbank):
        wb = A.view(o_wb[r], 0, 128, T, BF16)
        strip = A.view(o_strip[r], 0, 128, T, BF16)
        sch.op("gpsimd", lambda v=wb, sr=src_rows_ap:
               nc.gpsimd.dma_start(out=v, in_=sr),
               writes=(f"wb{r}",), dma=f"wb{r}")
        for grp in range(4):
            for ii in range(4):
                fi = grp * 4 + ii
                sch.op("tensor", lambda b=tbank, w=wb, f=fi, i2=ii:
                       nc.tensor.matmul(PS(b, i2 * 128, (i2 + 1) * 128),
                                        w[:, f * 128:(f + 1) * 128], I128b,
                                        start=True, stop=True),
                       reads=(f"wb{r}", "c_I128b"), writes=(f"ps{tbank}",),
                       nodrain="B")
            sch.op("vector", lambda b=tbank, st=strip, g2=grp:
                   nc.vector.tensor_copy(st[:, g2 * 512:(g2 + 1) * 512], PS(b)),
                   reads=(f"ps{tbank}",), writes=(f"strip{r}",))
        return strip

    o_lnown = A.alloc(FT * NQ * 2)
    lnown = A.view(o_lnown, 0, 128, FT * NQ, BF16)
    emit_ln("lo", ("dram", dxo), NQ, lnown, "lno_")

    if "lnown" in DEBUG:
        sch.op("sync", lambda: nc.sync.dma_start(out=ddbg["lnown"], in_=lnown),
               reads=tuple(f"lno_{i}" for i in range(FT)), writes=("dlo",), dma="dbg2")
        sch.barrier()

    o_cqc = A.alloc(NQ * 4); o_cqs = A.alloc(NQ * 4)
    cqc = A.view(o_cqc, 0, 128, NQ, F32)
    cqs = A.view(o_cqs, 0, 128, NQ, F32)
    const_load("c_cqc", cqc, dcqc)
    const_load("c_cqs", cqs, dcqs)
    o_Qg = A.alloc(16 * NQ * 2)
    Qg = A.view(o_Qg, 0, 128, 16 * NQ, BF16)
    o_qb = A.alloc(NQ * 4); o_x2q = A.alloc(NQ * 4)
    qb = A.view(o_qb, 0, 128, NQ, F32)
    x2q = A.view(o_x2q, 0, 128, NQ, F32)
    for fo in range(FT):
        r = fo % 2
        strip = emit_strip(r, dWq[fo * 128:(fo + 1) * 128, :], r)
        qpb = (2 + 2 * r, 3 + 2 * r)
        ptoks = (f"ps{qpb[0]}", f"ps{qpb[1]}")
        for fi in range(FT):
            for cb in range(2):
                sch.op("tensor", lambda b=qpb[cb], st=strip, f=fi, c=cb,
                       fr=(fi == 0), la=(fi == FT - 1):
                       nc.tensor.matmul(PS(b), st[:, f * 128:(f + 1) * 128],
                                        lnown[:, f * NQ + c * 512: f * NQ + c * 512 + 512],
                                        start=fr, stop=la),
                       reads=(f"strip{r}", f"lno_{fi}"), writes=(ptoks[cb],),
                       nodrain=('A' if fi != 0 else False))
        sch.op("vector", lambda b=qpb[0], f=fo: nc.vector.tensor_scalar(
            out=qb, in0=psum_h[:, b * 512:(b + 2) * 512],
            scalar1=bq_t[:, f:f + 1], scalar2=None, op0=OP.add),
            reads=ptoks + ("c_bq",), writes=("qb",))
        for (d0, d1, sg0, sg1, sgn) in ((0, 32, 32, 64, -1.0), (32, 64, 0, 32, 1.0),
                                        (64, 96, 96, 128, -1.0), (96, 128, 64, 96, 1.0)):
            sch.op("vector", lambda a=d0, b=d1, c=sg0, d=sg1, s=sgn:
                   nc.vector.tensor_scalar(
                       out=x2q[a:b, :], in0=qb[c:d, :], scalar1=s,
                       scalar2=None, op0=OP.mult),
                   reads=("qb",), writes=("x2q",))
        sch.op("vector", lambda: nc.vector.tensor_tensor(
            out=qb, in0=qb, in1=cqc, op=OP.mult),
            reads=("qb", "c_cqc"), writes=("qb",))
        sch.op("vector", lambda: nc.vector.tensor_tensor(
            out=x2q, in0=x2q, in1=cqs, op=OP.mult),
            reads=("x2q", "c_cqs"), writes=("x2q",))
        for half in range(2):
            sch.op("vector", lambda h=half, f=fo: nc.vector.tensor_tensor(
                out=Qg[64 * h:64 * h + 64, f * NQ:(f + 1) * NQ],
                in0=qb[64 * h:64 * h + 64, :], in1=x2q[64 * h:64 * h + 64, :],
                op=OP.add),
                reads=("qb", "x2q"), writes=("qg",))
    sch.barrier()
    A.free(o_lnown, FT * NQ * 2)
    A.free(o_qb, NQ * 4); A.free(o_x2q, NQ * 4)
    A.free(o_cqc, NQ * 4); A.free(o_cqs, NQ * 4)
    if "qpack" in DEBUG or "qpk" in DEBUG:
        sch.op("sync", lambda: nc.sync.dma_start(out=ddbg["qpack"], in_=Qg),
               reads=("qg",), writes=("dqp",), dma="dbg2")
        sch.barrier()

    o_ckc = A.alloc(T * 4); o_cks = A.alloc(T * 4)
    ckc = A.view(o_ckc, 0, 128, T, F32)
    cks = A.view(o_cks, 0, 128, T, F32)
    const_load("c_ckc", ckc, dckc)
    const_load("c_cks", cks, dcks)
    o_K = A.alloc(4 * T * 2)
    o_Ksw = A.alloc(4 * T * 2)
    kpk = A.view(o_K, 0, 128, 4 * T, BF16)
    ksw = A.view(o_Ksw, 0, 128, 4 * T, BF16)
    o_V = A.alloc(16 * KVD * 2)
    vtm = A.view(o_V, 0, 128, 16 * KVD, BF16)
    o_kb = A.alloc(T * 4); o_x2k = A.alloc(T * 4)
    kb = A.view(o_kb, 0, 128, T, F32)
    x2k = A.view(o_x2k, 0, 128, T, F32)
    o_vbf = A.alloc(T * 2)
    vbf = A.view(o_vbf, 0, 128, T, BF16)
    kvp = (4, 5, 6, 7)
    kvptoks = tuple(f"ps{b}" for b in kvp)
    for m, dW, bias_t, btok in ((("k", dWk, bk_t, "c_bk"),) if "konly" in DEBUG else (("k", dWk, bk_t, "c_bk"), ("v", dWv, bv_t, "c_bv"))):
        for fv in range(4):
            r = fv % 2
            strip = emit_strip(r, dW[fv * 128:(fv + 1) * 128, :], r)
            for fi in range(FT):
                for cb in range(4):
                    sch.op("tensor", lambda b=kvp[cb], st=strip, f=fi, c=cb,
                           fr=(fi == 0), la=(fi == FT - 1):
                           nc.tensor.matmul(PS(b), st[:, f * 128:(f + 1) * 128],
                                            ln1[:, f * T + c * 512: f * T + c * 512 + 512],
                                            start=fr, stop=la),
                           reads=(f"strip{r}", f"ln1_{fi}"), writes=(f"ps{kvp[cb]}",),
                           nodrain=('A' if fi != 0 else False))
            if m == "k":
                sch.op("vector", lambda f=fv, bt=bias_t: nc.vector.tensor_scalar(
                    out=kb, in0=psum_h[:, 4 * 512: 8 * 512],
                    scalar1=bt[:, f:f + 1], scalar2=None, op0=OP.add),
                    reads=kvptoks + (btok,), writes=("kb",))
                for (d0, d1, sg0, sg1, sgn) in ((0, 32, 32, 64, -1.0), (32, 64, 0, 32, 1.0),
                                                (64, 96, 96, 128, -1.0), (96, 128, 64, 96, 1.0)):
                    sch.op("vector", lambda a=d0, b=d1, c=sg0, d=sg1, s=sgn:
                           nc.vector.tensor_scalar(
                               out=x2k[a:b, :], in0=kb[c:d, :], scalar1=s,
                               scalar2=None, op0=OP.mult),
                           reads=("kb",), writes=("x2k",))
                sch.op("vector", lambda: nc.vector.tensor_tensor(
                    out=kb, in0=kb, in1=ckc, op=OP.mult),
                    reads=("kb", "c_ckc"), writes=("kb",))
                sch.op("vector", lambda: nc.vector.tensor_tensor(
                    out=x2k, in0=x2k, in1=cks, op=OP.mult),
                    reads=("x2k", "c_cks"), writes=("x2k",))
                sch.op("vector", lambda f=fv: nc.vector.tensor_tensor(
                    out=kpk[:, f * T:(f + 1) * T], in0=kb, in1=x2k, op=OP.add),
                    reads=("kb", "x2k"), writes=("kpack",))
            else:
                sch.op("vector", lambda f=fv, bt=bias_t: nc.vector.tensor_scalar(
                    out=vbf, in0=psum_h[:, 4 * 512: 8 * 512],
                    scalar1=bt[:, f:f + 1], scalar2=None, op0=OP.add),
                    reads=kvptoks + (btok,), writes=("vbf",))
                for ktg in range(4):
                    for ii in range(4):
                        kt = ktg * 4 + ii
                        sch.op("tensor", lambda k=kt, i2=ii:
                               nc.tensor.matmul(PS(2, i2 * 128, (i2 + 1) * 128),
                                                vbf[:, k * 128:(k + 1) * 128], I128b,
                                                start=True, stop=True),
                               reads=("vbf", "c_I128b"), writes=("ps2",),
                               nodrain="B")
                    sch.op("vector", lambda kg=ktg, f=fv: nc.vector.tensor_copy(
                        vtm.rearrange("p (kt v) -> p kt v", v=KVD)[
                            :, kg * 4:(kg + 1) * 4, f * 128:(f + 1) * 128],
                        PS(2).rearrange("p (a b) -> p a b", b=128)),
                        reads=("ps2",), writes=("vtm",))
    if "qpk" in DEBUG:
        sch.barrier()
        sch.op("sync", lambda: nc.sync.dma_start(out=ddbg["qpack2"], in_=Qg),
               reads=("qg",), writes=("dqp2",), dma="dbg3")
        sch.barrier()
    sch.op("sync", lambda: nc.sync.dma_start(out=ksw[0:64, :], in_=kpk[64:128, :]),
           reads=("kpack",), writes=("kswap",), dma="ksw")
    sch.op("sync", lambda: nc.sync.dma_start(out=ksw[64:128, :], in_=kpk[0:64, :]),
           reads=("kpack",), writes=("kswap",), dma="ksw")
    A.free(o_kb, T * 4); A.free(o_x2k, T * 4); A.free(o_vbf, T * 2)
    A.free(o_ckc, T * 4); A.free(o_cks, T * 4)
    sch.barrier()

    if "kpack" in DEBUG:
        sch.op("sync", lambda: nc.sync.dma_start(out=ddbg["kpack"], in_=kpk),
               reads=("kpack",), writes=("dkp",), dma="dbg2")
        sch.op("sync", lambda: nc.sync.dma_start(out=ddbg["vtm"], in_=vtm),
               reads=("vtm",), writes=("dvt",), dma="dbg2")
        sch.barrier()

    A.free(o_BIG, FT * T * 2)
    o_y = A.alloc(16 * NQ * 2)
    yv = A.view(o_y, 0, 128, 16 * NQ, BF16)
    o_maskT = A.alloc(NBLK * 1024 * 2)
    maskT = A.view(o_maskT, 0, 128, NBLK * 1024, BF16)
    o_PT = A.alloc(16 * 512 * 2)
    PT = A.view(o_PT, 0, 128, 16 * 512, BF16)
    o_maskS = A.alloc(NBLK * 256 * 2)
    maskS = A.view(o_maskS, 0, 128, NBLK * 256, BF16)
    const_load("c_maskS", maskS, dmaskS)
    const_load("c_maskT", maskT, dmaskT)
    o_sel4 = A.alloc(512 * 4)
    sel4f = A.view(o_sel4, 0, 4, 512, F32)
    const_load("c_sel4f", sel4f, dconst[0:4, 400:912])
    sch.barrier()
    o_nm4 = A.alloc(4 * 4); o_Z4 = A.alloc(4 * 4)
    o_lnZ = A.alloc(4 * 4); o_nC = A.alloc(4 * 4)
    nm4 = A.view(o_nm4, 0, 128, 4, F32)
    Z4 = A.view(o_Z4, 0, 128, 4, F32)
    lnZ = A.view(o_lnZ, 0, 128, 4, F32)
    nC4 = A.view(o_nC, 0, 128, 4, F32)
    o_nCT = A.alloc(512)
    nCT = A.view(o_nCT, 0, 4, 128, F32)
    o_ncb = A.alloc(512 * 4)
    ncb = A.view(o_ncb, 0, 128, 512, F32)

    o_nm4b = A.alloc(4 * 4); o_Z4b = A.alloc(4 * 4)
    o_lnZb = A.alloc(4 * 4); o_nCb = A.alloc(4 * 4)
    o_nCTb = A.alloc(512); o_ncbb = A.alloc(512 * 4)
    o_cm4a = A.alloc(16 * 4); o_cm4b = A.alloc(16 * 4)
    o_stscr = A.alloc(512 * 4)
    nm4s = (nm4, A.view(o_nm4b, 0, 128, 4, F32))
    cm4s = (A.view(o_cm4a, 0, 128, 16, F32), A.view(o_cm4b, 0, 128, 16, F32))
    stscr = A.view(o_stscr, 0, 128, 512, F32)
    Z4s = (Z4, A.view(o_Z4b, 0, 128, 4, F32))
    lnZs = (lnZ, A.view(o_lnZb, 0, 128, 4, F32))
    nC4s = (nC4, A.view(o_nCb, 0, 128, 4, F32))
    nCTs = (nCT, A.view(o_nCTb, 0, 4, 128, F32))
    ncbs = (ncb, A.view(o_ncbb, 0, 128, 512, F32))
    POSG = {0: 0, 2: 1, 1: 2, 3: 3}

    def emit_attn_S(h, qt, par):
        hc = h // 2
        NK = NKEYS[qt]
        ncb_nch = (NK + 511) // 512
        sptoks = tuple(f"ps{b}" for b in range(ncb_nch))
        nm4v, Z4v, lnZv, nC4v, nCTv, ncbv = (
            nm4s[par], Z4s[par], lnZs[par], nC4s[par], nCTs[par], ncbs[par])
        cm4v = cm4s[par]
        rot = (ncb_nch <= 2)
        for g in range(4):
            gp = g % 2
            base = 64 * gp
            b0 = (g % 2) * 2 if rot else 0
            gptoks = tuple(f"ps{b0 + b}" for b in range(ncb_nch))
            kbuf, ktok = (kpk, "kpack") if (h % 2) == gp else (ksw, "kswap")
            lhsq = Qg[base:base + 64,
                      (2 * h + g // 2) * NQ + qt * 128: (2 * h + g // 2) * NQ + qt * 128 + 128]
            for cb in range(ncb_nch):
                n = min(512, NK - cb * 512)
                sch.op("tensor", lambda b=b0 + cb, lq=lhsq, kv=kbuf, nn=n,
                       c=cb, bs=base, h2=hc:
                       nc.tensor.matmul(
                           PS(b, 0, nn), lq,
                           kv[bs:bs + 64, h2 * T + c * 512: h2 * T + c * 512 + nn],
                           start=True, stop=True),
                       reads=("qg", ktok), writes=(f"ps{b0+cb}",),
                       nodrain="C")
            c0 = b0 * 512
            for cb in range(ncb_nch - 1):
                n = min(512, NK - cb * 512)
                sch.op("vector", lambda b=b0 + cb, g2=g, c=cb, nn=n, v=cm4v:
                       nc.vector.tensor_reduce(
                           out=v[:, g2 * 4 + c:g2 * 4 + c + 1],
                           in_=PS(b, 0, nn), axis=AX.X, op=OP.max),
                       reads=(f"ps{b0+cb}",), writes=(f"cm{g}_{par}",))
            sch.op("vector", lambda q2=qt, nk=NK, cc=c0:
                   nc.vector.tensor_tensor(
                       out=psum_h[:, cc + nk - 256: cc + nk],
                       in0=psum_h[:, cc + nk - 256: cc + nk],
                       in1=maskS[:, q2 * 256:(q2 + 1) * 256], op=OP.add),
                   reads=(f"ps{b0+ncb_nch-1}",) + ((f"ps{b0+ncb_nch-2}",) if NK % 512 == 256 and ncb_nch > 1 else ()),
                   writes=(f"ps{b0+ncb_nch-1}",) + ((f"ps{b0+ncb_nch-2}",) if NK % 512 == 256 and ncb_nch > 1 else ()))
            lb = ncb_nch - 1
            ln_ = NK - lb * 512
            sch.op("vector", lambda b=b0 + lb, g2=g, c=lb, nn=ln_, v=cm4v:
                   nc.vector.tensor_reduce(
                       out=v[:, g2 * 4 + c:g2 * 4 + c + 1],
                       in_=PS(b, 0, nn), axis=AX.X, op=OP.max),
                   reads=(f"ps{b0+lb}",), writes=(f"cm{g}_{par}",))
            sch.op("vector", lambda g2=g, nch=ncb_nch, v=cm4v, o=nm4v:
                   nc.vector.tensor_reduce(
                       out=o[:, g2:g2 + 1], in_=v[:, g2 * 4:g2 * 4 + nch],
                       axis=AX.X, op=OP.max, negate=True),
                   reads=(f"cm{g}_{par}",), writes=(f"nm{g}_{par}",))
            sch.op("scalar", lambda g2=g, nk=NK, v=nm4v, zv=Z4v, cc=c0:
                   nc.scalar.activation(
                       out=psum_h[:, cc:cc + nk], in_=psum_h[:, cc:cc + nk],
                       func=AF.Exp, bias=v[:, g2:g2 + 1],
                       accum_out=zv[:, g2:g2 + 1]),
                   reads=gptoks + (f"nm{g}_{par}", "acttab"),
                   writes=gptoks + (f"Z{g}_{par}",))
        zt = tuple(f"Z{g}_{par}" for g in range(4))
        nmt = tuple(f"nm{g}_{par}" for g in range(4))
        sch.op("scalar", lambda zv=Z4v, lv=lnZv: nc.scalar.activation(
            out=lv, in_=zv, func=AF.Ln),
            reads=zt + ("acttab",), writes=(f"lnZ_{par}",))
        sch.op("vector", lambda nv=nC4v, mv=nm4v, lv=lnZv: nc.vector.tensor_tensor(
            out=nv, in0=mv, in1=lv, op=OP.subtract),
            reads=nmt + (f"lnZ_{par}",), writes=(f"nC_{par}",))

    def emit_negc_tensor(par):
        nC4v, nCTv, ncbv = nC4s[par], nCTs[par], ncbs[par]
        sch.op("tensor", lambda nv=nC4v: nc.tensor.matmul(
            PS(6, 0, 128, 0, 4), nv, I128f, start=True, stop=True),
            reads=(f"nC_{par}", "c_I128f"), writes=("ps6",), nodrain="E")
        sch.op("vector", lambda tv=nCTv: nc.vector.tensor_copy(
            tv, PS(6, 0, 128, 0, 4)),
            reads=("ps6",), writes=(f"nCT_{par}",))
        for g in range(4):
            pos = POSG[g]
            sch.op("tensor", lambda g2=g, p2=pos, tv=nCTv: nc.tensor.matmul(
                PS(6, p2 * 128, (p2 + 1) * 128),
                sel4f[:, g2 * 128:(g2 + 1) * 128], tv,
                start=True, stop=True),
                reads=(f"nCT_{par}", "c_sel4f"), writes=("ps6",), nodrain="E")
        sch.op("vector", lambda bv=ncbv: nc.vector.tensor_copy(bv, PS(6)),
               reads=("ps6",), writes=(f"ncbt_{par}",))

    def emit_attn_STAV(h, qt, par):
        hc = h // 2
        NK = NKEYS[qt]
        nkt = NK // 128
        ncbv = ncbs[par]
        emit_negc_tensor(par)
        for kt in range(nkt):
            pbA = 4 if kt % 2 == 0 else 0
            pbB = pbA + 1
            for g in range(4):
                gp = g % 2
                base = 64 * gp
                pb = pbA if gp == 0 else pbB
                col = (POSG[g] % 2) * 128
                kbuf, ktok = (kpk, "kpack") if (h % 2) == gp else (ksw, "kswap")
                sch.op("tensor", lambda b=pb, c2=col, k=kt, kv=kbuf, bs=base,
                       h2=hc, hh=h, q2=qt, g2=g:
                       nc.tensor.matmul(
                           PS(b, c2, c2 + 128),
                           kv[bs:bs + 64, h2 * T + k * 128: h2 * T + (k + 1) * 128],
                           Qg[bs:bs + 64, (2 * hh + g2 // 2) * NQ + q2 * 128:
                              (2 * hh + g2 // 2) * NQ + q2 * 128 + 128],
                           start=True, stop=True),
                       reads=("qg", ktok), writes=(f"ps{pb}",),
                       nodrain="D")
            span = psum_h[:, pbA * 512: pbA * 512 + 1024].rearrange(
                "p (b c) -> p b c", c=512)[:, :, 0:256]
            ptoks2 = (f"ps{pbA}", f"ps{pbB}")
            sch.op("vector", lambda sp=span, bv=ncbv: nc.vector.tensor_tensor(
                out=stscr.rearrange("p (b c) -> p b c", c=256), in0=sp,
                in1=bv.rearrange("p (b c) -> p b c", c=256), op=OP.add),
                reads=ptoks2 + (f"ncbt_{par}",), writes=("stscr",))
            if kt >= nkt - 2:
                kt2 = kt - (nkt - 2)
                sch.op("vector", lambda q2=qt, k2=kt2:
                       nc.vector.tensor_tensor(
                           out=stscr, in0=stscr,
                           in1=maskT[:, q2 * 1024 + k2 * 512:
                                     q2 * 1024 + (k2 + 1) * 512],
                           op=OP.add),
                       reads=("stscr", "c_maskT"), writes=("stscr",))
            sch.op("scalar", lambda k=kt: nc.scalar.activation(
                out=PT[:, k * 512:(k + 1) * 512], in_=stscr, func=AF.Exp),
                reads=("stscr", "acttab"), writes=(f"PT{kt}",))
        for gp2 in range(2):
            for gg in range(2):
                g = 2 * gp2 + gg
                colb = 64 * gg
                pos = POSG[g]
                for kt in range(nkt):
                    sch.op("tensor", lambda k=kt, p2=pos, cb2=colb, hh=h,
                           fr=(kt == 0), la=(kt == nkt - 1):
                           nc.tensor.matmul(
                               PS(7, 0, 128, cb2, cb2 + 64),
                               vtm[:, k * 512 + hh * 64: k * 512 + (hh + 1) * 64],
                               PT[:, k * 512 + p2 * 128: k * 512 + (p2 + 1) * 128],
                               start=fr, stop=la, tile_position=(0, cb2)),
                           reads=(f"PT{kt}", "vtm"), writes=("ps7",),
                           nodrain=('A' if kt != 0 else False))
            sch.op("vector", lambda gp3=gp2, hh=h, q2=qt: nc.vector.tensor_copy(
                yv[:, (2 * hh + gp3) * NQ + q2 * 128: (2 * hh + gp3) * NQ + q2 * 128 + 128],
                PS(7, 0, 128)),
                reads=("ps7",), writes=(f"y_{2*h+gp2}",))

    ITERS = [(h, qt) for h in range(HKV) for qt in range(NBLK)]
    for i, (h, qt) in enumerate(ITERS):
        emit_attn_S(h, qt, i % 2)
        emit_attn_STAV(h, qt, i % 2)

    sch.barrier()
    A.free(o_PT, 16 * 512 * 2)
    A.free(o_K, 4 * T * 2); A.free(o_Ksw, 4 * T * 2)
    A.free(o_V, 16 * KVD * 2)
    A.free(o_Qg, 16 * NQ * 2)
    A.free(o_nm4, 16); A.free(o_Z4, 16); A.free(o_lnZ, 16); A.free(o_nC, 16)
    A.free(o_nCT, 512); A.free(o_ncb, 2048)
    A.free(o_maskS, NBLK * 256 * 2); A.free(o_maskT, NBLK * 1024 * 2)
    A.free(o_sel4, 512 * 4)

    if "y" in DEBUG:
        sch.op("sync", lambda: nc.sync.dma_start(out=ddbg["y"], in_=yv),
               reads=tuple(f"y_{i}" for i in range(16)), writes=("dy",), dma="dbg2")
        sch.barrier()

    o_xown = A.alloc(FT * NQ * 4)
    x_own = A.view(o_xown, 0, 128, FT * NQ, F32)
    sch.op("sync", lambda: nc.sync.dma_start(
        out=x_own.rearrange("p (a b) -> p a b", b=NQ),
        in_=bass.AP(hxo, 0, [[NQ, 128], [128 * NQ, FT], [1, NQ]])),
        writes=("xown",), dma="xod")
    for fo in range(FT):
        r = fo % 2
        strip = emit_strip(r, dWo[fo * 128:(fo + 1) * 128, :], r)
        qpb = (2 + 2 * r, 3 + 2 * r)
        ptoks = (f"ps{qpb[0]}", f"ps{qpb[1]}")
        for fi in range(FT):
            for cb in range(2):
                sch.op("tensor", lambda b=qpb[cb], st=strip, f=fi, c=cb,
                       fr=(fi == 0), la=(fi == FT - 1):
                       nc.tensor.matmul(PS(b), st[:, f * 128:(f + 1) * 128],
                                        yv[:, f * NQ + c * 512: f * NQ + c * 512 + 512],
                                        start=fr, stop=la),
                       reads=(f"strip{r}", f"y_{fi}"), writes=(ptoks[cb],),
                       nodrain=('A' if fi != 0 else False))
        sch.op("vector", lambda b=qpb[0], f=fo: nc.vector.scalar_tensor_tensor(
            out=x_own[:, f * NQ:(f + 1) * NQ],
            in0=psum_h[:, b * 512:(b + 2) * 512],
            scalar=bo_t[:, f:f + 1], in1=x_own[:, f * NQ:(f + 1) * NQ],
            op0=OP.add, op1=OP.add),
            reads=ptoks + ("c_bo", "xown"), writes=("xown",))
    sch.barrier()
    A.free(o_y, 16 * NQ * 2)

    if "xattn" in DEBUG:
        sch.op("sync", lambda: nc.sync.dma_start(out=ddbg["xattn"], in_=x_own),
               reads=("xown",), writes=("dxa",), dma="dbg2")
        sch.barrier()

    A.free(o_wb[0], T * 2); A.free(o_wb[1], T * 2)
    A.free(o_strip[0], T * 2); A.free(o_strip[1], T * 2)
    o_ln2 = A.alloc(FT * NQ * 2)
    ln2v = A.view(o_ln2, 0, 128, FT * NQ, BF16)
    emit_ln("l2", ("sbuf", x_own, "xown"), NQ, ln2v, "ln2_")
    ln2toks = tuple(f"ln2_{i}" for i in range(FT))

    if "ln2" in DEBUG:
        sch.op("sync", lambda: nc.sync.dma_start(out=ddbg["ln2"], in_=ln2v),
               reads=ln2toks, writes=("dl2",), dma="dbg2")
        sch.barrier()

    o_wgr = A.alloc(FT * 8 * 2)
    wgr = A.view(o_wgr, 0, 128, FT * 8, BF16)
    sch.op("gpsimd", lambda: nc.gpsimd.dma_start(
        out=wgr.rearrange("p (a b) -> p a b", b=8),
        in_=bass.AP(hWg_r, 0, [[NEXP, 128], [128 * NEXP, FT], [1, NEXP]])),
        writes=("wgr",), dma="wgr")
    for fi in range(FT):
        for cb in range(2):
            sch.op("tensor", lambda f=fi, c=cb, fr=(fi == 0), la=(fi == FT - 1):
                   nc.tensor.matmul(PS(c, 0, 512, 0, 8),
                                    wgr[:, f * 8:(f + 1) * 8],
                                    ln2v[:, f * NQ + c * 512: f * NQ + c * 512 + 512],
                                    start=fr, stop=la),
                   reads=("wgr", f"ln2_{fi}"), writes=(f"ps{cb}",),
                   nodrain=('A' if fi != 0 else False))
    o_lg = A.alloc(NQ * 4)
    lg = A.view(o_lg, 0, 8, NQ, F32)
    sch.op("vector", lambda: nc.vector.tensor_copy(
        lg, psum_h[0:8, 0:1024]), reads=("ps0", "ps1"), writes=("lg",))
    for tt in range(8):
        sch.op("tensor", lambda t=tt: nc.tensor.matmul(
            PS(2, t * 8, (t + 1) * 8), lg[:, t * 128:(t + 1) * 128],
            I128f[0:8, 0:8], start=True, stop=True),
            reads=("lg", "c_I128f"), writes=("ps2",), nodrain="E")
    o_lgt = A.alloc(64 * 4); o_elg = A.alloc(64 * 4)
    o_m1 = A.alloc(8 * 4); o_m1n = A.alloc(8 * 4); o_m2 = A.alloc(8 * 4)
    o_zr = A.alloc(8 * 4); o_msk = A.alloc(64 * 4); o_lg2 = A.alloc(64 * 4)
    o_cmb = A.alloc(64 * 4)
    lgt = A.view(o_lgt, 0, 128, 64, F32)
    elg = A.view(o_elg, 0, 128, 64, F32)
    m1 = A.view(o_m1, 0, 128, 8, F32)
    m1n = A.view(o_m1n, 0, 128, 8, F32)
    m2v = A.view(o_m2, 0, 128, 8, F32)
    zr = A.view(o_zr, 0, 128, 8, F32)
    msk = A.view(o_msk, 0, 128, 64, F32)
    lg2 = A.view(o_lg2, 0, 128, 64, F32)
    cmb = A.view(o_cmb, 0, 128, 64, F32)
    sch.op("vector", lambda: nc.vector.tensor_copy(lgt, PS(2, 0, 64)),
           reads=("ps2",), writes=("lgt",))
    for tt in range(8):
        sl = slice(tt * 8, (tt + 1) * 8)
        sch.op("vector", lambda t=tt, s2=sl: nc.vector.tensor_reduce(
            out=m1[:, t:t + 1], in_=lgt[:, s2], axis=AX.X, op=OP.max),
            reads=("lgt",), writes=(f"m1_{tt}",))
        sch.op("vector", lambda t=tt: nc.vector.tensor_scalar(
            out=m1n[:, t:t + 1], in0=m1[:, t:t + 1], scalar1=-1.0,
            scalar2=None, op0=OP.mult),
            reads=(f"m1_{tt}",), writes=(f"m1n_{tt}",))
        sch.op("scalar", lambda t=tt, s2=sl: nc.scalar.activation(
            out=elg[:, s2], in_=lgt[:, s2], func=AF.Exp,
            bias=m1n[:, t:t + 1], accum_out=zr[:, t:t + 1]),
            reads=("lgt", f"m1n_{tt}", "acttab"), writes=(f"elg_{tt}", f"zr_{tt}"))
        sch.op("vector", lambda t=tt: nc.vector.reciprocal(
            out=zr[:, t:t + 1], in_=zr[:, t:t + 1]),
            reads=(f"zr_{tt}",), writes=(f"zr_{tt}",))
        sch.op("vector", lambda t=tt, s2=sl: nc.vector.tensor_scalar(
            out=msk[:, s2], in0=lgt[:, s2], scalar1=m1[:, t:t + 1],
            scalar2=None, op0=OP.is_ge),
            reads=("lgt", f"m1_{tt}"), writes=(f"msk_{tt}",))
        sch.op("vector", lambda t=tt, s2=sl: nc.vector.scalar_tensor_tensor(
            out=lg2[:, s2], in0=msk[:, s2], scalar=-30000.0,
            in1=lgt[:, s2], op0=OP.mult, op1=OP.add),
            reads=(f"msk_{tt}", "lgt"), writes=(f"lg2_{tt}",))
        sch.op("vector", lambda t=tt, s2=sl: nc.vector.tensor_reduce(
            out=m2v[:, t:t + 1], in_=lg2[:, s2], axis=AX.X, op=OP.max),
            reads=(f"lg2_{tt}",), writes=(f"m2_{tt}",))
        sch.op("vector", lambda t=tt, s2=sl: nc.vector.tensor_scalar(
            out=lg2[:, s2], in0=lg2[:, s2], scalar1=m2v[:, t:t + 1],
            scalar2=None, op0=OP.is_ge),
            reads=(f"lg2_{tt}", f"m2_{tt}"), writes=(f"lg2_{tt}",))
        sch.op("vector", lambda t=tt, s2=sl: nc.vector.tensor_tensor(
            out=msk[:, s2], in0=msk[:, s2], in1=lg2[:, s2], op=OP.add),
            reads=(f"msk_{tt}", f"lg2_{tt}"), writes=(f"msk_{tt}",))
        sch.op("vector", lambda t=tt, s2=sl: nc.vector.scalar_tensor_tensor(
            out=cmb[:, s2], in0=elg[:, s2], scalar=zr[:, t:t + 1],
            in1=msk[:, s2], op0=OP.mult, op1=OP.mult),
            reads=(f"elg_{tt}", f"zr_{tt}", f"msk_{tt}"), writes=(f"cmb_{tt}",))
    for tt in range(8):
        sch.op("tensor", lambda t=tt: nc.tensor.matmul(
            PS(4 + t // 4, (t % 4) * 128, (t % 4 + 1) * 128, 0, 8),
            cmb[:, t * 8:(t + 1) * 8], I128f, start=True, stop=True),
            reads=(f"cmb_{tt}", "c_I128f"), writes=(f"ps{4 + tt//4}",),
            nodrain="E")
    o_cfm = A.alloc(NQ * 4)
    cfm = A.view(o_cfm, 0, 8, NQ, F32)
    sch.op("vector", lambda: nc.vector.tensor_copy(
        cfm, psum_h[0:8, 4 * 512: 6 * 512]),
        reads=("ps4", "ps5"), writes=("cfm",))
    if "comb" in DEBUG:
        sch.op("sync", lambda: nc.sync.dma_start(out=ddbg["comb"], in_=cfm),
               reads=("cfm",), writes=("dcm",), dma="dbg2")
    o_sel8 = A.alloc(1024 * 4)
    sel8f = A.view(o_sel8, 0, 8, 1024, F32)
    const_load("c_sel8f", sel8f, dconst2)
    o_call = A.alloc(NEXP * NQ * 2)
    call = A.view(o_call, 0, 128, NEXP * NQ, BF16)
    for e in range(NEXP):
        for cb in range(2):
            sch.op("tensor", lambda e2=e, c=cb: nc.tensor.matmul(
                PS(6 + c), sel8f[:, e2 * 128:(e2 + 1) * 128],
                cfm[:, c * 512:(c + 1) * 512], start=True, stop=True),
                reads=("cfm", "c_sel8f"), writes=(f"ps{6+cb}",), nodrain="E")
        sch.op("vector", lambda e2=e: nc.vector.tensor_copy(
            call[:, e2 * NQ:(e2 + 1) * NQ], psum_h[:, 6 * 512: 8 * 512]),
            reads=("ps6", "ps7"), writes=(f"call{e}",))
    sch.barrier()

    sch.op("scalar", lambda: load_act_set(18), writes=("acttab2",))
    for _ in range(4):
        sch.op("scalar", lambda: nc.scalar.activation(
            out=warm, in_=warm, func=AF.Silu),
            reads=("acttab2", "c_fc"), writes=("warm",))
    sch.barrier()

    NCH = 11
    o_h = A.alloc(NCH * NQ * 2)
    hsb = A.view(o_h, 0, 128, NCH * NQ, BF16)
    o_wgc = [A.alloc(T * 2), A.alloc(T * 2)]
    o_wuc = [A.alloc(T * 2), A.alloc(T * 2)]
    o_wdc = [A.alloc(NCH * 128 * 2), A.alloc(NCH * 128 * 2)]
    o_sg = [A.alloc(NQ * 2), A.alloc(NQ * 2)]

    def ffn_col_ap(handle, base_off, FH, c):
        return bass.AP(handle, base_off + c * 128,
                       [[FH, 128], [128 * FH, FT], [1, 128]])

    def ffn_wd_ap(handle, base_off, row0, fo):
        return bass.AP(handle, base_off + row0 * E + fo * 128,
                       [[E, 128], [128 * E, NCH], [1, 128]])

    def emit_ffn(tag, wg_src, wu_src, wd_src, cexp):
        for ci in range(NCH):
            r = ci % 2
            gb = 4 * r
            wgc = A.view(o_wgc[r], 0, 128, T, BF16)
            wuc = A.view(o_wuc[r], 0, 128, T, BF16)
            sgv = A.view(o_sg[r], 0, 128, NQ, BF16)
            sch.op("gpsimd", lambda v=wgc, c=ci, f=wg_src: nc.gpsimd.dma_start(
                out=v.rearrange("p (a b) -> p a b", b=128), in_=f(c)),
                writes=(f"wgc{r}",), dma=f"wgc{r}")
            sch.op("gpsimd", lambda v=wuc, c=ci, f=wu_src: nc.gpsimd.dma_start(
                out=v.rearrange("p (a b) -> p a b", b=128), in_=f(c)),
                writes=(f"wuc{r}",), dma=f"wuc{r}")
            for wtok, wv, b0 in ((f"wgc{r}", wgc, gb), (f"wuc{r}", wuc, gb + 2)):
                for fi in range(FT):
                    for cb in range(2):
                        sch.op("tensor", lambda b=b0 + cb, w=wv, f=fi, c=cb,
                               fr=(fi == 0), la=(fi == FT - 1):
                               nc.tensor.matmul(
                                   PS(b), w[:, f * 128:(f + 1) * 128],
                                   ln2v[:, f * NQ + c * 512: f * NQ + c * 512 + 512],
                                   start=fr, stop=la),
                               reads=(wtok, f"ln2_{fi}"), writes=(f"ps{b0+cb}",),
                               nodrain=('A' if fi != 0 else False))
            sch.op("scalar", lambda b=gb, s2=sgv: nc.scalar.activation(
                out=s2, in_=psum_h[:, b * 512:(b + 2) * 512], func=AF.Silu),
                reads=(f"ps{gb}", f"ps{gb+1}", "acttab2"), writes=(f"sg{r}",))
            sch.op("vector", lambda b=gb + 2, s2=sgv, c=ci:
                   nc.vector.tensor_tensor(
                       out=hsb[:, c * NQ:(c + 1) * NQ], in0=s2,
                       in1=psum_h[:, b * 512:(b + 2) * 512], op=OP.mult),
                   reads=(f"sg{r}", f"ps{gb+2}", f"ps{gb+3}"), writes=(f"h{ci}",))
            if cexp is not None:
                sch.op("vector", lambda c=ci, e=cexp: nc.vector.tensor_tensor(
                    out=hsb[:, c * NQ:(c + 1) * NQ],
                    in0=hsb[:, c * NQ:(c + 1) * NQ],
                    in1=call[:, e * NQ:(e + 1) * NQ], op=OP.mult),
                    reads=(f"h{ci}", f"call{e}"), writes=(f"h{ci}",))
        htoks = tuple(f"h{i}" for i in range(NCH))
        for fo in range(FT):
            r = fo % 2
            db = (2 + 2 * r, 3 + 2 * r)
            wdc = A.view(o_wdc[r], 0, 128, NCH * 128, BF16)
            sch.op("gpsimd", lambda v=wdc, f2=fo, f=wd_src: nc.gpsimd.dma_start(
                out=v.rearrange("p (a b) -> p a b", b=128), in_=f(f2)),
                writes=(f"wdc{r}",), dma=f"wdc{r}")
            for ci in range(NCH):
                for cb in range(2):
                    sch.op("tensor", lambda b=db[cb], w=wdc, c=ci, c2=cb,
                           fr=(ci == 0), la=(ci == NCH - 1):
                           nc.tensor.matmul(
                               PS(b), w[:, c * 128:(c + 1) * 128],
                               hsb[:, c * NQ + c2 * 512: c * NQ + c2 * 512 + 512],
                               start=fr, stop=la),
                           reads=(f"wdc{r}",) + htoks, writes=(f"ps{db[cb]}",),
                           nodrain=('A' if ci != 0 else False))
            sch.op("vector", lambda b=db[0], f=fo: nc.vector.tensor_tensor(
                out=x_own[:, f * NQ:(f + 1) * NQ],
                in0=psum_h[:, b * 512:(b + 2) * 512],
                in1=x_own[:, f * NQ:(f + 1) * NQ], op=OP.add),
                reads=(f"ps{db[0]}", f"ps{db[1]}", "xown"), writes=("xown",))

    emit_ffn("s0", lambda c: ffn_col_ap(hWsg, 0, FH_S, c),
             lambda c: ffn_col_ap(hWsu, 0, FH_S, c),
             lambda fo: ffn_wd_ap(hWsd, 0, 0, fo), None)
    emit_ffn("s1", lambda c: ffn_col_ap(hWsg, 0, FH_S, NCH + c),
             lambda c: ffn_col_ap(hWsu, 0, FH_S, NCH + c),
             lambda fo: ffn_wd_ap(hWsd, 0, NCH * 128, fo), None)
    for e in range(NEXP):
        emit_ffn(f"e{e}",
                 lambda c, e2=e: ffn_col_ap(hWg, e2 * E * FH_E, FH_E, c),
                 lambda c, e2=e: ffn_col_ap(hWu, e2 * E * FH_E, FH_E, c),
                 lambda fo, e2=e: ffn_wd_ap(hWd, e2 * FH_E * E, 0, fo), e)
    sch.barrier()

    sch.op("sync", lambda: nc.sync.dma_start(
        out=bass.AP(hout, 0, [[NQ, 128], [128 * NQ, FT], [1, NQ]]),
        in_=x_own.rearrange("p (a b) -> p a b", b=NQ)),
        reads=("xown",), writes=("outd",), dma="out")
    sch.barrier()
    sch.finalize()
    sch.plan()
    eng_sems = {}
    dma_sems = {}
    cms = []
    for e in ENGINES:
        cm = nc.semaphore(f"sem_{e}")
        eng_sems[e] = cm.__enter__()
        cms.append(cm)
    for k in sch.dma_keys:
        cm = nc.semaphore(f"dma_{k}")
        dma_sems[k] = cm.__enter__()
        cms.append(cm)
    my_sems = [s.num for s in eng_sems.values()] + [s.num for s in dma_sems.values()]
    lo, hi = min(my_sems), max(my_sems)
    assert set(my_sems) == set(range(lo, hi + 1)), "non-contiguous sems"

    nc.all_engine_barrier()
    nc.gpsimd.dma_reset(range(lo, hi + 1))
    nc.gpsimd.sem_clear(range(lo, hi + 1))
    nc.all_engine_barrier()

    eng_objs = dict(sync=nc.sync, gpsimd=nc.gpsimd, scalar=nc.scalar,
                    vector=nc.vector, tensor=nc.tensor)
    for e in ENGINES:
        sch.emit(e, eng_objs[e], eng_sems, dma_sems)

    nc.all_engine_barrier()
    nc.gpsimd.dma_reset(range(lo, hi + 1))
    nc.gpsimd.sem_clear(range(lo, hi + 1))
    nc.all_engine_barrier()
    return nc


def _host_tables(own_blocks):
    pos_own = (np.asarray(own_blocks)[:, None] * 128 + np.arange(128)[None, :]).ravel()
    j = np.arange(D // 2, dtype=np.float64)
    theta = 10000.0 ** (-2.0 * j / D)

    def tables(pos, scale):
        ang = (pos[None, :].astype(np.float64) + 1.0) * theta[:, None]
        C = np.concatenate([ang, ang], axis=0)
        cos = (np.cos(C) * scale).astype(np.float32)
        sin = (np.sin(C) * scale).astype(np.float32)
        return np.concatenate([cos, cos], 0), np.concatenate([sin, sin], 0)

    cqc, cqs = tables(pos_own, 0.125)
    ckc, cks = tables(np.arange(T), 1.0)
    return cqc, cqs, ckc, cks


def _host_masks(own_blocks):
    NEG = np.float32(-30000.0)
    q = np.arange(128)
    mS = np.zeros((128, NBLK * 256), np.float32)
    mT = np.zeros((128, NBLK * 1024), np.float32)
    for jslot in range(NBLK):
        g = own_blocks[jslot]
        nk = NKEYS[jslot]
        k = nk - 256 + np.arange(256)
        valid = k[None, :] <= (g * 128 + q)[:, None]
        mS[:, jslot * 256:(jslot + 1) * 256] = np.where(valid, 0.0, NEG)
        for kt2 in range(2):
            kk = nk - 256 + kt2 * 128 + np.arange(128)
            validT = kk[:, None] <= (g * 128 + q)[None, :]
            tile = np.where(validT, 0.0, NEG)
            for g4 in range(4):
                mT[:, jslot * 1024 + kt2 * 512 + g4 * 128:
                     jslot * 1024 + kt2 * 512 + (g4 + 1) * 128] = tile
    return mS.astype(ml_dtypes.bfloat16), mT.astype(ml_dtypes.bfloat16)


def _host_consts():
    ib = np.eye(128, dtype=ml_dtypes.bfloat16)
    raw = np.zeros((128, 4096), np.uint8)
    raw[:, 0:128 * 2] = ib.view(np.uint8).reshape(128, 256)
    i32 = np.eye(128, dtype=np.float32)
    raw[:, 256:256 + 512] = i32.view(np.uint8).reshape(128, 512)
    ones_b = np.ones((1, 128), dtype=ml_dtypes.bfloat16)
    raw[0:1, 768:768 + 256] = ones_b.view(np.uint8).reshape(1, 256)
    ones_f = np.ones((1, 128), dtype=np.float32)
    raw[0:1, 1024:1024 + 512] = ones_f.view(np.uint8).reshape(1, 512)
    onecol = np.ones((128, 1), dtype=ml_dtypes.bfloat16)
    raw[:, 1536:1538] = onecol.view(np.uint8).reshape(128, 2)
    fconst = np.zeros((128, 8), np.float32)
    fconst[:, 1] = EPS
    raw[:, 1568:1568 + 32] = fconst.view(np.uint8).reshape(128, 32)
    sel4 = np.zeros((4, 512), np.float32)
    for g in range(4):
        sel4[g, g * 128:(g + 1) * 128] = 1.0
    raw[0:4, 1600:1600 + 2048] = sel4.view(np.uint8).reshape(4, 2048)
    return raw.view(np.float32)


def _host_consts2():
    sel8 = np.zeros((8, 1024), np.float32)
    for e in range(8):
        sel8[e, e * 128:(e + 1) * 128] = 1.0
    return sel8


_PROG_CACHE = {}


def kernel(L0_attn_x, L0_attn_Wq, L0_attn_bq, L0_attn_Wk, L0_attn_bk,
           L0_attn_Wv, L0_attn_bv, L0_attn_Wo, L0_attn_bo,
           L0_ffn_W_g, L0_ffn_Wg, L0_ffn_Wu, L0_ffn_Wd,
           L0_ffn_Ws_gate, L0_ffn_Ws_up, L0_ffn_Ws_down):
    x = np.asarray(L0_attn_x, np.float32)
    consts = _host_consts()
    weights = dict(
        Wq=np.asarray(L0_attn_Wq, np.float32), Wk=np.asarray(L0_attn_Wk, np.float32),
        Wv=np.asarray(L0_attn_Wv, np.float32), Wo=np.asarray(L0_attn_Wo, np.float32),
        W_g=np.asarray(L0_ffn_W_g, np.float32),
        Wg=np.asarray(L0_ffn_Wg, np.float32), Wu=np.asarray(L0_ffn_Wu, np.float32),
        Wd=np.asarray(L0_ffn_Wd, np.float32),
        Ws_gate=np.asarray(L0_ffn_Ws_gate, np.float32),
        Ws_up=np.asarray(L0_ffn_Ws_up, np.float32),
        Ws_down=np.asarray(L0_ffn_Ws_down, np.float32),
    )
    bq = np.asarray(L0_attn_bq, np.float32).reshape(FT, 128).T.copy()
    bk = np.asarray(L0_attn_bk, np.float32).reshape(4, 128).T.copy()
    bv = np.asarray(L0_attn_bv, np.float32).reshape(4, 128).T.copy()
    bo = np.asarray(L0_attn_bo, np.float32).reshape(FT, 128).T.copy()

    consts2 = _host_consts2()
    in_maps = []
    metas = []
    for c in range(8):
        s = c // 2
        own = OWN_A if c % 2 == 0 else OWN_B
        x_fm = np.ascontiguousarray(x[s].T)
        tok = (np.asarray(own)[:, None] * 128 + np.arange(128)[None, :]).ravel()
        x_own_fm = np.ascontiguousarray(x_fm[:, tok])
        cqc, cqs, ckc, cks = _host_tables(own)
        mS, mT = _host_masks(own)
        m = dict(weights)
        m.update(x_fm=x_fm, x_own_fm=x_own_fm, bq_t=bq, bk_t=bk, bv_t=bv,
                 bo_t=bo, cq_cos=cqc, cq_sin=cqs, ck_cos=ckc, ck_sin=cks,
                 maskS=mS, maskT=mT, consts=consts, consts2=consts2)
        in_maps.append(m)
        metas.append((s, own))

    key = "prog"
    if key not in _PROG_CACHE:
        _PROG_CACHE[key] = build_program()
    nc = _PROG_CACHE[key]

    res = run_bass_kernel_spmd(nc, in_maps, core_ids=list(range(8)))

    out = np.empty((B, T, E), np.float32)
    for c in range(8):
        s, own = metas[c]
        tok = (np.asarray(own)[:, None] * 128 + np.arange(128)[None, :]).ravel()
        out[s, tok, :] = res.results[c]["out_fm"].T
    if DEBUG:
        kernel.debug = [{k: v for k, v in r.items() if k.startswith("dbg_")}
                        for r in res.results]
    return out


# version 10
# speedup vs baseline: 1.0966x; 1.0966x over previous
import numpy as np
import ml_dtypes
import concourse.bass as bass
import concourse.mybir as mybir
from concourse.bass_utils import run_bass_kernel_spmd

F32 = mybir.dt.float32
BF16 = mybir.dt.bfloat16
AF = mybir.ActivationFunctionType
OP = mybir.AluOpType
AX = mybir.AxisListType

B, T, E = 4, 2048, 2048
HKV, NG, D = 8, 4, 64
KVD = 512
FH_E, NEXP = 1408, 8
FH_S = 2816
NQ = 1024
NBLK = 8
FT = 16
EPS = 1e-5

OWN_A = [0, 4, 8, 12, 3, 7, 11, 15]
OWN_B = [1, 5, 9, 13, 2, 6, 10, 14]
NKEYS = [256, 768, 1280, 1792, 512, 1024, 1536, 2048]

DEBUG = ()
TENSOR_DRAINS = False
NODRAIN_CATS = set("ABCDEF")
PRUNE_WAITS = True


ENGINES = ("sync", "gpsimd", "scalar", "vector", "tensor")


class Sched:
    def __init__(self):
        self.ops = []

    def op(self, engine, fn, reads=(), writes=(), dma=None, nodrain=False):
        assert engine in ENGINES
        self.ops.append(dict(engine=engine, fn=fn, reads=tuple(reads),
                             writes=tuple(writes), dma=dma, barrier=False,
                             nodrain=nodrain))

    def barrier(self):
        self.ops.append(dict(engine=None, fn=None, reads=(), writes=(),
                             dma=None, barrier=True))

    def finalize(self):
        last_writer = {}
        readers = {}
        last_barrier = -1
        n = len(self.ops)
        deps = [set() for _ in range(n)]
        barrier_ids = []
        since_barrier = []
        for i, o in enumerate(self.ops):
            if o["barrier"]:
                deps[i] = set(since_barrier)
                barrier_ids.append(i)
                since_barrier = []
                last_barrier = i
                continue
            dd = set()
            for r in o["reads"]:
                if r in last_writer:
                    dd.add(last_writer[r])
            for w in o["writes"]:
                if w in last_writer:
                    dd.add(last_writer[w])
                for rd in readers.get(w, ()):
                    dd.add(rd)
            if last_barrier >= 0:
                dd.add(last_barrier)
            deps[i] = {d for d in dd if d >= last_barrier}
            for w in o["writes"]:
                last_writer[w] = i
                readers[w] = []
            for r in o["reads"]:
                readers.setdefault(r, []).append(i)
            since_barrier.append(i)

        emit_ops = []
        for i, o in enumerate(self.ops):
            if o["barrier"]:
                emit_ops.append(dict(kind="barrier", idx=i))
            else:
                emit_ops.append(dict(kind="op", idx=i, **o))

        ord_of = {}
        eng_count = {e: 0 for e in ENGINES}
        for eo in emit_ops:
            i = eo["idx"]
            if eo["kind"] == "barrier":
                for e in ENGINES:
                    eng_count[e] += 1
                    ord_of[(i, e)] = eng_count[e]
            else:
                e = eo["engine"]
                eng_count[e] += 1
                ord_of[(i, e)] = eng_count[e]

        self.inc_needed = set()
        for i, o in enumerate(self.ops):
            if not o["barrier"] and o["dma"] is not None:
                self.inc_needed.add(i)
        for i, o in enumerate(self.ops):
            my_engines = set(ENGINES) if o["barrier"] else {o["engine"]}
            for d in deps[i]:
                od = self.ops[d]
                if od["barrier"]:
                    continue
                if od["dma"] is not None:
                    self.inc_needed.add(d)
                elif od["engine"] not in my_engines or o["barrier"]:
                    self.inc_needed.add(d)
        for b in barrier_ids:
            self.inc_needed.add(b)
        self.dma_issuer = {}
        for o in self.ops:
            if not o["barrier"] and o["dma"] is not None:
                prev = self.dma_issuer.setdefault(o["dma"], o["engine"])
                assert prev == o["engine"], f"dma key {o['dma']} on two engines"

        inc_ord = {}
        eng_inc = {e: 0 for e in ENGINES}
        dma_inc = {}
        self.dma_keys = []
        self.barrier_dma_snap = {}
        for eo in emit_ops:
            i = eo["idx"]
            if eo["kind"] == "barrier":
                self.barrier_dma_snap[i] = dict(dma_inc)
                for e in ENGINES:
                    eng_inc[e] += 1
                inc_ord[i] = ("barrier", None, {e: eng_inc[e] for e in ENGINES})
            else:
                o = self.ops[i]
                if i in self.inc_needed:
                    if o["dma"] is not None:
                        k = o["dma"]
                        if k not in dma_inc:
                            dma_inc[k] = 0
                            self.dma_keys.append(k)
                        dma_inc[k] += 16
                        inc_ord[i] = ("dma", k, dma_inc[k])
                    else:
                        e = o["engine"]
                        eng_inc[e] += 1
                        inc_ord[i] = ("eng", e, eng_inc[e])
        self.deps = deps
        self.inc_ord = inc_ord
        self.eng_inc_total = eng_inc
        self.barrier_set = set(barrier_ids)

    def plan(self):
        n = len(self.ops)
        ordv = [None] * n
        cnt = {e: 0 for e in ENGINES}
        for i, o in enumerate(self.ops):
            if o["barrier"]:
                for e in ENGINES:
                    cnt[e] += 1
                ordv[i] = dict(cnt)
            else:
                e = o["engine"]
                cnt[e] += 1
                ordv[i] = cnt[e]
        clock = [None] * n
        self.plan_waits = [[] for _ in range(n)]
        self.plan_drain = [False] * n
        last_op_on = {e: None for e in ENGINES}
        last_dma_on_key = {}
        drained_to = {e: 0 for e in ENGINES}
        ZERO = {e: 0 for e in ENGINES}

        def merge(a, b):
            return {e: max(a[e], b[e]) for e in ENGINES}

        for i, o in enumerate(self.ops):
            if o["barrier"]:
                c = dict(ordv[i])
                clock[i] = c
                for e in ENGINES:
                    last_op_on[e] = i
                    drained_to[e] = ordv[i][e]
                continue
            eng = o["engine"]
            know = dict(ZERO)
            waits = []
            dset = sorted(self.deps[i])
            for d in dset:
                if self.ops[d]["barrier"]:
                    know = merge(know, clock[d])
            for d in dset:
                od = self.ops[d]
                if od["barrier"]:
                    continue
                kind, key, cval = self.inc_ord[d]
                if kind == "dma":
                    if know.get(("dma", key), 0) if False else False:
                        pass
                    waits.append(("dma", key, cval, d))
                    know = merge(know, clock[d])
                elif key == eng:
                    continue
                else:
                    if PRUNE_WAITS and know[key] >= ordv[d]:
                        continue
                    waits.append(("eng", key, cval, d))
                    know = merge(know, clock[d])
            need_drain = False
            nd = o.get("nodrain", False)
            if isinstance(nd, str):
                nd = nd in NODRAIN_CATS
            if not nd:
                for d in dset:
                    od = self.ops[d]
                    if od["barrier"] or od["engine"] != eng or od["dma"] is not None:
                        continue
                    if know[eng] < ordv[d] and drained_to[eng] < ordv[d]:
                        need_drain = True
            if need_drain:
                p = last_op_on[eng]
                if p is not None:
                    know = merge(know, clock[p])
                know[eng] = max(know[eng], ordv[i] - 1)
                drained_to[eng] = ordv[i] - 1
            self.plan_waits[i] = waits
            self.plan_drain[i] = need_drain
            c = dict(know)
            if o["dma"] is not None:
                p = last_dma_on_key.get(o["dma"])
                if p is not None:
                    c = merge(c, clock[p])
                last_dma_on_key[o["dma"]] = i
            else:
                p = last_op_on[eng]
                if p is not None:
                    c = merge(c, clock[p])
                c[eng] = max(c[eng], ordv[i])
                last_op_on[eng] = i
            clock[i] = c

    def emit(self, engine, eng_obj, eng_sems, dma_sems):
        waited = {}

        def wait(kind, key, val):
            cur = waited.get((kind, key), 0)
            if val <= cur:
                return
            waited[(kind, key)] = val
            sem = eng_sems[key] if kind == "eng" else dma_sems[key]
            eng_obj.wait_ge(sem, val)

        for i, o in enumerate(self.ops):
            if o["barrier"]:
                kind, _, totals = self.inc_ord[i]
                for k, tot in self.barrier_dma_snap[i].items():
                    if self.dma_issuer.get(k) == engine and tot > 0:
                        wait("dma", k, tot)
                eng_obj.drain().then_inc(eng_sems[engine], 1)
                for e in ENGINES:
                    if e == engine:
                        continue
                    wait("eng", e, totals[e])
                continue
            if o["engine"] != engine:
                continue
            for kind, key, cval, d in self.plan_waits[i]:
                wait(kind, key, cval)
            if self.plan_drain[i]:
                eng_obj.drain()
            instr = o["fn"]()
            info = self.inc_ord.get(i)
            if info is not None:
                kind, key, cnt = info
                if kind == "dma":
                    instr.then_inc(dma_sems[key], 16)
                else:
                    instr.then_inc(eng_sems[key], 1)


class Arena:

    def __init__(self, handle, nbytes):
        self.h = handle
        self.nbytes = nbytes
        self.free_list = [(0, nbytes)]
        self.marks = []
        self.peak = 0

    def alloc(self, nbytes, align=32):
        nbytes = (int(nbytes) + align - 1) // align * align
        if not hasattr(self, "live"):
            self.live = {}
        for i, (off, size) in enumerate(self.free_list):
            aoff = (off + align - 1) // align * align
            pad = aoff - off
            if size - pad >= nbytes:
                rest = size - pad - nbytes
                new = []
                if pad:
                    new.append((off, pad))
                if rest:
                    new.append((aoff + nbytes, rest))
                self.free_list[i:i + 1] = new
                used = self.nbytes - sum(s for _, s in self.free_list)
                self.peak = max(self.peak, used)
                self.live[aoff] = nbytes
                return aoff
        raise AssertionError(
            f"SBUF arena overflow: want {nbytes}, free={self.free_list}")

    def free(self, off, nbytes, align=32):
        nbytes = (int(nbytes) + align - 1) // align * align
        assert self.live.get(off) == nbytes, (
            f"bad free: off={off} size={nbytes} live={self.live.get(off)}")
        del self.live[off]
        self.free_list.append((off, nbytes))
        self.free_list.sort()
        merged = []
        for o, sz in self.free_list:
            if merged and merged[-1][0] + merged[-1][1] == o:
                merged[-1] = (merged[-1][0], merged[-1][1] + sz)
            else:
                merged.append((o, sz))
        self.free_list = [tuple(x) for x in merged]

    def mark(self):
        pass

    def release(self):
        self.cur = self.marks.pop()

    def view(self, off, p0, p1, cols, dtype):
        esz = mybir.dt.size(dtype)
        return self.h[p0:p1, off:off + cols * esz].bitcast(dtype)


def vslice(arena, off, dtype, p0, p1, c0, c1):
    esz = mybir.dt.size(dtype)
    return arena.h[p0:p1, off + c0 * esz: off + c1 * esz].bitcast(dtype)


def build_program():
    nc = bass.Bass("TRN2", target_bir_lowering=False)

    hx = nc.dram_tensor("x_fm", [E, T], F32, kind="ExternalInput")
    dx = hx.ap()
    hxo = nc.dram_tensor("x_own_fm", [E, NQ], F32, kind="ExternalInput")
    dxo = hxo.ap()
    dWq = nc.dram_tensor("Wq", [E, E], F32, kind="ExternalInput").ap()
    dWk = nc.dram_tensor("Wk", [KVD, E], F32, kind="ExternalInput").ap()
    dWv = nc.dram_tensor("Wv", [KVD, E], F32, kind="ExternalInput").ap()
    dWo = nc.dram_tensor("Wo", [E, E], F32, kind="ExternalInput").ap()
    hWg_r = nc.dram_tensor("W_g", [E, NEXP], F32, kind="ExternalInput")
    hWg = nc.dram_tensor("Wg", [NEXP, E, FH_E], F32, kind="ExternalInput")
    hWu = nc.dram_tensor("Wu", [NEXP, E, FH_E], F32, kind="ExternalInput")
    hWd = nc.dram_tensor("Wd", [NEXP, FH_E, E], F32, kind="ExternalInput")
    hWsg = nc.dram_tensor("Ws_gate", [E, FH_S], F32, kind="ExternalInput")
    hWsu = nc.dram_tensor("Ws_up", [E, FH_S], F32, kind="ExternalInput")
    hWsd = nc.dram_tensor("Ws_down", [FH_S, E], F32, kind="ExternalInput")
    dbq = nc.dram_tensor("bq_t", [128, FT], F32, kind="ExternalInput").ap()
    dbk = nc.dram_tensor("bk_t", [128, 4], F32, kind="ExternalInput").ap()
    dbv = nc.dram_tensor("bv_t", [128, 4], F32, kind="ExternalInput").ap()
    dbo = nc.dram_tensor("bo_t", [128, FT], F32, kind="ExternalInput").ap()
    dcqc = nc.dram_tensor("cq_cos", [128, NQ], F32, kind="ExternalInput").ap()
    dcqs = nc.dram_tensor("cq_sin", [128, NQ], F32, kind="ExternalInput").ap()
    dckc = nc.dram_tensor("ck_cos", [128, T], F32, kind="ExternalInput").ap()
    dcks = nc.dram_tensor("ck_sin", [128, T], F32, kind="ExternalInput").ap()
    dmaskS = nc.dram_tensor("maskS", [128, NBLK * 256], BF16, kind="ExternalInput").ap()
    dmaskT = nc.dram_tensor("maskT", [128, NBLK * 1024], BF16, kind="ExternalInput").ap()
    dconst = nc.dram_tensor("consts", [128, 1024], F32, kind="ExternalInput").ap()
    dconst2 = nc.dram_tensor("consts2", [8, 1024], F32, kind="ExternalInput").ap()

    hout = nc.dram_tensor("out_fm", [E, NQ], F32, kind="ExternalOutput")
    dout = hout.ap()
    ddbg = {}
    if "ln1" in DEBUG:
        ddbg["ln1"] = nc.dram_tensor("dbg_ln1", [E, T], F32, kind="ExternalOutput").ap()
    if "kpack" in DEBUG:
        ddbg["kpack"] = nc.dram_tensor("dbg_kpack", [128, 4 * T], BF16, kind="ExternalOutput").ap()
        ddbg["vtm"] = nc.dram_tensor("dbg_vtm", [128, 16 * KVD], BF16, kind="ExternalOutput").ap()
    if "qpack" in DEBUG or "qpk" in DEBUG:
        ddbg["qpack"] = nc.dram_tensor("dbg_qpack", [128, 4 * 4096], BF16, kind="ExternalOutput").ap()
    if "qpk" in DEBUG:
        ddbg["qpack2"] = nc.dram_tensor("dbg_qpack2", [128, 4 * 4096], BF16, kind="ExternalOutput").ap()
    if "lnown" in DEBUG:
        ddbg["lnown"] = nc.dram_tensor("dbg_lnown", [128, FT * NQ], BF16, kind="ExternalOutput").ap()
    if "y" in DEBUG:
        ddbg["y"] = nc.dram_tensor("dbg_y", [128, FT * NQ], BF16, kind="ExternalOutput").ap()
    if "xattn" in DEBUG:
        ddbg["xattn"] = nc.dram_tensor("dbg_xattn", [128, FT * NQ], F32, kind="ExternalOutput").ap()
    if "ln2" in DEBUG:
        ddbg["ln2"] = nc.dram_tensor("dbg_ln2", [128, FT * NQ], BF16, kind="ExternalOutput").ap()
    if "comb" in DEBUG:
        ddbg["comb"] = nc.dram_tensor("dbg_comb", [NEXP, NQ], F32, kind="ExternalOutput").ap()

    sch = Sched()

    ARENA_BYTES = 204 * 1024
    arena_cm = nc.sbuf_tensor("arena", [128, ARENA_BYTES], mybir.dt.uint8)
    arena_h = arena_cm.__enter__()
    A = Arena(arena_h, ARENA_BYTES)

    psum_cm = nc.psum_tensor("psum", [128, 4096], F32)
    psum_h = psum_cm.__enter__()

    def PS(bank, c0=0, c1=512, p0=0, p1=128):
        return psum_h[p0:p1, bank * 512 + c0: bank * 512 + c1]

    o_I128b = A.alloc(128 * 2)
    o_I128f = A.alloc(128 * 4)
    o_ones1b = A.alloc(128 * 2)
    o_ones1f = A.alloc(128 * 4)
    o_onescol = A.alloc(32)
    o_bq = A.alloc(FT * 4)
    o_bk = A.alloc(4 * 4)
    o_bv = A.alloc(4 * 4)
    o_bo = A.alloc(FT * 4)

    I128b = A.view(o_I128b, 0, 128, 128, BF16)
    I128f = A.view(o_I128f, 0, 128, 128, F32)
    ones1b = A.view(o_ones1b, 0, 1, 128, BF16)
    ones1f = A.view(o_ones1f, 0, 1, 128, F32)
    onescol16 = A.view(o_onescol, 0, 128, 16, BF16)
    onescol = onescol16[:, 0:1]
    bq_t = A.view(o_bq, 0, 128, FT, F32)
    bk_t = A.view(o_bk, 0, 128, 4, F32)
    bv_t = A.view(o_bv, 0, 128, 4, F32)
    bo_t = A.view(o_bo, 0, 128, FT, F32)

    o_fc = A.alloc(8 * 4)
    fc = A.view(o_fc, 0, 128, 8, F32)
    const_load_later = []
    nc.const_aps.register(F32, 0.0, fc[:, 0:1])
    nc.const_aps.register(F32, EPS, fc[:, 1:2])

    def const_load(name, view, src):
        sch.op("sync", lambda v=view, s=src: nc.sync.dma_start(out=v, in_=s),
               writes=(name,), dma=name)

    const_load("c_I128b", I128b, dconst[:, 0:64].bitcast(BF16))
    const_load("c_I128f", I128f, dconst[:, 64:192])
    const_load("c_ones1b", ones1b, dconst[0:1, 192:256].bitcast(BF16))
    const_load("c_ones1f", ones1f, dconst[0:1, 256:384])
    const_load("c_onescol", onescol16, dconst[:, 384:392].bitcast(BF16))
    const_load("c_fc", fc, dconst[:, 392:400])
    const_load("c_bq", bq_t, dbq)
    const_load("c_bk", bk_t, dbk)
    const_load("c_bv", bv_t, dbv)
    const_load("c_bo", bo_t, dbo)

    o_warm = A.alloc(2048 * 2)
    warm = A.view(o_warm, 0, 128, 2048, BF16)

    def load_act_set(idx):
        return nc.scalar.add_instruction(
            mybir.InstLoadActFuncSet(
                name=nc.get_next_instruction_name(),
                act_func_set_id=idx, ins=[], outs=[]))

    sch.op("gpsimd", lambda: nc.gpsimd.memset(warm, 0.0), writes=("warm",))
    sch.op("scalar", lambda: load_act_set(6), writes=("acttab",))
    for _ in range(4):
        sch.op("scalar", lambda: nc.scalar.activation(
            out=warm, in_=warm, func=AF.Exp),
            reads=("acttab", "c_fc"), writes=("warm",))
    sch.barrier()

    o_BIG = A.alloc(FT * T * 2)
    ln1 = A.view(o_BIG, 0, 128, FT * T, BF16)

    def emit_ln(tag, src_dram, TL, out_view, out_tok_prefix):
        nch = TL // 512
        o_xt = [A.alloc(TL * 4), A.alloc(TL * 4)]
        o_xbf = [A.alloc(TL * 2), A.alloc(TL * 2)]
        o_x2bf = [A.alloc(TL * 2), A.alloc(TL * 2)]
        o_nm = A.alloc(TL * 4); o_rs = A.alloc(TL * 4)
        o_sa = A.alloc(TL * 4); o_sb2 = A.alloc(TL * 4)
        nm = A.view(o_nm, 0, 1, TL, F32); rs = A.view(o_rs, 0, 1, TL, F32)
        sa = A.view(o_sa, 0, 1, TL, F32); sb2 = A.view(o_sb2, 0, 1, TL, F32)
        from_sbuf = (src_dram[0] == "sbuf")
        for ft in range(FT):
            r = ft % 2
            xbf = A.view(o_xbf[r], 0, 128, TL, BF16)
            x2bf = A.view(o_x2bf[r], 0, 128, TL, BF16)
            if from_sbuf:
                xt = src_dram[1][:, ft * TL:(ft + 1) * TL]
                xtok = src_dram[2]
            else:
                xt = A.view(o_xt[r], 0, 128, TL, F32)
                xtok = f"{tag}xt{r}"
                sch.op("sync", lambda v=xt, f=ft, sd=src_dram[1]:
                       nc.sync.dma_start(out=v, in_=sd[f * 128:(f + 1) * 128, :]),
                       writes=(xtok,), dma=xtok)
            sch.op("scalar", lambda o=xbf, i=xt: nc.scalar.copy(out=o, in_=i),
                   reads=(xtok,), writes=(f"{tag}xbf{r}",))
            sch.op("scalar", lambda o=x2bf, i=xt:
                   nc.scalar.activation(out=o, in_=i, func=AF.Square),
                   reads=(xtok, "acttab"), writes=(f"{tag}x2bf{r}",))
            for cb in range(nch):
                for sti, (srcv, tok) in enumerate(
                        ((xbf, f"{tag}xbf{r}"), (x2bf, f"{tag}x2bf{r}"))):
                    bank = sti * nch + cb
                    sch.op("tensor",
                           lambda b=bank, sv=srcv, c=cb, f=ft:
                           nc.tensor.matmul(PS(b, 0, 512, 0, 1),
                                            onescol, sv[:, c * 512:(c + 1) * 512],
                                            start=(f == 0), stop=(f == FT - 1)),
                           reads=(tok, "c_onescol"), writes=(f"ps{bank}",),
                           nodrain=('A' if ft != 0 else False))
        for cb in range(nch):
            sch.op("vector", lambda c=cb, v=nm: nc.vector.tensor_scalar(
                out=v[:, c * 512:(c + 1) * 512], in0=PS(c, 0, 512, 0, 1),
                scalar1=-1.0 / E, scalar2=None, op0=OP.mult),
                reads=(f"ps{cb}",), writes=(f"{tag}nm",))
            sch.op("vector", lambda c=cb, v=sa, n=nch: nc.vector.tensor_scalar(
                out=v[:, c * 512:(c + 1) * 512], in0=PS(n + c, 0, 512, 0, 1),
                scalar1=1.0 / E, scalar2=None, op0=OP.mult),
                reads=(f"ps{nch+cb}",), writes=(f"{tag}sa",))
        sch.op("vector", lambda a=sb2, b=nm: nc.vector.tensor_tensor(
            out=a, in0=b, in1=b, op=OP.mult),
            reads=(f"{tag}nm",), writes=(f"{tag}sb",))
        sch.op("vector", lambda a=sa, b=sb2: nc.vector.tensor_tensor(
            out=a, in0=a, in1=b, op=OP.subtract),
            reads=(f"{tag}sa", f"{tag}sb"), writes=(f"{tag}sa",))
        sch.op("scalar", lambda a=sb2, b=sa: nc.scalar.activation(
            out=a, in_=b, func=AF.Ln, bias=EPS),
            reads=(f"{tag}sa", "acttab"), writes=(f"{tag}sb",))
        sch.op("scalar", lambda a=rs, b=sb2: nc.scalar.activation(
            out=a, in_=b, func=AF.Exp, scale=-0.5),
            reads=(f"{tag}sb",), writes=(f"{tag}rs",))
        nhalf = TL // 1024
        for half in range(nhalf):
            for cb in range(2):
                c0 = half * 1024 + cb * 512
                bnm = half * 2 + cb
                brs = 2 * nhalf + half * 2 + cb
                sch.op("tensor", lambda b=bnm, c=c0, v=nm: nc.tensor.matmul(
                    PS(b), ones1f, v[:, c:c + 512], start=True, stop=True),
                    reads=(f"{tag}nm", "c_ones1f"), writes=(f"ps{bnm}",),
                    nodrain="F")
                sch.op("tensor", lambda b=brs, c=c0, v=rs: nc.tensor.matmul(
                    PS(b), ones1f, v[:, c:c + 512], start=True, stop=True),
                    reads=(f"{tag}rs", "c_ones1f"), writes=(f"ps{brs}",),
                    nodrain="F")
        o_tmpf = [A.alloc(1024 * 4), A.alloc(1024 * 4)]
        for ft in range(FT):
            r = ft % 2
            if from_sbuf:
                xt = src_dram[1][:, ft * TL:(ft + 1) * TL]
                xtok = src_dram[2]
            else:
                xt = A.view(o_xt[r], 0, 128, TL, F32)
                xtok = f"{tag}xt{r}"
                sch.op("sync", lambda v=xt, f=ft, sd=src_dram[1]:
                       nc.sync.dma_start(out=v, in_=sd[f * 128:(f + 1) * 128, :]),
                       writes=(xtok,), dma=xtok)
            for half in range(nhalf):
                tmpf = A.view(o_tmpf[r], 0, 128, 1024, F32)
                hb = half * 2
                rb = 2 * nhalf + hb
                sch.op("vector", lambda v=xt, h=half, t=tmpf, b=hb:
                       nc.vector.tensor_tensor(
                           out=t, in0=v[:, h * 1024:(h + 1) * 1024],
                           in1=psum_h[:, b * 512:(b + 2) * 512], op=OP.add),
                       reads=(xtok, f"ps{hb}", f"ps{hb+1}"),
                       writes=(f"{tag}tmp{r}",))
                sch.op("vector", lambda t=tmpf, f=ft, h=half, b=rb, ov=out_view:
                       nc.vector.tensor_tensor(
                           out=ov[:, f * TL + h * 1024: f * TL + (h + 1) * 1024],
                           in0=t, in1=psum_h[:, b * 512:(b + 2) * 512], op=OP.mult),
                       reads=(f"{tag}tmp{r}", f"ps{rb}", f"ps{rb+1}"),
                       writes=(f"{out_tok_prefix}{ft}",))
        sch.barrier()
        for off in o_xt:
            A.free(off, TL * 4)
        for off in o_xbf + o_x2bf:
            A.free(off, TL * 2)
        A.free(o_tmpf[0], 1024 * 4); A.free(o_tmpf[1], 1024 * 4)
        A.free(o_nm, TL * 4); A.free(o_rs, TL * 4)
        A.free(o_sa, TL * 4); A.free(o_sb2, TL * 4)

    emit_ln("l1", ("dram", dx), T, ln1, "ln1_")

    if "ln1" in DEBUG:
        o_dbg = A.alloc(T * 4)
        dbgt = A.view(o_dbg, 0, 128, T, F32)
        for ft in range(FT):
            sch.op("vector", lambda f=ft: nc.vector.tensor_copy(
                dbgt, ln1[:, f * T:(f + 1) * T]),
                reads=(f"ln1_{ft}",), writes=("dbgt",))
            sch.op("sync", lambda f=ft: nc.sync.dma_start(
                out=ddbg["ln1"][f * 128:(f + 1) * 128, :], in_=dbgt),
                reads=("dbgt",), writes=(f"dbgo{ft}",), dma="dbg")
            sch.barrier()
        A.free(o_dbg, T * 4)

    o_wb = [A.alloc(T * 2), A.alloc(T * 2)]
    o_strip = [A.alloc(T * 2), A.alloc(T * 2)]

    def emit_strip(r, src_rows_ap, tbank):
        wb = A.view(o_wb[r], 0, 128, T, BF16)
        strip = A.view(o_strip[r], 0, 128, T, BF16)
        sch.op("gpsimd", lambda v=wb, sr=src_rows_ap:
               nc.gpsimd.dma_start(out=v, in_=sr),
               writes=(f"wb{r}",), dma=f"wb{r}")
        for grp in range(4):
            for ii in range(4):
                fi = grp * 4 + ii
                sch.op("tensor", lambda b=tbank, w=wb, f=fi, i2=ii:
                       nc.tensor.matmul(PS(b, i2 * 128, (i2 + 1) * 128),
                                        w[:, f * 128:(f + 1) * 128], I128b,
                                        start=True, stop=True),
                       reads=(f"wb{r}", "c_I128b"), writes=(f"ps{tbank}",),
                       nodrain="B")
            sch.op("vector", lambda b=tbank, st=strip, g2=grp:
                   nc.vector.tensor_copy(st[:, g2 * 512:(g2 + 1) * 512], PS(b)),
                   reads=(f"ps{tbank}",), writes=(f"strip{r}",))
        return strip

    o_lnown = A.alloc(FT * NQ * 2)
    lnown = A.view(o_lnown, 0, 128, FT * NQ, BF16)
    emit_ln("lo", ("dram", dxo), NQ, lnown, "lno_")

    if "lnown" in DEBUG:
        sch.op("sync", lambda: nc.sync.dma_start(out=ddbg["lnown"], in_=lnown),
               reads=tuple(f"lno_{i}" for i in range(FT)), writes=("dlo",), dma="dbg2")
        sch.barrier()

    o_cqc = A.alloc(NQ * 4); o_cqs = A.alloc(NQ * 4)
    cqc = A.view(o_cqc, 0, 128, NQ, F32)
    cqs = A.view(o_cqs, 0, 128, NQ, F32)
    const_load("c_cqc", cqc, dcqc)
    const_load("c_cqs", cqs, dcqs)
    o_Qg = A.alloc(16 * NQ * 2)
    Qg = A.view(o_Qg, 0, 128, 16 * NQ, BF16)
    o_qb = A.alloc(NQ * 4); o_x2q = A.alloc(NQ * 4)
    qb = A.view(o_qb, 0, 128, NQ, F32)
    x2q = A.view(o_x2q, 0, 128, NQ, F32)
    for fo in range(FT):
        r = fo % 2
        strip = emit_strip(r, dWq[fo * 128:(fo + 1) * 128, :], r)
        qpb = (2 + 2 * r, 3 + 2 * r)
        ptoks = (f"ps{qpb[0]}", f"ps{qpb[1]}")
        for fi in range(FT):
            for cb in range(2):
                sch.op("tensor", lambda b=qpb[cb], st=strip, f=fi, c=cb,
                       fr=(fi == 0), la=(fi == FT - 1):
                       nc.tensor.matmul(PS(b), st[:, f * 128:(f + 1) * 128],
                                        lnown[:, f * NQ + c * 512: f * NQ + c * 512 + 512],
                                        start=fr, stop=la),
                       reads=(f"strip{r}", f"lno_{fi}"), writes=(ptoks[cb],),
                       nodrain=('A' if fi != 0 else False))
        sch.op("vector", lambda b=qpb[0], f=fo: nc.vector.tensor_scalar(
            out=qb, in0=psum_h[:, b * 512:(b + 2) * 512],
            scalar1=bq_t[:, f:f + 1], scalar2=None, op0=OP.add),
            reads=ptoks + ("c_bq",), writes=("qb",))
        for (d0, d1, sg0, sg1, sgn) in ((0, 32, 32, 64, -1.0), (32, 64, 0, 32, 1.0),
                                        (64, 96, 96, 128, -1.0), (96, 128, 64, 96, 1.0)):
            sch.op("vector", lambda a=d0, b=d1, c=sg0, d=sg1, s=sgn:
                   nc.vector.tensor_scalar(
                       out=x2q[a:b, :], in0=qb[c:d, :], scalar1=s,
                       scalar2=None, op0=OP.mult),
                   reads=("qb",), writes=("x2q",))
        sch.op("vector", lambda: nc.vector.tensor_tensor(
            out=qb, in0=qb, in1=cqc, op=OP.mult),
            reads=("qb", "c_cqc"), writes=("qb",))
        sch.op("vector", lambda: nc.vector.tensor_tensor(
            out=x2q, in0=x2q, in1=cqs, op=OP.mult),
            reads=("x2q", "c_cqs"), writes=("x2q",))
        for half in range(2):
            sch.op("vector", lambda h=half, f=fo: nc.vector.tensor_tensor(
                out=Qg[64 * h:64 * h + 64, f * NQ:(f + 1) * NQ],
                in0=qb[64 * h:64 * h + 64, :], in1=x2q[64 * h:64 * h + 64, :],
                op=OP.add),
                reads=("qb", "x2q"), writes=("qg",))
    sch.barrier()
    A.free(o_lnown, FT * NQ * 2)
    A.free(o_qb, NQ * 4); A.free(o_x2q, NQ * 4)
    A.free(o_cqc, NQ * 4); A.free(o_cqs, NQ * 4)
    if "qpack" in DEBUG or "qpk" in DEBUG:
        sch.op("sync", lambda: nc.sync.dma_start(out=ddbg["qpack"], in_=Qg),
               reads=("qg",), writes=("dqp",), dma="dbg2")
        sch.barrier()

    o_ckc = A.alloc(T * 4); o_cks = A.alloc(T * 4)
    ckc = A.view(o_ckc, 0, 128, T, F32)
    cks = A.view(o_cks, 0, 128, T, F32)
    const_load("c_ckc", ckc, dckc)
    const_load("c_cks", cks, dcks)
    o_K = A.alloc(4 * T * 2)
    o_Ksw = A.alloc(4 * T * 2)
    kpk = A.view(o_K, 0, 128, 4 * T, BF16)
    ksw = A.view(o_Ksw, 0, 128, 4 * T, BF16)
    o_V = A.alloc(16 * KVD * 2)
    vtm = A.view(o_V, 0, 128, 16 * KVD, BF16)
    o_kb = A.alloc(T * 4); o_x2k = A.alloc(T * 4)
    kb = A.view(o_kb, 0, 128, T, F32)
    x2k = A.view(o_x2k, 0, 128, T, F32)
    o_vbf = A.alloc(T * 2)
    vbf = A.view(o_vbf, 0, 128, T, BF16)
    kvp = (4, 5, 6, 7)
    kvptoks = tuple(f"ps{b}" for b in kvp)
    for m, dW, bias_t, btok in ((("k", dWk, bk_t, "c_bk"),) if "konly" in DEBUG else (("k", dWk, bk_t, "c_bk"), ("v", dWv, bv_t, "c_bv"))):
        for fv in range(4):
            r = fv % 2
            strip = emit_strip(r, dW[fv * 128:(fv + 1) * 128, :], r)
            for fi in range(FT):
                for cb in range(4):
                    sch.op("tensor", lambda b=kvp[cb], st=strip, f=fi, c=cb,
                           fr=(fi == 0), la=(fi == FT - 1):
                           nc.tensor.matmul(PS(b), st[:, f * 128:(f + 1) * 128],
                                            ln1[:, f * T + c * 512: f * T + c * 512 + 512],
                                            start=fr, stop=la),
                           reads=(f"strip{r}", f"ln1_{fi}"), writes=(f"ps{kvp[cb]}",),
                           nodrain=('A' if fi != 0 else False))
            if m == "k":
                sch.op("vector", lambda f=fv, bt=bias_t: nc.vector.tensor_scalar(
                    out=kb, in0=psum_h[:, 4 * 512: 8 * 512],
                    scalar1=bt[:, f:f + 1], scalar2=None, op0=OP.add),
                    reads=kvptoks + (btok,), writes=("kb",))
                for (d0, d1, sg0, sg1, sgn) in ((0, 32, 32, 64, -1.0), (32, 64, 0, 32, 1.0),
                                                (64, 96, 96, 128, -1.0), (96, 128, 64, 96, 1.0)):
                    sch.op("vector", lambda a=d0, b=d1, c=sg0, d=sg1, s=sgn:
                           nc.vector.tensor_scalar(
                               out=x2k[a:b, :], in0=kb[c:d, :], scalar1=s,
                               scalar2=None, op0=OP.mult),
                           reads=("kb",), writes=("x2k",))
                sch.op("vector", lambda: nc.vector.tensor_tensor(
                    out=kb, in0=kb, in1=ckc, op=OP.mult),
                    reads=("kb", "c_ckc"), writes=("kb",))
                sch.op("vector", lambda: nc.vector.tensor_tensor(
                    out=x2k, in0=x2k, in1=cks, op=OP.mult),
                    reads=("x2k", "c_cks"), writes=("x2k",))
                sch.op("vector", lambda f=fv: nc.vector.tensor_tensor(
                    out=kpk[:, f * T:(f + 1) * T], in0=kb, in1=x2k, op=OP.add),
                    reads=("kb", "x2k"), writes=("kpack",))
            else:
                sch.op("vector", lambda f=fv, bt=bias_t: nc.vector.tensor_scalar(
                    out=vbf, in0=psum_h[:, 4 * 512: 8 * 512],
                    scalar1=bt[:, f:f + 1], scalar2=None, op0=OP.add),
                    reads=kvptoks + (btok,), writes=("vbf",))
                for ktg in range(4):
                    for ii in range(4):
                        kt = ktg * 4 + ii
                        sch.op("tensor", lambda k=kt, i2=ii:
                               nc.tensor.matmul(PS(2, i2 * 128, (i2 + 1) * 128),
                                                vbf[:, k * 128:(k + 1) * 128], I128b,
                                                start=True, stop=True),
                               reads=("vbf", "c_I128b"), writes=("ps2",),
                               nodrain="B")
                    sch.op("vector", lambda kg=ktg, f=fv: nc.vector.tensor_copy(
                        vtm.rearrange("p (kt v) -> p kt v", v=KVD)[
                            :, kg * 4:(kg + 1) * 4, f * 128:(f + 1) * 128],
                        PS(2).rearrange("p (a b) -> p a b", b=128)),
                        reads=("ps2",), writes=("vtm",))
    if "qpk" in DEBUG:
        sch.barrier()
        sch.op("sync", lambda: nc.sync.dma_start(out=ddbg["qpack2"], in_=Qg),
               reads=("qg",), writes=("dqp2",), dma="dbg3")
        sch.barrier()
    sch.op("sync", lambda: nc.sync.dma_start(out=ksw[0:64, :], in_=kpk[64:128, :]),
           reads=("kpack",), writes=("kswap",), dma="ksw")
    sch.op("sync", lambda: nc.sync.dma_start(out=ksw[64:128, :], in_=kpk[0:64, :]),
           reads=("kpack",), writes=("kswap",), dma="ksw")
    A.free(o_kb, T * 4); A.free(o_x2k, T * 4); A.free(o_vbf, T * 2)
    A.free(o_ckc, T * 4); A.free(o_cks, T * 4)
    sch.barrier()

    if "kpack" in DEBUG:
        sch.op("sync", lambda: nc.sync.dma_start(out=ddbg["kpack"], in_=kpk),
               reads=("kpack",), writes=("dkp",), dma="dbg2")
        sch.op("sync", lambda: nc.sync.dma_start(out=ddbg["vtm"], in_=vtm),
               reads=("vtm",), writes=("dvt",), dma="dbg2")
        sch.barrier()

    A.free(o_BIG, FT * T * 2)
    o_y = A.alloc(16 * NQ * 2)
    yv = A.view(o_y, 0, 128, 16 * NQ, BF16)
    o_maskT = A.alloc(NBLK * 1024 * 2)
    maskT = A.view(o_maskT, 0, 128, NBLK * 1024, BF16)
    o_PT = A.alloc(16 * 512 * 2)
    PT = A.view(o_PT, 0, 128, 16 * 512, BF16)
    o_maskS = A.alloc(NBLK * 256 * 2)
    maskS = A.view(o_maskS, 0, 128, NBLK * 256, BF16)
    const_load("c_maskS", maskS, dmaskS)
    const_load("c_maskT", maskT, dmaskT)
    o_sel4 = A.alloc(512 * 4)
    sel4f = A.view(o_sel4, 0, 4, 512, F32)
    const_load("c_sel4f", sel4f, dconst[0:4, 400:912])
    sch.barrier()
    o_nm4 = A.alloc(4 * 4); o_Z4 = A.alloc(4 * 4)
    o_lnZ = A.alloc(4 * 4); o_nC = A.alloc(4 * 4)
    nm4 = A.view(o_nm4, 0, 128, 4, F32)
    Z4 = A.view(o_Z4, 0, 128, 4, F32)
    lnZ = A.view(o_lnZ, 0, 128, 4, F32)
    nC4 = A.view(o_nC, 0, 128, 4, F32)
    o_nCT = A.alloc(512)
    nCT = A.view(o_nCT, 0, 4, 128, F32)
    o_ncb = A.alloc(512 * 4)
    ncb = A.view(o_ncb, 0, 128, 512, F32)

    o_nm4b = A.alloc(4 * 4); o_Z4b = A.alloc(4 * 4)
    o_lnZb = A.alloc(4 * 4); o_nCb = A.alloc(4 * 4)
    o_nCTb = A.alloc(512); o_ncbb = A.alloc(512 * 4)
    o_cm4a = A.alloc(16 * 4); o_cm4b = A.alloc(16 * 4)
    o_stscr = A.alloc(512 * 4)
    o_stscr2 = A.alloc(512 * 4)
    nm4s = (nm4, A.view(o_nm4b, 0, 128, 4, F32))
    cm4s = (A.view(o_cm4a, 0, 128, 16, F32), A.view(o_cm4b, 0, 128, 16, F32))
    stscr_bufs = (A.view(o_stscr, 0, 128, 512, F32),
                  A.view(o_stscr2, 0, 128, 512, F32))
    Z4s = (Z4, A.view(o_Z4b, 0, 128, 4, F32))
    lnZs = (lnZ, A.view(o_lnZb, 0, 128, 4, F32))
    nC4s = (nC4, A.view(o_nCb, 0, 128, 4, F32))
    nCTs = (nCT, A.view(o_nCTb, 0, 4, 128, F32))
    ncbs = (ncb, A.view(o_ncbb, 0, 128, 512, F32))
    POSG = {0: 0, 2: 1, 1: 2, 3: 3}

    def emit_attn_S(h, qt, par):
        hc = h // 2
        NK = NKEYS[qt]
        ncb_nch = (NK + 511) // 512
        sptoks = tuple(f"ps{b}" for b in range(ncb_nch))
        nm4v, Z4v, lnZv, nC4v, nCTv, ncbv = (
            nm4s[par], Z4s[par], lnZs[par], nC4s[par], nCTs[par], ncbs[par])
        cm4v = cm4s[par]
        rot = (ncb_nch <= 2)
        for g in range(4):
            gp = g % 2
            base = 64 * gp
            b0 = (g % 2) * 2 if rot else 0
            gptoks = tuple(f"ps{b0 + b}" for b in range(ncb_nch))
            kbuf, ktok = (kpk, "kpack") if (h % 2) == gp else (ksw, "kswap")
            lhsq = Qg[base:base + 64,
                      (2 * h + g // 2) * NQ + qt * 128: (2 * h + g // 2) * NQ + qt * 128 + 128]
            for cb in range(ncb_nch):
                n = min(512, NK - cb * 512)
                sch.op("tensor", lambda b=b0 + cb, lq=lhsq, kv=kbuf, nn=n,
                       c=cb, bs=base, h2=hc:
                       nc.tensor.matmul(
                           PS(b, 0, nn), lq,
                           kv[bs:bs + 64, h2 * T + c * 512: h2 * T + c * 512 + nn],
                           start=True, stop=True),
                       reads=("qg", ktok), writes=(f"ps{b0+cb}",),
                       nodrain="C")
            c0 = b0 * 512
            for cb in range(ncb_nch - 1):
                n = min(512, NK - cb * 512)
                sch.op("vector", lambda b=b0 + cb, g2=g, c=cb, nn=n, v=cm4v:
                       nc.vector.tensor_reduce(
                           out=v[:, g2 * 4 + c:g2 * 4 + c + 1],
                           in_=PS(b, 0, nn), axis=AX.X, op=OP.max),
                       reads=(f"ps{b0+cb}",), writes=(f"cm{g}_{par}",))
            sch.op("vector", lambda q2=qt, nk=NK, cc=c0:
                   nc.vector.tensor_tensor(
                       out=psum_h[:, cc + nk - 256: cc + nk],
                       in0=psum_h[:, cc + nk - 256: cc + nk],
                       in1=maskS[:, q2 * 256:(q2 + 1) * 256], op=OP.add),
                   reads=(f"ps{b0+ncb_nch-1}",) + ((f"ps{b0+ncb_nch-2}",) if NK % 512 == 256 and ncb_nch > 1 else ()),
                   writes=(f"ps{b0+ncb_nch-1}",) + ((f"ps{b0+ncb_nch-2}",) if NK % 512 == 256 and ncb_nch > 1 else ()))
            lb = ncb_nch - 1
            ln_ = NK - lb * 512
            sch.op("vector", lambda b=b0 + lb, g2=g, c=lb, nn=ln_, v=cm4v:
                   nc.vector.tensor_reduce(
                       out=v[:, g2 * 4 + c:g2 * 4 + c + 1],
                       in_=PS(b, 0, nn), axis=AX.X, op=OP.max),
                   reads=(f"ps{b0+lb}",), writes=(f"cm{g}_{par}",))
            sch.op("vector", lambda g2=g, nch=ncb_nch, v=cm4v, o=nm4v:
                   nc.vector.tensor_reduce(
                       out=o[:, g2:g2 + 1], in_=v[:, g2 * 4:g2 * 4 + nch],
                       axis=AX.X, op=OP.max, negate=True),
                   reads=(f"cm{g}_{par}",), writes=(f"nm{g}_{par}",))
            sch.op("scalar", lambda g2=g, nk=NK, v=nm4v, zv=Z4v, cc=c0:
                   nc.scalar.activation(
                       out=psum_h[:, cc:cc + nk], in_=psum_h[:, cc:cc + nk],
                       func=AF.Exp, bias=v[:, g2:g2 + 1],
                       accum_out=zv[:, g2:g2 + 1]),
                   reads=gptoks + (f"nm{g}_{par}", "acttab"),
                   writes=gptoks + (f"Z{g}_{par}",))
        zt = tuple(f"Z{g}_{par}" for g in range(4))
        nmt = tuple(f"nm{g}_{par}" for g in range(4))
        sch.op("scalar", lambda zv=Z4v, lv=lnZv: nc.scalar.activation(
            out=lv, in_=zv, func=AF.Ln),
            reads=zt + ("acttab",), writes=(f"lnZ_{par}",))
        sch.op("vector", lambda nv=nC4v, mv=nm4v, lv=lnZv: nc.vector.tensor_tensor(
            out=nv, in0=mv, in1=lv, op=OP.subtract),
            reads=nmt + (f"lnZ_{par}",), writes=(f"nC_{par}",))

    def emit_negc_tensor(par):
        nC4v, nCTv, ncbv = nC4s[par], nCTs[par], ncbs[par]
        sch.op("tensor", lambda nv=nC4v: nc.tensor.matmul(
            PS(6, 0, 128, 0, 4), nv, I128f, start=True, stop=True),
            reads=(f"nC_{par}", "c_I128f"), writes=("ps6",), nodrain="E")
        sch.op("vector", lambda tv=nCTv: nc.vector.tensor_copy(
            tv, PS(6, 0, 128, 0, 4)),
            reads=("ps6",), writes=(f"nCT_{par}",))
        for g in range(4):
            pos = POSG[g]
            sch.op("tensor", lambda g2=g, p2=pos, tv=nCTv: nc.tensor.matmul(
                PS(6, p2 * 128, (p2 + 1) * 128),
                sel4f[:, g2 * 128:(g2 + 1) * 128], tv,
                start=True, stop=True),
                reads=(f"nCT_{par}", "c_sel4f"), writes=("ps6",), nodrain="E")
        sch.op("vector", lambda bv=ncbv: nc.vector.tensor_copy(bv, PS(6)),
               reads=("ps6",), writes=(f"ncbt_{par}",))

    def emit_attn_STAV(h, qt, par):
        hc = h // 2
        NK = NKEYS[qt]
        nkt = NK // 128
        ncbv = ncbs[par]
        emit_negc_tensor(par)
        for kt in range(nkt):
            pbA = 4 if kt % 2 == 0 else 0
            pbB = pbA + 1
            for g in range(4):
                gp = g % 2
                base = 64 * gp
                pb = pbA if gp == 0 else pbB
                col = (POSG[g] % 2) * 128
                kbuf, ktok = (kpk, "kpack") if (h % 2) == gp else (ksw, "kswap")
                sch.op("tensor", lambda b=pb, c2=col, k=kt, kv=kbuf, bs=base,
                       h2=hc, hh=h, q2=qt, g2=g:
                       nc.tensor.matmul(
                           PS(b, c2, c2 + 128),
                           kv[bs:bs + 64, h2 * T + k * 128: h2 * T + (k + 1) * 128],
                           Qg[bs:bs + 64, (2 * hh + g2 // 2) * NQ + q2 * 128:
                              (2 * hh + g2 // 2) * NQ + q2 * 128 + 128],
                           start=True, stop=True),
                       reads=("qg", ktok), writes=(f"ps{pb}",),
                       nodrain="D")
            span = psum_h[:, pbA * 512: pbA * 512 + 1024].rearrange(
                "p (b c) -> p b c", c=512)[:, :, 0:256]
            ptoks2 = (f"ps{pbA}", f"ps{pbB}")
            stscr = stscr_bufs[kt % 2]
            sttok = f"stscr{kt % 2}"
            sch.op("vector", lambda sp=span, bv=ncbv, sc=stscr:
                   nc.vector.tensor_tensor(
                       out=sc.rearrange("p (b c) -> p b c", c=256), in0=sp,
                       in1=bv.rearrange("p (b c) -> p b c", c=256), op=OP.add),
                   reads=ptoks2 + (f"ncbt_{par}",), writes=(sttok,))
            if kt >= nkt - 2:
                kt2 = kt - (nkt - 2)
                sch.op("vector", lambda q2=qt, k2=kt2, sc=stscr:
                       nc.vector.tensor_tensor(
                           out=sc, in0=sc,
                           in1=maskT[:, q2 * 1024 + k2 * 512:
                                     q2 * 1024 + (k2 + 1) * 512],
                           op=OP.add),
                       reads=(sttok, "c_maskT"), writes=(sttok,))
            sch.op("scalar", lambda k=kt, sc=stscr: nc.scalar.activation(
                out=PT[:, k * 512:(k + 1) * 512], in_=sc, func=AF.Exp),
                reads=(sttok, "acttab"), writes=(f"PT{kt}",))
        for gp2 in range(2):
            for gg in range(2):
                g = 2 * gp2 + gg
                colb = 64 * gg
                pos = POSG[g]
                for kt in range(nkt):
                    sch.op("tensor", lambda k=kt, p2=pos, cb2=colb, hh=h,
                           fr=(kt == 0), la=(kt == nkt - 1):
                           nc.tensor.matmul(
                               PS(7, 0, 128, cb2, cb2 + 64),
                               vtm[:, k * 512 + hh * 64: k * 512 + (hh + 1) * 64],
                               PT[:, k * 512 + p2 * 128: k * 512 + (p2 + 1) * 128],
                               start=fr, stop=la, tile_position=(0, cb2)),
                           reads=(f"PT{kt}", "vtm"), writes=("ps7",),
                           nodrain=('A' if kt != 0 else False))
            sch.op("vector", lambda gp3=gp2, hh=h, q2=qt: nc.vector.tensor_copy(
                yv[:, (2 * hh + gp3) * NQ + q2 * 128: (2 * hh + gp3) * NQ + q2 * 128 + 128],
                PS(7, 0, 128)),
                reads=("ps7",), writes=(f"y_{2*h+gp2}",))

    ITERS = [(h, qt) for h in range(HKV) for qt in range(NBLK)]
    for i, (h, qt) in enumerate(ITERS):
        emit_attn_S(h, qt, i % 2)
        emit_attn_STAV(h, qt, i % 2)

    sch.barrier()
    A.free(o_PT, 16 * 512 * 2)
    A.free(o_K, 4 * T * 2); A.free(o_Ksw, 4 * T * 2)
    A.free(o_V, 16 * KVD * 2)
    A.free(o_Qg, 16 * NQ * 2)
    A.free(o_nm4, 16); A.free(o_Z4, 16); A.free(o_lnZ, 16); A.free(o_nC, 16)
    A.free(o_nCT, 512); A.free(o_ncb, 2048)
    A.free(o_maskS, NBLK * 256 * 2); A.free(o_maskT, NBLK * 1024 * 2)
    A.free(o_sel4, 512 * 4)

    if "y" in DEBUG:
        sch.op("sync", lambda: nc.sync.dma_start(out=ddbg["y"], in_=yv),
               reads=tuple(f"y_{i}" for i in range(16)), writes=("dy",), dma="dbg2")
        sch.barrier()

    o_xown = A.alloc(FT * NQ * 4)
    x_own = A.view(o_xown, 0, 128, FT * NQ, F32)
    sch.op("sync", lambda: nc.sync.dma_start(
        out=x_own.rearrange("p (a b) -> p a b", b=NQ),
        in_=bass.AP(hxo, 0, [[NQ, 128], [128 * NQ, FT], [1, NQ]])),
        writes=("xown",), dma="xod")
    for fo in range(FT):
        r = fo % 2
        strip = emit_strip(r, dWo[fo * 128:(fo + 1) * 128, :], r)
        qpb = (2 + 2 * r, 3 + 2 * r)
        ptoks = (f"ps{qpb[0]}", f"ps{qpb[1]}")
        for fi in range(FT):
            for cb in range(2):
                sch.op("tensor", lambda b=qpb[cb], st=strip, f=fi, c=cb,
                       fr=(fi == 0), la=(fi == FT - 1):
                       nc.tensor.matmul(PS(b), st[:, f * 128:(f + 1) * 128],
                                        yv[:, f * NQ + c * 512: f * NQ + c * 512 + 512],
                                        start=fr, stop=la),
                       reads=(f"strip{r}", f"y_{fi}"), writes=(ptoks[cb],),
                       nodrain=('A' if fi != 0 else False))
        sch.op("vector", lambda b=qpb[0], f=fo: nc.vector.scalar_tensor_tensor(
            out=x_own[:, f * NQ:(f + 1) * NQ],
            in0=psum_h[:, b * 512:(b + 2) * 512],
            scalar=bo_t[:, f:f + 1], in1=x_own[:, f * NQ:(f + 1) * NQ],
            op0=OP.add, op1=OP.add),
            reads=ptoks + ("c_bo", "xown"), writes=("xown",))
    sch.barrier()
    A.free(o_y, 16 * NQ * 2)

    if "xattn" in DEBUG:
        sch.op("sync", lambda: nc.sync.dma_start(out=ddbg["xattn"], in_=x_own),
               reads=("xown",), writes=("dxa",), dma="dbg2")
        sch.barrier()

    A.free(o_wb[0], T * 2); A.free(o_wb[1], T * 2)
    A.free(o_strip[0], T * 2); A.free(o_strip[1], T * 2)
    o_ln2 = A.alloc(FT * NQ * 2)
    ln2v = A.view(o_ln2, 0, 128, FT * NQ, BF16)
    emit_ln("l2", ("sbuf", x_own, "xown"), NQ, ln2v, "ln2_")
    ln2toks = tuple(f"ln2_{i}" for i in range(FT))

    if "ln2" in DEBUG:
        sch.op("sync", lambda: nc.sync.dma_start(out=ddbg["ln2"], in_=ln2v),
               reads=ln2toks, writes=("dl2",), dma="dbg2")
        sch.barrier()

    o_wgr = A.alloc(FT * 8 * 2)
    wgr = A.view(o_wgr, 0, 128, FT * 8, BF16)
    sch.op("gpsimd", lambda: nc.gpsimd.dma_start(
        out=wgr.rearrange("p (a b) -> p a b", b=8),
        in_=bass.AP(hWg_r, 0, [[NEXP, 128], [128 * NEXP, FT], [1, NEXP]])),
        writes=("wgr",), dma="wgr")
    for fi in range(FT):
        for cb in range(2):
            sch.op("tensor", lambda f=fi, c=cb, fr=(fi == 0), la=(fi == FT - 1):
                   nc.tensor.matmul(PS(c, 0, 512, 0, 8),
                                    wgr[:, f * 8:(f + 1) * 8],
                                    ln2v[:, f * NQ + c * 512: f * NQ + c * 512 + 512],
                                    start=fr, stop=la),
                   reads=("wgr", f"ln2_{fi}"), writes=(f"ps{cb}",),
                   nodrain=('A' if fi != 0 else False))
    o_lg = A.alloc(NQ * 4)
    lg = A.view(o_lg, 0, 8, NQ, F32)
    sch.op("vector", lambda: nc.vector.tensor_copy(
        lg, psum_h[0:8, 0:1024]), reads=("ps0", "ps1"), writes=("lg",))
    for tt in range(8):
        sch.op("tensor", lambda t=tt: nc.tensor.matmul(
            PS(2, t * 8, (t + 1) * 8), lg[:, t * 128:(t + 1) * 128],
            I128f[0:8, 0:8], start=True, stop=True),
            reads=("lg", "c_I128f"), writes=("ps2",), nodrain="E")
    o_lgt = A.alloc(64 * 4); o_elg = A.alloc(64 * 4)
    o_m1 = A.alloc(8 * 4); o_m1n = A.alloc(8 * 4); o_m2 = A.alloc(8 * 4)
    o_zr = A.alloc(8 * 4); o_msk = A.alloc(64 * 4); o_lg2 = A.alloc(64 * 4)
    o_cmb = A.alloc(64 * 4)
    lgt = A.view(o_lgt, 0, 128, 64, F32)
    elg = A.view(o_elg, 0, 128, 64, F32)
    m1 = A.view(o_m1, 0, 128, 8, F32)
    m1n = A.view(o_m1n, 0, 128, 8, F32)
    m2v = A.view(o_m2, 0, 128, 8, F32)
    zr = A.view(o_zr, 0, 128, 8, F32)
    msk = A.view(o_msk, 0, 128, 64, F32)
    lg2 = A.view(o_lg2, 0, 128, 64, F32)
    cmb = A.view(o_cmb, 0, 128, 64, F32)
    sch.op("vector", lambda: nc.vector.tensor_copy(lgt, PS(2, 0, 64)),
           reads=("ps2",), writes=("lgt",))
    for tt in range(8):
        sl = slice(tt * 8, (tt + 1) * 8)
        sch.op("vector", lambda t=tt, s2=sl: nc.vector.tensor_reduce(
            out=m1[:, t:t + 1], in_=lgt[:, s2], axis=AX.X, op=OP.max),
            reads=("lgt",), writes=(f"m1_{tt}",))
        sch.op("vector", lambda t=tt: nc.vector.tensor_scalar(
            out=m1n[:, t:t + 1], in0=m1[:, t:t + 1], scalar1=-1.0,
            scalar2=None, op0=OP.mult),
            reads=(f"m1_{tt}",), writes=(f"m1n_{tt}",))
        sch.op("scalar", lambda t=tt, s2=sl: nc.scalar.activation(
            out=elg[:, s2], in_=lgt[:, s2], func=AF.Exp,
            bias=m1n[:, t:t + 1], accum_out=zr[:, t:t + 1]),
            reads=("lgt", f"m1n_{tt}", "acttab"), writes=(f"elg_{tt}", f"zr_{tt}"))
        sch.op("vector", lambda t=tt: nc.vector.reciprocal(
            out=zr[:, t:t + 1], in_=zr[:, t:t + 1]),
            reads=(f"zr_{tt}",), writes=(f"zr_{tt}",))
        sch.op("vector", lambda t=tt, s2=sl: nc.vector.tensor_scalar(
            out=msk[:, s2], in0=lgt[:, s2], scalar1=m1[:, t:t + 1],
            scalar2=None, op0=OP.is_ge),
            reads=("lgt", f"m1_{tt}"), writes=(f"msk_{tt}",))
        sch.op("vector", lambda t=tt, s2=sl: nc.vector.scalar_tensor_tensor(
            out=lg2[:, s2], in0=msk[:, s2], scalar=-30000.0,
            in1=lgt[:, s2], op0=OP.mult, op1=OP.add),
            reads=(f"msk_{tt}", "lgt"), writes=(f"lg2_{tt}",))
        sch.op("vector", lambda t=tt, s2=sl: nc.vector.tensor_reduce(
            out=m2v[:, t:t + 1], in_=lg2[:, s2], axis=AX.X, op=OP.max),
            reads=(f"lg2_{tt}",), writes=(f"m2_{tt}",))
        sch.op("vector", lambda t=tt, s2=sl: nc.vector.tensor_scalar(
            out=lg2[:, s2], in0=lg2[:, s2], scalar1=m2v[:, t:t + 1],
            scalar2=None, op0=OP.is_ge),
            reads=(f"lg2_{tt}", f"m2_{tt}"), writes=(f"lg2_{tt}",))
        sch.op("vector", lambda t=tt, s2=sl: nc.vector.tensor_tensor(
            out=msk[:, s2], in0=msk[:, s2], in1=lg2[:, s2], op=OP.add),
            reads=(f"msk_{tt}", f"lg2_{tt}"), writes=(f"msk_{tt}",))
        sch.op("vector", lambda t=tt, s2=sl: nc.vector.scalar_tensor_tensor(
            out=cmb[:, s2], in0=elg[:, s2], scalar=zr[:, t:t + 1],
            in1=msk[:, s2], op0=OP.mult, op1=OP.mult),
            reads=(f"elg_{tt}", f"zr_{tt}", f"msk_{tt}"), writes=(f"cmb_{tt}",))
    for tt in range(8):
        sch.op("tensor", lambda t=tt: nc.tensor.matmul(
            PS(4 + t // 4, (t % 4) * 128, (t % 4 + 1) * 128, 0, 8),
            cmb[:, t * 8:(t + 1) * 8], I128f, start=True, stop=True),
            reads=(f"cmb_{tt}", "c_I128f"), writes=(f"ps{4 + tt//4}",),
            nodrain="E")
    o_cfm = A.alloc(NQ * 4)
    cfm = A.view(o_cfm, 0, 8, NQ, F32)
    sch.op("vector", lambda: nc.vector.tensor_copy(
        cfm, psum_h[0:8, 4 * 512: 6 * 512]),
        reads=("ps4", "ps5"), writes=("cfm",))
    if "comb" in DEBUG:
        sch.op("sync", lambda: nc.sync.dma_start(out=ddbg["comb"], in_=cfm),
               reads=("cfm",), writes=("dcm",), dma="dbg2")
    o_sel8 = A.alloc(1024 * 4)
    sel8f = A.view(o_sel8, 0, 8, 1024, F32)
    const_load("c_sel8f", sel8f, dconst2)
    o_call = A.alloc(NEXP * NQ * 2)
    call = A.view(o_call, 0, 128, NEXP * NQ, BF16)
    for e in range(NEXP):
        for cb in range(2):
            sch.op("tensor", lambda e2=e, c=cb: nc.tensor.matmul(
                PS(6 + c), sel8f[:, e2 * 128:(e2 + 1) * 128],
                cfm[:, c * 512:(c + 1) * 512], start=True, stop=True),
                reads=("cfm", "c_sel8f"), writes=(f"ps{6+cb}",), nodrain="E")
        sch.op("vector", lambda e2=e: nc.vector.tensor_copy(
            call[:, e2 * NQ:(e2 + 1) * NQ], psum_h[:, 6 * 512: 8 * 512]),
            reads=("ps6", "ps7"), writes=(f"call{e}",))
    sch.barrier()

    sch.op("scalar", lambda: load_act_set(18), writes=("acttab2",))
    for _ in range(4):
        sch.op("scalar", lambda: nc.scalar.activation(
            out=warm, in_=warm, func=AF.Silu),
            reads=("acttab2", "c_fc"), writes=("warm",))
    sch.barrier()

    NCH = 11
    o_h = A.alloc(NCH * NQ * 2)
    hsb = A.view(o_h, 0, 128, NCH * NQ, BF16)
    o_wgc = [A.alloc(T * 2), A.alloc(T * 2)]
    o_wuc = [A.alloc(T * 2), A.alloc(T * 2)]
    o_wdc = [A.alloc(NCH * 128 * 2), A.alloc(NCH * 128 * 2)]
    o_sg = [A.alloc(NQ * 2), A.alloc(NQ * 2)]

    def ffn_col_ap(handle, base_off, FH, c):
        return bass.AP(handle, base_off + c * 128,
                       [[FH, 128], [128 * FH, FT], [1, 128]])

    def ffn_wd_ap(handle, base_off, row0, fo):
        return bass.AP(handle, base_off + row0 * E + fo * 128,
                       [[E, 128], [128 * E, NCH], [1, 128]])

    def emit_ffn(tag, wg_src, wu_src, wd_src, cexp):
        for ci in range(NCH):
            r = ci % 2
            gb = 4 * r
            wgc = A.view(o_wgc[r], 0, 128, T, BF16)
            wuc = A.view(o_wuc[r], 0, 128, T, BF16)
            sgv = A.view(o_sg[r], 0, 128, NQ, BF16)
            sch.op("gpsimd", lambda v=wgc, c=ci, f=wg_src: nc.gpsimd.dma_start(
                out=v.rearrange("p (a b) -> p a b", b=128), in_=f(c)),
                writes=(f"wgc{r}",), dma=f"wgc{r}")
            sch.op("gpsimd", lambda v=wuc, c=ci, f=wu_src: nc.gpsimd.dma_start(
                out=v.rearrange("p (a b) -> p a b", b=128), in_=f(c)),
                writes=(f"wuc{r}",), dma=f"wuc{r}")
            for wtok, wv, b0 in ((f"wgc{r}", wgc, gb), (f"wuc{r}", wuc, gb + 2)):
                for fi in range(FT):
                    for cb in range(2):
                        sch.op("tensor", lambda b=b0 + cb, w=wv, f=fi, c=cb,
                               fr=(fi == 0), la=(fi == FT - 1):
                               nc.tensor.matmul(
                                   PS(b), w[:, f * 128:(f + 1) * 128],
                                   ln2v[:, f * NQ + c * 512: f * NQ + c * 512 + 512],
                                   start=fr, stop=la),
                               reads=(wtok, f"ln2_{fi}"), writes=(f"ps{b0+cb}",),
                               nodrain=('A' if fi != 0 else False))
            sch.op("scalar", lambda b=gb, s2=sgv: nc.scalar.activation(
                out=s2, in_=psum_h[:, b * 512:(b + 2) * 512], func=AF.Silu),
                reads=(f"ps{gb}", f"ps{gb+1}", "acttab2"), writes=(f"sg{r}",))
            sch.op("vector", lambda b=gb + 2, s2=sgv, c=ci:
                   nc.vector.tensor_tensor(
                       out=hsb[:, c * NQ:(c + 1) * NQ], in0=s2,
                       in1=psum_h[:, b * 512:(b + 2) * 512], op=OP.mult),
                   reads=(f"sg{r}", f"ps{gb+2}", f"ps{gb+3}"), writes=(f"h{ci}",))
            if cexp is not None:
                sch.op("vector", lambda c=ci, e=cexp: nc.vector.tensor_tensor(
                    out=hsb[:, c * NQ:(c + 1) * NQ],
                    in0=hsb[:, c * NQ:(c + 1) * NQ],
                    in1=call[:, e * NQ:(e + 1) * NQ], op=OP.mult),
                    reads=(f"h{ci}", f"call{e}"), writes=(f"h{ci}",))
        htoks = tuple(f"h{i}" for i in range(NCH))
        for fo in range(FT):
            r = fo % 2
            db = (2 + 2 * r, 3 + 2 * r)
            wdc = A.view(o_wdc[r], 0, 128, NCH * 128, BF16)
            sch.op("gpsimd", lambda v=wdc, f2=fo, f=wd_src: nc.gpsimd.dma_start(
                out=v.rearrange("p (a b) -> p a b", b=128), in_=f(f2)),
                writes=(f"wdc{r}",), dma=f"wdc{r}")
            for ci in range(NCH):
                for cb in range(2):
                    sch.op("tensor", lambda b=db[cb], w=wdc, c=ci, c2=cb,
                           fr=(ci == 0), la=(ci == NCH - 1):
                           nc.tensor.matmul(
                               PS(b), w[:, c * 128:(c + 1) * 128],
                               hsb[:, c * NQ + c2 * 512: c * NQ + c2 * 512 + 512],
                               start=fr, stop=la),
                           reads=(f"wdc{r}",) + htoks, writes=(f"ps{db[cb]}",),
                           nodrain=('A' if ci != 0 else False))
            sch.op("vector", lambda b=db[0], f=fo: nc.vector.tensor_tensor(
                out=x_own[:, f * NQ:(f + 1) * NQ],
                in0=psum_h[:, b * 512:(b + 2) * 512],
                in1=x_own[:, f * NQ:(f + 1) * NQ], op=OP.add),
                reads=(f"ps{db[0]}", f"ps{db[1]}", "xown"), writes=("xown",))

    emit_ffn("s0", lambda c: ffn_col_ap(hWsg, 0, FH_S, c),
             lambda c: ffn_col_ap(hWsu, 0, FH_S, c),
             lambda fo: ffn_wd_ap(hWsd, 0, 0, fo), None)
    emit_ffn("s1", lambda c: ffn_col_ap(hWsg, 0, FH_S, NCH + c),
             lambda c: ffn_col_ap(hWsu, 0, FH_S, NCH + c),
             lambda fo: ffn_wd_ap(hWsd, 0, NCH * 128, fo), None)
    for e in range(NEXP):
        emit_ffn(f"e{e}",
                 lambda c, e2=e: ffn_col_ap(hWg, e2 * E * FH_E, FH_E, c),
                 lambda c, e2=e: ffn_col_ap(hWu, e2 * E * FH_E, FH_E, c),
                 lambda fo, e2=e: ffn_wd_ap(hWd, e2 * FH_E * E, 0, fo), e)
    sch.barrier()

    sch.op("sync", lambda: nc.sync.dma_start(
        out=bass.AP(hout, 0, [[NQ, 128], [128 * NQ, FT], [1, NQ]]),
        in_=x_own.rearrange("p (a b) -> p a b", b=NQ)),
        reads=("xown",), writes=("outd",), dma="out")
    sch.barrier()
    sch.finalize()
    sch.plan()
    eng_sems = {}
    dma_sems = {}
    cms = []
    for e in ENGINES:
        cm = nc.semaphore(f"sem_{e}")
        eng_sems[e] = cm.__enter__()
        cms.append(cm)
    for k in sch.dma_keys:
        cm = nc.semaphore(f"dma_{k}")
        dma_sems[k] = cm.__enter__()
        cms.append(cm)
    my_sems = [s.num for s in eng_sems.values()] + [s.num for s in dma_sems.values()]
    lo, hi = min(my_sems), max(my_sems)
    assert set(my_sems) == set(range(lo, hi + 1)), "non-contiguous sems"

    nc.all_engine_barrier()
    nc.gpsimd.dma_reset(range(lo, hi + 1))
    nc.gpsimd.sem_clear(range(lo, hi + 1))
    nc.all_engine_barrier()

    eng_objs = dict(sync=nc.sync, gpsimd=nc.gpsimd, scalar=nc.scalar,
                    vector=nc.vector, tensor=nc.tensor)
    for e in ENGINES:
        sch.emit(e, eng_objs[e], eng_sems, dma_sems)

    nc.all_engine_barrier()
    nc.gpsimd.dma_reset(range(lo, hi + 1))
    nc.gpsimd.sem_clear(range(lo, hi + 1))
    nc.all_engine_barrier()
    return nc


def _host_tables(own_blocks):
    pos_own = (np.asarray(own_blocks)[:, None] * 128 + np.arange(128)[None, :]).ravel()
    j = np.arange(D // 2, dtype=np.float64)
    theta = 10000.0 ** (-2.0 * j / D)

    def tables(pos, scale):
        ang = (pos[None, :].astype(np.float64) + 1.0) * theta[:, None]
        C = np.concatenate([ang, ang], axis=0)
        cos = (np.cos(C) * scale).astype(np.float32)
        sin = (np.sin(C) * scale).astype(np.float32)
        return np.concatenate([cos, cos], 0), np.concatenate([sin, sin], 0)

    cqc, cqs = tables(pos_own, 0.125)
    ckc, cks = tables(np.arange(T), 1.0)
    return cqc, cqs, ckc, cks


def _host_masks(own_blocks):
    NEG = np.float32(-30000.0)
    q = np.arange(128)
    mS = np.zeros((128, NBLK * 256), np.float32)
    mT = np.zeros((128, NBLK * 1024), np.float32)
    for jslot in range(NBLK):
        g = own_blocks[jslot]
        nk = NKEYS[jslot]
        k = nk - 256 + np.arange(256)
        valid = k[None, :] <= (g * 128 + q)[:, None]
        mS[:, jslot * 256:(jslot + 1) * 256] = np.where(valid, 0.0, NEG)
        for kt2 in range(2):
            kk = nk - 256 + kt2 * 128 + np.arange(128)
            validT = kk[:, None] <= (g * 128 + q)[None, :]
            tile = np.where(validT, 0.0, NEG)
            for g4 in range(4):
                mT[:, jslot * 1024 + kt2 * 512 + g4 * 128:
                     jslot * 1024 + kt2 * 512 + (g4 + 1) * 128] = tile
    return mS.astype(ml_dtypes.bfloat16), mT.astype(ml_dtypes.bfloat16)


def _host_consts():
    ib = np.eye(128, dtype=ml_dtypes.bfloat16)
    raw = np.zeros((128, 4096), np.uint8)
    raw[:, 0:128 * 2] = ib.view(np.uint8).reshape(128, 256)
    i32 = np.eye(128, dtype=np.float32)
    raw[:, 256:256 + 512] = i32.view(np.uint8).reshape(128, 512)
    ones_b = np.ones((1, 128), dtype=ml_dtypes.bfloat16)
    raw[0:1, 768:768 + 256] = ones_b.view(np.uint8).reshape(1, 256)
    ones_f = np.ones((1, 128), dtype=np.float32)
    raw[0:1, 1024:1024 + 512] = ones_f.view(np.uint8).reshape(1, 512)
    onecol = np.ones((128, 1), dtype=ml_dtypes.bfloat16)
    raw[:, 1536:1538] = onecol.view(np.uint8).reshape(128, 2)
    fconst = np.zeros((128, 8), np.float32)
    fconst[:, 1] = EPS
    raw[:, 1568:1568 + 32] = fconst.view(np.uint8).reshape(128, 32)
    sel4 = np.zeros((4, 512), np.float32)
    for g in range(4):
        sel4[g, g * 128:(g + 1) * 128] = 1.0
    raw[0:4, 1600:1600 + 2048] = sel4.view(np.uint8).reshape(4, 2048)
    return raw.view(np.float32)


def _host_consts2():
    sel8 = np.zeros((8, 1024), np.float32)
    for e in range(8):
        sel8[e, e * 128:(e + 1) * 128] = 1.0
    return sel8


_PROG_CACHE = {}


def kernel(L0_attn_x, L0_attn_Wq, L0_attn_bq, L0_attn_Wk, L0_attn_bk,
           L0_attn_Wv, L0_attn_bv, L0_attn_Wo, L0_attn_bo,
           L0_ffn_W_g, L0_ffn_Wg, L0_ffn_Wu, L0_ffn_Wd,
           L0_ffn_Ws_gate, L0_ffn_Ws_up, L0_ffn_Ws_down):
    x = np.asarray(L0_attn_x, np.float32)
    consts = _host_consts()
    weights = dict(
        Wq=np.asarray(L0_attn_Wq, np.float32), Wk=np.asarray(L0_attn_Wk, np.float32),
        Wv=np.asarray(L0_attn_Wv, np.float32), Wo=np.asarray(L0_attn_Wo, np.float32),
        W_g=np.asarray(L0_ffn_W_g, np.float32),
        Wg=np.asarray(L0_ffn_Wg, np.float32), Wu=np.asarray(L0_ffn_Wu, np.float32),
        Wd=np.asarray(L0_ffn_Wd, np.float32),
        Ws_gate=np.asarray(L0_ffn_Ws_gate, np.float32),
        Ws_up=np.asarray(L0_ffn_Ws_up, np.float32),
        Ws_down=np.asarray(L0_ffn_Ws_down, np.float32),
    )
    bq = np.asarray(L0_attn_bq, np.float32).reshape(FT, 128).T.copy()
    bk = np.asarray(L0_attn_bk, np.float32).reshape(4, 128).T.copy()
    bv = np.asarray(L0_attn_bv, np.float32).reshape(4, 128).T.copy()
    bo = np.asarray(L0_attn_bo, np.float32).reshape(FT, 128).T.copy()

    consts2 = _host_consts2()
    in_maps = []
    metas = []
    for c in range(8):
        s = c // 2
        own = OWN_A if c % 2 == 0 else OWN_B
        x_fm = np.ascontiguousarray(x[s].T)
        tok = (np.asarray(own)[:, None] * 128 + np.arange(128)[None, :]).ravel()
        x_own_fm = np.ascontiguousarray(x_fm[:, tok])
        cqc, cqs, ckc, cks = _host_tables(own)
        mS, mT = _host_masks(own)
        m = dict(weights)
        m.update(x_fm=x_fm, x_own_fm=x_own_fm, bq_t=bq, bk_t=bk, bv_t=bv,
                 bo_t=bo, cq_cos=cqc, cq_sin=cqs, ck_cos=ckc, ck_sin=cks,
                 maskS=mS, maskT=mT, consts=consts, consts2=consts2)
        in_maps.append(m)
        metas.append((s, own))

    key = "prog"
    if key not in _PROG_CACHE:
        _PROG_CACHE[key] = build_program()
    nc = _PROG_CACHE[key]

    res = run_bass_kernel_spmd(nc, in_maps, core_ids=list(range(8)))

    out = np.empty((B, T, E), np.float32)
    for c in range(8):
        s, own = metas[c]
        tok = (np.asarray(own)[:, None] * 128 + np.arange(128)[None, :]).ravel()
        out[s, tok, :] = res.results[c]["out_fm"].T
    if DEBUG:
        kernel.debug = [{k: v for k, v in r.items() if k.startswith("dbg_")}
                        for r in res.results]
    return out


# version 11 (final)
# speedup vs baseline: 1.0977x; 1.0010x over previous
import numpy as np
import ml_dtypes
import concourse.bass as bass
import concourse.mybir as mybir
from concourse.bass_utils import run_bass_kernel_spmd

F32 = mybir.dt.float32
BF16 = mybir.dt.bfloat16
AF = mybir.ActivationFunctionType
OP = mybir.AluOpType
AX = mybir.AxisListType

B, T, E = 4, 2048, 2048
HKV, NG, D = 8, 4, 64
KVD = 512
FH_E, NEXP = 1408, 8
FH_S = 2816
NQ = 1024
NBLK = 8
FT = 16
EPS = 1e-5

OWN_A = [0, 4, 8, 12, 3, 7, 11, 15]
OWN_B = [1, 5, 9, 13, 2, 6, 10, 14]
NKEYS = [256, 768, 1280, 1792, 512, 1024, 1536, 2048]

DEBUG = ()
TENSOR_DRAINS = False
NODRAIN_CATS = set("ABCDEF")
PRUNE_WAITS = True


ENGINES = ("sync", "gpsimd", "scalar", "vector", "tensor")


class Sched:
    def __init__(self):
        self.ops = []

    def op(self, engine, fn, reads=(), writes=(), dma=None, nodrain=False):
        assert engine in ENGINES
        self.ops.append(dict(engine=engine, fn=fn, reads=tuple(reads),
                             writes=tuple(writes), dma=dma, barrier=False,
                             nodrain=nodrain))

    def barrier(self):
        self.ops.append(dict(engine=None, fn=None, reads=(), writes=(),
                             dma=None, barrier=True))

    def finalize(self):
        last_writer = {}
        readers = {}
        last_barrier = -1
        n = len(self.ops)
        deps = [set() for _ in range(n)]
        barrier_ids = []
        since_barrier = []
        for i, o in enumerate(self.ops):
            if o["barrier"]:
                deps[i] = set(since_barrier)
                barrier_ids.append(i)
                since_barrier = []
                last_barrier = i
                continue
            dd = set()
            for r in o["reads"]:
                if r in last_writer:
                    dd.add(last_writer[r])
            for w in o["writes"]:
                if w in last_writer:
                    dd.add(last_writer[w])
                for rd in readers.get(w, ()):
                    dd.add(rd)
            if last_barrier >= 0:
                dd.add(last_barrier)
            deps[i] = {d for d in dd if d >= last_barrier}
            for w in o["writes"]:
                last_writer[w] = i
                readers[w] = []
            for r in o["reads"]:
                readers.setdefault(r, []).append(i)
            since_barrier.append(i)

        emit_ops = []
        for i, o in enumerate(self.ops):
            if o["barrier"]:
                emit_ops.append(dict(kind="barrier", idx=i))
            else:
                emit_ops.append(dict(kind="op", idx=i, **o))

        ord_of = {}
        eng_count = {e: 0 for e in ENGINES}
        for eo in emit_ops:
            i = eo["idx"]
            if eo["kind"] == "barrier":
                for e in ENGINES:
                    eng_count[e] += 1
                    ord_of[(i, e)] = eng_count[e]
            else:
                e = eo["engine"]
                eng_count[e] += 1
                ord_of[(i, e)] = eng_count[e]

        self.inc_needed = set()
        for i, o in enumerate(self.ops):
            if not o["barrier"] and o["dma"] is not None:
                self.inc_needed.add(i)
        for i, o in enumerate(self.ops):
            my_engines = set(ENGINES) if o["barrier"] else {o["engine"]}
            for d in deps[i]:
                od = self.ops[d]
                if od["barrier"]:
                    continue
                if od["dma"] is not None:
                    self.inc_needed.add(d)
                elif od["engine"] not in my_engines or o["barrier"]:
                    self.inc_needed.add(d)
        for b in barrier_ids:
            self.inc_needed.add(b)
        self.dma_issuer = {}
        for o in self.ops:
            if not o["barrier"] and o["dma"] is not None:
                prev = self.dma_issuer.setdefault(o["dma"], o["engine"])
                assert prev == o["engine"], f"dma key {o['dma']} on two engines"

        inc_ord = {}
        eng_inc = {e: 0 for e in ENGINES}
        dma_inc = {}
        self.dma_keys = []
        self.barrier_dma_snap = {}
        for eo in emit_ops:
            i = eo["idx"]
            if eo["kind"] == "barrier":
                self.barrier_dma_snap[i] = dict(dma_inc)
                for e in ENGINES:
                    eng_inc[e] += 1
                inc_ord[i] = ("barrier", None, {e: eng_inc[e] for e in ENGINES})
            else:
                o = self.ops[i]
                if i in self.inc_needed:
                    if o["dma"] is not None:
                        k = o["dma"]
                        if k not in dma_inc:
                            dma_inc[k] = 0
                            self.dma_keys.append(k)
                        dma_inc[k] += 16
                        inc_ord[i] = ("dma", k, dma_inc[k])
                    else:
                        e = o["engine"]
                        eng_inc[e] += 1
                        inc_ord[i] = ("eng", e, eng_inc[e])
        self.deps = deps
        self.inc_ord = inc_ord
        self.eng_inc_total = eng_inc
        self.barrier_set = set(barrier_ids)

    def plan(self):
        n = len(self.ops)
        ordv = [None] * n
        cnt = {e: 0 for e in ENGINES}
        for i, o in enumerate(self.ops):
            if o["barrier"]:
                for e in ENGINES:
                    cnt[e] += 1
                ordv[i] = dict(cnt)
            else:
                e = o["engine"]
                cnt[e] += 1
                ordv[i] = cnt[e]
        clock = [None] * n
        self.plan_waits = [[] for _ in range(n)]
        self.plan_drain = [False] * n
        last_op_on = {e: None for e in ENGINES}
        last_dma_on_key = {}
        drained_to = {e: 0 for e in ENGINES}
        ZERO = {e: 0 for e in ENGINES}

        def merge(a, b):
            return {e: max(a[e], b[e]) for e in ENGINES}

        for i, o in enumerate(self.ops):
            if o["barrier"]:
                c = dict(ordv[i])
                clock[i] = c
                for e in ENGINES:
                    last_op_on[e] = i
                    drained_to[e] = ordv[i][e]
                continue
            eng = o["engine"]
            know = dict(ZERO)
            waits = []
            dset = sorted(self.deps[i])
            for d in dset:
                if self.ops[d]["barrier"]:
                    know = merge(know, clock[d])
            for d in dset:
                od = self.ops[d]
                if od["barrier"]:
                    continue
                kind, key, cval = self.inc_ord[d]
                if kind == "dma":
                    if know.get(("dma", key), 0) if False else False:
                        pass
                    waits.append(("dma", key, cval, d))
                    know = merge(know, clock[d])
                elif key == eng:
                    continue
                else:
                    if PRUNE_WAITS and know[key] >= ordv[d]:
                        continue
                    waits.append(("eng", key, cval, d))
                    know = merge(know, clock[d])
            need_drain = False
            nd = o.get("nodrain", False)
            if isinstance(nd, str):
                nd = nd in NODRAIN_CATS
            if not nd:
                for d in dset:
                    od = self.ops[d]
                    if od["barrier"] or od["engine"] != eng or od["dma"] is not None:
                        continue
                    if know[eng] < ordv[d] and drained_to[eng] < ordv[d]:
                        need_drain = True
            if need_drain:
                p = last_op_on[eng]
                if p is not None:
                    know = merge(know, clock[p])
                know[eng] = max(know[eng], ordv[i] - 1)
                drained_to[eng] = ordv[i] - 1
            self.plan_waits[i] = waits
            self.plan_drain[i] = need_drain
            c = dict(know)
            if o["dma"] is not None:
                p = last_dma_on_key.get(o["dma"])
                if p is not None:
                    c = merge(c, clock[p])
                last_dma_on_key[o["dma"]] = i
            else:
                p = last_op_on[eng]
                if p is not None:
                    c = merge(c, clock[p])
                c[eng] = max(c[eng], ordv[i])
                last_op_on[eng] = i
            clock[i] = c

    def emit(self, engine, eng_obj, eng_sems, dma_sems):
        waited = {}

        def wait(kind, key, val):
            cur = waited.get((kind, key), 0)
            if val <= cur:
                return
            waited[(kind, key)] = val
            sem = eng_sems[key] if kind == "eng" else dma_sems[key]
            eng_obj.wait_ge(sem, val)

        for i, o in enumerate(self.ops):
            if o["barrier"]:
                kind, _, totals = self.inc_ord[i]
                for k, tot in self.barrier_dma_snap[i].items():
                    if self.dma_issuer.get(k) == engine and tot > 0:
                        wait("dma", k, tot)
                eng_obj.drain().then_inc(eng_sems[engine], 1)
                for e in ENGINES:
                    if e == engine:
                        continue
                    wait("eng", e, totals[e])
                continue
            if o["engine"] != engine:
                continue
            for kind, key, cval, d in self.plan_waits[i]:
                wait(kind, key, cval)
            if self.plan_drain[i]:
                eng_obj.drain()
            instr = o["fn"]()
            info = self.inc_ord.get(i)
            if info is not None:
                kind, key, cnt = info
                if kind == "dma":
                    instr.then_inc(dma_sems[key], 16)
                else:
                    instr.then_inc(eng_sems[key], 1)


class Arena:

    def __init__(self, handle, nbytes):
        self.h = handle
        self.nbytes = nbytes
        self.free_list = [(0, nbytes)]
        self.marks = []
        self.peak = 0

    def alloc(self, nbytes, align=32):
        nbytes = (int(nbytes) + align - 1) // align * align
        if not hasattr(self, "live"):
            self.live = {}
        for i, (off, size) in enumerate(self.free_list):
            aoff = (off + align - 1) // align * align
            pad = aoff - off
            if size - pad >= nbytes:
                rest = size - pad - nbytes
                new = []
                if pad:
                    new.append((off, pad))
                if rest:
                    new.append((aoff + nbytes, rest))
                self.free_list[i:i + 1] = new
                used = self.nbytes - sum(s for _, s in self.free_list)
                self.peak = max(self.peak, used)
                self.live[aoff] = nbytes
                return aoff
        raise AssertionError(
            f"SBUF arena overflow: want {nbytes}, free={self.free_list}")

    def free(self, off, nbytes, align=32):
        nbytes = (int(nbytes) + align - 1) // align * align
        assert self.live.get(off) == nbytes, (
            f"bad free: off={off} size={nbytes} live={self.live.get(off)}")
        del self.live[off]
        self.free_list.append((off, nbytes))
        self.free_list.sort()
        merged = []
        for o, sz in self.free_list:
            if merged and merged[-1][0] + merged[-1][1] == o:
                merged[-1] = (merged[-1][0], merged[-1][1] + sz)
            else:
                merged.append((o, sz))
        self.free_list = [tuple(x) for x in merged]

    def mark(self):
        pass

    def release(self):
        self.cur = self.marks.pop()

    def view(self, off, p0, p1, cols, dtype):
        esz = mybir.dt.size(dtype)
        return self.h[p0:p1, off:off + cols * esz].bitcast(dtype)


def vslice(arena, off, dtype, p0, p1, c0, c1):
    esz = mybir.dt.size(dtype)
    return arena.h[p0:p1, off + c0 * esz: off + c1 * esz].bitcast(dtype)


def build_program():
    nc = bass.Bass("TRN2", target_bir_lowering=False)

    hx = nc.dram_tensor("x_fm", [E, T], F32, kind="ExternalInput")
    dx = hx.ap()
    hxo = nc.dram_tensor("x_own_fm", [E, NQ], F32, kind="ExternalInput")
    dxo = hxo.ap()
    dWq = nc.dram_tensor("Wq", [E, E], F32, kind="ExternalInput").ap()
    dWk = nc.dram_tensor("Wk", [KVD, E], F32, kind="ExternalInput").ap()
    dWv = nc.dram_tensor("Wv", [KVD, E], F32, kind="ExternalInput").ap()
    dWo = nc.dram_tensor("Wo", [E, E], F32, kind="ExternalInput").ap()
    hWg_r = nc.dram_tensor("W_g", [E, NEXP], F32, kind="ExternalInput")
    hWg = nc.dram_tensor("Wg", [NEXP, E, FH_E], F32, kind="ExternalInput")
    hWu = nc.dram_tensor("Wu", [NEXP, E, FH_E], F32, kind="ExternalInput")
    hWd = nc.dram_tensor("Wd", [NEXP, FH_E, E], F32, kind="ExternalInput")
    hWsg = nc.dram_tensor("Ws_gate", [E, FH_S], F32, kind="ExternalInput")
    hWsu = nc.dram_tensor("Ws_up", [E, FH_S], F32, kind="ExternalInput")
    hWsd = nc.dram_tensor("Ws_down", [FH_S, E], F32, kind="ExternalInput")
    dbq = nc.dram_tensor("bq_t", [128, FT], F32, kind="ExternalInput").ap()
    dbk = nc.dram_tensor("bk_t", [128, 4], F32, kind="ExternalInput").ap()
    dbv = nc.dram_tensor("bv_t", [128, 4], F32, kind="ExternalInput").ap()
    dbo = nc.dram_tensor("bo_t", [128, FT], F32, kind="ExternalInput").ap()
    dcqc = nc.dram_tensor("cq_cos", [128, NQ], F32, kind="ExternalInput").ap()
    dcqs = nc.dram_tensor("cq_sin", [128, NQ], F32, kind="ExternalInput").ap()
    dckc = nc.dram_tensor("ck_cos", [128, T], F32, kind="ExternalInput").ap()
    dcks = nc.dram_tensor("ck_sin", [128, T], F32, kind="ExternalInput").ap()
    dmaskS = nc.dram_tensor("maskS", [128, NBLK * 256], BF16, kind="ExternalInput").ap()
    dmaskT = nc.dram_tensor("maskT", [128, NBLK * 1024], BF16, kind="ExternalInput").ap()
    dconst = nc.dram_tensor("consts", [128, 1024], F32, kind="ExternalInput").ap()
    dconst2 = nc.dram_tensor("consts2", [8, 1024], F32, kind="ExternalInput").ap()

    hout = nc.dram_tensor("out_fm", [E, NQ], F32, kind="ExternalOutput")
    dout = hout.ap()
    ddbg = {}
    if "ln1" in DEBUG:
        ddbg["ln1"] = nc.dram_tensor("dbg_ln1", [E, T], F32, kind="ExternalOutput").ap()
    if "kpack" in DEBUG:
        ddbg["kpack"] = nc.dram_tensor("dbg_kpack", [128, 4 * T], BF16, kind="ExternalOutput").ap()
        ddbg["vtm"] = nc.dram_tensor("dbg_vtm", [128, 16 * KVD], BF16, kind="ExternalOutput").ap()
    if "qpack" in DEBUG or "qpk" in DEBUG:
        ddbg["qpack"] = nc.dram_tensor("dbg_qpack", [128, 4 * 4096], BF16, kind="ExternalOutput").ap()
    if "qpk" in DEBUG:
        ddbg["qpack2"] = nc.dram_tensor("dbg_qpack2", [128, 4 * 4096], BF16, kind="ExternalOutput").ap()
    if "lnown" in DEBUG:
        ddbg["lnown"] = nc.dram_tensor("dbg_lnown", [128, FT * NQ], BF16, kind="ExternalOutput").ap()
    if "y" in DEBUG:
        ddbg["y"] = nc.dram_tensor("dbg_y", [128, FT * NQ], BF16, kind="ExternalOutput").ap()
    if "xattn" in DEBUG:
        ddbg["xattn"] = nc.dram_tensor("dbg_xattn", [128, FT * NQ], F32, kind="ExternalOutput").ap()
    if "ln2" in DEBUG:
        ddbg["ln2"] = nc.dram_tensor("dbg_ln2", [128, FT * NQ], BF16, kind="ExternalOutput").ap()
    if "comb" in DEBUG:
        ddbg["comb"] = nc.dram_tensor("dbg_comb", [NEXP, NQ], F32, kind="ExternalOutput").ap()

    sch = Sched()

    ARENA_BYTES = 204 * 1024
    arena_cm = nc.sbuf_tensor("arena", [128, ARENA_BYTES], mybir.dt.uint8)
    arena_h = arena_cm.__enter__()
    A = Arena(arena_h, ARENA_BYTES)

    psum_cm = nc.psum_tensor("psum", [128, 4096], F32)
    psum_h = psum_cm.__enter__()

    def PS(bank, c0=0, c1=512, p0=0, p1=128):
        return psum_h[p0:p1, bank * 512 + c0: bank * 512 + c1]

    o_I128b = A.alloc(128 * 2)
    o_I128f = A.alloc(128 * 4)
    o_ones1b = A.alloc(128 * 2)
    o_ones1f = A.alloc(128 * 4)
    o_onescol = A.alloc(32)
    o_bq = A.alloc(FT * 4)
    o_bk = A.alloc(4 * 4)
    o_bv = A.alloc(4 * 4)
    o_bo = A.alloc(FT * 4)

    I128b = A.view(o_I128b, 0, 128, 128, BF16)
    I128f = A.view(o_I128f, 0, 128, 128, F32)
    ones1b = A.view(o_ones1b, 0, 1, 128, BF16)
    ones1f = A.view(o_ones1f, 0, 1, 128, F32)
    onescol16 = A.view(o_onescol, 0, 128, 16, BF16)
    onescol = onescol16[:, 0:1]
    bq_t = A.view(o_bq, 0, 128, FT, F32)
    bk_t = A.view(o_bk, 0, 128, 4, F32)
    bv_t = A.view(o_bv, 0, 128, 4, F32)
    bo_t = A.view(o_bo, 0, 128, FT, F32)

    o_fc = A.alloc(8 * 4)
    fc = A.view(o_fc, 0, 128, 8, F32)
    const_load_later = []
    nc.const_aps.register(F32, 0.0, fc[:, 0:1])
    nc.const_aps.register(F32, EPS, fc[:, 1:2])

    def const_load(name, view, src):
        sch.op("sync", lambda v=view, s=src: nc.sync.dma_start(out=v, in_=s),
               writes=(name,), dma=name)

    const_load("c_I128b", I128b, dconst[:, 0:64].bitcast(BF16))
    const_load("c_I128f", I128f, dconst[:, 64:192])
    const_load("c_ones1b", ones1b, dconst[0:1, 192:256].bitcast(BF16))
    const_load("c_ones1f", ones1f, dconst[0:1, 256:384])
    const_load("c_onescol", onescol16, dconst[:, 384:392].bitcast(BF16))
    const_load("c_fc", fc, dconst[:, 392:400])
    const_load("c_bq", bq_t, dbq)
    const_load("c_bk", bk_t, dbk)
    const_load("c_bv", bv_t, dbv)
    const_load("c_bo", bo_t, dbo)

    o_warm = A.alloc(2048 * 2)
    warm = A.view(o_warm, 0, 128, 2048, BF16)

    def load_act_set(idx):
        return nc.scalar.add_instruction(
            mybir.InstLoadActFuncSet(
                name=nc.get_next_instruction_name(),
                act_func_set_id=idx, ins=[], outs=[]))

    sch.op("gpsimd", lambda: nc.gpsimd.memset(warm, 0.0), writes=("warm",))
    sch.op("scalar", lambda: load_act_set(6), writes=("acttab",))
    for _ in range(4):
        sch.op("scalar", lambda: nc.scalar.activation(
            out=warm, in_=warm, func=AF.Exp),
            reads=("acttab", "c_fc"), writes=("warm",))
    sch.barrier()

    o_BIG = A.alloc(FT * T * 2)
    ln1 = A.view(o_BIG, 0, 128, FT * T, BF16)

    def emit_ln(tag, src_dram, TL, out_view, out_tok_prefix):
        nch = TL // 512
        o_xt = [A.alloc(TL * 4), A.alloc(TL * 4)]
        o_xbf = [A.alloc(TL * 2), A.alloc(TL * 2)]
        o_x2bf = [A.alloc(TL * 2), A.alloc(TL * 2)]
        o_nm = A.alloc(TL * 4); o_rs = A.alloc(TL * 4)
        o_sa = A.alloc(TL * 4); o_sb2 = A.alloc(TL * 4)
        nm = A.view(o_nm, 0, 1, TL, F32); rs = A.view(o_rs, 0, 1, TL, F32)
        sa = A.view(o_sa, 0, 1, TL, F32); sb2 = A.view(o_sb2, 0, 1, TL, F32)
        from_sbuf = (src_dram[0] == "sbuf")
        for ft in range(FT):
            r = ft % 2
            xbf = A.view(o_xbf[r], 0, 128, TL, BF16)
            x2bf = A.view(o_x2bf[r], 0, 128, TL, BF16)
            if from_sbuf:
                xt = src_dram[1][:, ft * TL:(ft + 1) * TL]
                xtok = src_dram[2]
            else:
                xt = A.view(o_xt[r], 0, 128, TL, F32)
                xtok = f"{tag}xt{r}"
                sch.op("sync", lambda v=xt, f=ft, sd=src_dram[1]:
                       nc.sync.dma_start(out=v, in_=sd[f * 128:(f + 1) * 128, :]),
                       writes=(xtok,), dma=xtok)
            sch.op("scalar", lambda o=xbf, i=xt: nc.scalar.copy(out=o, in_=i),
                   reads=(xtok,), writes=(f"{tag}xbf{r}",))
            sch.op("scalar", lambda o=x2bf, i=xt:
                   nc.scalar.activation(out=o, in_=i, func=AF.Square),
                   reads=(xtok, "acttab"), writes=(f"{tag}x2bf{r}",))
            for cb in range(nch):
                for sti, (srcv, tok) in enumerate(
                        ((xbf, f"{tag}xbf{r}"), (x2bf, f"{tag}x2bf{r}"))):
                    bank = sti * nch + cb
                    sch.op("tensor",
                           lambda b=bank, sv=srcv, c=cb, f=ft:
                           nc.tensor.matmul(PS(b, 0, 512, 0, 1),
                                            onescol, sv[:, c * 512:(c + 1) * 512],
                                            start=(f == 0), stop=(f == FT - 1)),
                           reads=(tok, "c_onescol"), writes=(f"ps{bank}",),
                           nodrain=('A' if ft != 0 else False))
        for cb in range(nch):
            sch.op("vector", lambda c=cb, v=nm: nc.vector.tensor_scalar(
                out=v[:, c * 512:(c + 1) * 512], in0=PS(c, 0, 512, 0, 1),
                scalar1=-1.0 / E, scalar2=None, op0=OP.mult),
                reads=(f"ps{cb}",), writes=(f"{tag}nm",))
            sch.op("vector", lambda c=cb, v=sa, n=nch: nc.vector.tensor_scalar(
                out=v[:, c * 512:(c + 1) * 512], in0=PS(n + c, 0, 512, 0, 1),
                scalar1=1.0 / E, scalar2=None, op0=OP.mult),
                reads=(f"ps{nch+cb}",), writes=(f"{tag}sa",))
        sch.op("vector", lambda a=sb2, b=nm: nc.vector.tensor_tensor(
            out=a, in0=b, in1=b, op=OP.mult),
            reads=(f"{tag}nm",), writes=(f"{tag}sb",))
        sch.op("vector", lambda a=sa, b=sb2: nc.vector.tensor_tensor(
            out=a, in0=a, in1=b, op=OP.subtract),
            reads=(f"{tag}sa", f"{tag}sb"), writes=(f"{tag}sa",))
        sch.op("scalar", lambda a=sb2, b=sa: nc.scalar.activation(
            out=a, in_=b, func=AF.Ln, bias=EPS),
            reads=(f"{tag}sa", "acttab"), writes=(f"{tag}sb",))
        sch.op("scalar", lambda a=rs, b=sb2: nc.scalar.activation(
            out=a, in_=b, func=AF.Exp, scale=-0.5),
            reads=(f"{tag}sb",), writes=(f"{tag}rs",))
        nhalf = TL // 1024
        for half in range(nhalf):
            for cb in range(2):
                c0 = half * 1024 + cb * 512
                bnm = half * 2 + cb
                brs = 2 * nhalf + half * 2 + cb
                sch.op("tensor", lambda b=bnm, c=c0, v=nm: nc.tensor.matmul(
                    PS(b), ones1f, v[:, c:c + 512], start=True, stop=True),
                    reads=(f"{tag}nm", "c_ones1f"), writes=(f"ps{bnm}",),
                    nodrain="F")
                sch.op("tensor", lambda b=brs, c=c0, v=rs: nc.tensor.matmul(
                    PS(b), ones1f, v[:, c:c + 512], start=True, stop=True),
                    reads=(f"{tag}rs", "c_ones1f"), writes=(f"ps{brs}",),
                    nodrain="F")
        o_tmpf = [A.alloc(1024 * 4), A.alloc(1024 * 4)]
        for ft in range(FT):
            r = ft % 2
            if from_sbuf:
                xt = src_dram[1][:, ft * TL:(ft + 1) * TL]
                xtok = src_dram[2]
            else:
                xt = A.view(o_xt[r], 0, 128, TL, F32)
                xtok = f"{tag}xt{r}"
                sch.op("sync", lambda v=xt, f=ft, sd=src_dram[1]:
                       nc.sync.dma_start(out=v, in_=sd[f * 128:(f + 1) * 128, :]),
                       writes=(xtok,), dma=xtok)
            for half in range(nhalf):
                tmpf = A.view(o_tmpf[r], 0, 128, 1024, F32)
                hb = half * 2
                rb = 2 * nhalf + hb
                sch.op("vector", lambda v=xt, h=half, t=tmpf, b=hb:
                       nc.vector.tensor_tensor(
                           out=t, in0=v[:, h * 1024:(h + 1) * 1024],
                           in1=psum_h[:, b * 512:(b + 2) * 512], op=OP.add),
                       reads=(xtok, f"ps{hb}", f"ps{hb+1}"),
                       writes=(f"{tag}tmp{r}",))
                sch.op("vector", lambda t=tmpf, f=ft, h=half, b=rb, ov=out_view:
                       nc.vector.tensor_tensor(
                           out=ov[:, f * TL + h * 1024: f * TL + (h + 1) * 1024],
                           in0=t, in1=psum_h[:, b * 512:(b + 2) * 512], op=OP.mult),
                       reads=(f"{tag}tmp{r}", f"ps{rb}", f"ps{rb+1}"),
                       writes=(f"{out_tok_prefix}{ft}",))
        sch.barrier()
        for off in o_xt:
            A.free(off, TL * 4)
        for off in o_xbf + o_x2bf:
            A.free(off, TL * 2)
        A.free(o_tmpf[0], 1024 * 4); A.free(o_tmpf[1], 1024 * 4)
        A.free(o_nm, TL * 4); A.free(o_rs, TL * 4)
        A.free(o_sa, TL * 4); A.free(o_sb2, TL * 4)

    emit_ln("l1", ("dram", dx), T, ln1, "ln1_")

    if "ln1" in DEBUG:
        o_dbg = A.alloc(T * 4)
        dbgt = A.view(o_dbg, 0, 128, T, F32)
        for ft in range(FT):
            sch.op("vector", lambda f=ft: nc.vector.tensor_copy(
                dbgt, ln1[:, f * T:(f + 1) * T]),
                reads=(f"ln1_{ft}",), writes=("dbgt",))
            sch.op("sync", lambda f=ft: nc.sync.dma_start(
                out=ddbg["ln1"][f * 128:(f + 1) * 128, :], in_=dbgt),
                reads=("dbgt",), writes=(f"dbgo{ft}",), dma="dbg")
            sch.barrier()
        A.free(o_dbg, T * 4)

    o_wb = [A.alloc(T * 2), A.alloc(T * 2)]
    o_strip = [A.alloc(T * 2), A.alloc(T * 2)]

    def emit_strip(r, src_rows_ap, tbank):
        wb = A.view(o_wb[r], 0, 128, T, BF16)
        strip = A.view(o_strip[r], 0, 128, T, BF16)
        sch.op("gpsimd", lambda v=wb, sr=src_rows_ap:
               nc.gpsimd.dma_start(out=v, in_=sr),
               writes=(f"wb{r}",), dma=f"wb{r}")
        for grp in range(4):
            for ii in range(4):
                fi = grp * 4 + ii
                sch.op("tensor", lambda b=tbank, w=wb, f=fi, i2=ii:
                       nc.tensor.matmul(PS(b, i2 * 128, (i2 + 1) * 128),
                                        w[:, f * 128:(f + 1) * 128], I128b,
                                        start=True, stop=True),
                       reads=(f"wb{r}", "c_I128b"), writes=(f"ps{tbank}",),
                       nodrain="B")
            sch.op("vector", lambda b=tbank, st=strip, g2=grp:
                   nc.vector.tensor_copy(st[:, g2 * 512:(g2 + 1) * 512], PS(b)),
                   reads=(f"ps{tbank}",), writes=(f"strip{r}",))
        return strip

    o_lnown = A.alloc(FT * NQ * 2)
    lnown = A.view(o_lnown, 0, 128, FT * NQ, BF16)
    emit_ln("lo", ("dram", dxo), NQ, lnown, "lno_")

    if "lnown" in DEBUG:
        sch.op("sync", lambda: nc.sync.dma_start(out=ddbg["lnown"], in_=lnown),
               reads=tuple(f"lno_{i}" for i in range(FT)), writes=("dlo",), dma="dbg2")
        sch.barrier()

    o_cqc = A.alloc(NQ * 4); o_cqs = A.alloc(NQ * 4)
    cqc = A.view(o_cqc, 0, 128, NQ, F32)
    cqs = A.view(o_cqs, 0, 128, NQ, F32)
    const_load("c_cqc", cqc, dcqc)
    const_load("c_cqs", cqs, dcqs)
    o_Qg = A.alloc(16 * NQ * 2)
    Qg = A.view(o_Qg, 0, 128, 16 * NQ, BF16)
    o_qb = A.alloc(NQ * 4); o_x2q = A.alloc(NQ * 4)
    qb = A.view(o_qb, 0, 128, NQ, F32)
    x2q = A.view(o_x2q, 0, 128, NQ, F32)
    for fo in range(FT):
        r = fo % 2
        strip = emit_strip(r, dWq[fo * 128:(fo + 1) * 128, :], r)
        qpb = (2 + 2 * r, 3 + 2 * r)
        ptoks = (f"ps{qpb[0]}", f"ps{qpb[1]}")
        for fi in range(FT):
            for cb in range(2):
                sch.op("tensor", lambda b=qpb[cb], st=strip, f=fi, c=cb,
                       fr=(fi == 0), la=(fi == FT - 1):
                       nc.tensor.matmul(PS(b), st[:, f * 128:(f + 1) * 128],
                                        lnown[:, f * NQ + c * 512: f * NQ + c * 512 + 512],
                                        start=fr, stop=la),
                       reads=(f"strip{r}", f"lno_{fi}"), writes=(ptoks[cb],),
                       nodrain=('A' if fi != 0 else False))
        sch.op("vector", lambda b=qpb[0], f=fo: nc.vector.tensor_scalar(
            out=qb, in0=psum_h[:, b * 512:(b + 2) * 512],
            scalar1=bq_t[:, f:f + 1], scalar2=None, op0=OP.add),
            reads=ptoks + ("c_bq",), writes=("qb",))
        for (d0, d1, sg0, sg1, sgn) in ((0, 32, 32, 64, -1.0), (32, 64, 0, 32, 1.0),
                                        (64, 96, 96, 128, -1.0), (96, 128, 64, 96, 1.0)):
            sch.op("vector", lambda a=d0, b=d1, c=sg0, d=sg1, s=sgn:
                   nc.vector.tensor_scalar(
                       out=x2q[a:b, :], in0=qb[c:d, :], scalar1=s,
                       scalar2=None, op0=OP.mult),
                   reads=("qb",), writes=("x2q",))
        sch.op("vector", lambda: nc.vector.tensor_tensor(
            out=qb, in0=qb, in1=cqc, op=OP.mult),
            reads=("qb", "c_cqc"), writes=("qb",))
        sch.op("vector", lambda: nc.vector.tensor_tensor(
            out=x2q, in0=x2q, in1=cqs, op=OP.mult),
            reads=("x2q", "c_cqs"), writes=("x2q",))
        for half in range(2):
            sch.op("vector", lambda h=half, f=fo: nc.vector.tensor_tensor(
                out=Qg[64 * h:64 * h + 64, f * NQ:(f + 1) * NQ],
                in0=qb[64 * h:64 * h + 64, :], in1=x2q[64 * h:64 * h + 64, :],
                op=OP.add),
                reads=("qb", "x2q"), writes=("qg",))
    sch.barrier()
    A.free(o_lnown, FT * NQ * 2)
    A.free(o_qb, NQ * 4); A.free(o_x2q, NQ * 4)
    A.free(o_cqc, NQ * 4); A.free(o_cqs, NQ * 4)
    if "qpack" in DEBUG or "qpk" in DEBUG:
        sch.op("sync", lambda: nc.sync.dma_start(out=ddbg["qpack"], in_=Qg),
               reads=("qg",), writes=("dqp",), dma="dbg2")
        sch.barrier()

    o_ckc = A.alloc(T * 4); o_cks = A.alloc(T * 4)
    ckc = A.view(o_ckc, 0, 128, T, F32)
    cks = A.view(o_cks, 0, 128, T, F32)
    const_load("c_ckc", ckc, dckc)
    const_load("c_cks", cks, dcks)
    o_K = A.alloc(4 * T * 2)
    o_Ksw = A.alloc(4 * T * 2)
    kpk = A.view(o_K, 0, 128, 4 * T, BF16)
    ksw = A.view(o_Ksw, 0, 128, 4 * T, BF16)
    o_V = A.alloc(16 * KVD * 2)
    vtm = A.view(o_V, 0, 128, 16 * KVD, BF16)
    o_kb = A.alloc(T * 4); o_x2k = A.alloc(T * 4)
    kb = A.view(o_kb, 0, 128, T, F32)
    x2k = A.view(o_x2k, 0, 128, T, F32)
    o_vbf = A.alloc(T * 2)
    vbf = A.view(o_vbf, 0, 128, T, BF16)
    kvp = (4, 5, 6, 7)
    kvptoks = tuple(f"ps{b}" for b in kvp)
    for m, dW, bias_t, btok in ((("k", dWk, bk_t, "c_bk"),) if "konly" in DEBUG else (("k", dWk, bk_t, "c_bk"), ("v", dWv, bv_t, "c_bv"))):
        for fv in range(4):
            r = fv % 2
            strip = emit_strip(r, dW[fv * 128:(fv + 1) * 128, :], r)
            for fi in range(FT):
                for cb in range(4):
                    sch.op("tensor", lambda b=kvp[cb], st=strip, f=fi, c=cb,
                           fr=(fi == 0), la=(fi == FT - 1):
                           nc.tensor.matmul(PS(b), st[:, f * 128:(f + 1) * 128],
                                            ln1[:, f * T + c * 512: f * T + c * 512 + 512],
                                            start=fr, stop=la),
                           reads=(f"strip{r}", f"ln1_{fi}"), writes=(f"ps{kvp[cb]}",),
                           nodrain=('A' if fi != 0 else False))
            if m == "k":
                sch.op("vector", lambda f=fv, bt=bias_t: nc.vector.tensor_scalar(
                    out=kb, in0=psum_h[:, 4 * 512: 8 * 512],
                    scalar1=bt[:, f:f + 1], scalar2=None, op0=OP.add),
                    reads=kvptoks + (btok,), writes=("kb",))
                for (d0, d1, sg0, sg1, sgn) in ((0, 32, 32, 64, -1.0), (32, 64, 0, 32, 1.0),
                                                (64, 96, 96, 128, -1.0), (96, 128, 64, 96, 1.0)):
                    sch.op("vector", lambda a=d0, b=d1, c=sg0, d=sg1, s=sgn:
                           nc.vector.tensor_scalar(
                               out=x2k[a:b, :], in0=kb[c:d, :], scalar1=s,
                               scalar2=None, op0=OP.mult),
                           reads=("kb",), writes=("x2k",))
                sch.op("vector", lambda: nc.vector.tensor_tensor(
                    out=kb, in0=kb, in1=ckc, op=OP.mult),
                    reads=("kb", "c_ckc"), writes=("kb",))
                sch.op("vector", lambda: nc.vector.tensor_tensor(
                    out=x2k, in0=x2k, in1=cks, op=OP.mult),
                    reads=("x2k", "c_cks"), writes=("x2k",))
                sch.op("vector", lambda f=fv: nc.vector.tensor_tensor(
                    out=kpk[:, f * T:(f + 1) * T], in0=kb, in1=x2k, op=OP.add),
                    reads=("kb", "x2k"), writes=("kpack",))
            else:
                sch.op("vector", lambda f=fv, bt=bias_t: nc.vector.tensor_scalar(
                    out=vbf, in0=psum_h[:, 4 * 512: 8 * 512],
                    scalar1=bt[:, f:f + 1], scalar2=None, op0=OP.add),
                    reads=kvptoks + (btok,), writes=("vbf",))
                for ktg in range(4):
                    for ii in range(4):
                        kt = ktg * 4 + ii
                        sch.op("tensor", lambda k=kt, i2=ii:
                               nc.tensor.matmul(PS(2, i2 * 128, (i2 + 1) * 128),
                                                vbf[:, k * 128:(k + 1) * 128], I128b,
                                                start=True, stop=True),
                               reads=("vbf", "c_I128b"), writes=("ps2",),
                               nodrain="B")
                    sch.op("vector", lambda kg=ktg, f=fv: nc.vector.tensor_copy(
                        vtm.rearrange("p (kt v) -> p kt v", v=KVD)[
                            :, kg * 4:(kg + 1) * 4, f * 128:(f + 1) * 128],
                        PS(2).rearrange("p (a b) -> p a b", b=128)),
                        reads=("ps2",), writes=("vtm",))
    if "qpk" in DEBUG:
        sch.barrier()
        sch.op("sync", lambda: nc.sync.dma_start(out=ddbg["qpack2"], in_=Qg),
               reads=("qg",), writes=("dqp2",), dma="dbg3")
        sch.barrier()
    sch.op("sync", lambda: nc.sync.dma_start(out=ksw[0:64, :], in_=kpk[64:128, :]),
           reads=("kpack",), writes=("kswap",), dma="ksw")
    sch.op("sync", lambda: nc.sync.dma_start(out=ksw[64:128, :], in_=kpk[0:64, :]),
           reads=("kpack",), writes=("kswap",), dma="ksw")
    A.free(o_kb, T * 4); A.free(o_x2k, T * 4); A.free(o_vbf, T * 2)
    A.free(o_ckc, T * 4); A.free(o_cks, T * 4)
    sch.barrier()

    if "kpack" in DEBUG:
        sch.op("sync", lambda: nc.sync.dma_start(out=ddbg["kpack"], in_=kpk),
               reads=("kpack",), writes=("dkp",), dma="dbg2")
        sch.op("sync", lambda: nc.sync.dma_start(out=ddbg["vtm"], in_=vtm),
               reads=("vtm",), writes=("dvt",), dma="dbg2")
        sch.barrier()

    A.free(o_BIG, FT * T * 2)
    o_y = A.alloc(16 * NQ * 2)
    yv = A.view(o_y, 0, 128, 16 * NQ, BF16)
    o_maskT = A.alloc(NBLK * 1024 * 2)
    maskT = A.view(o_maskT, 0, 128, NBLK * 1024, BF16)
    o_PT = A.alloc(16 * 512 * 2)
    PT = A.view(o_PT, 0, 128, 16 * 512, BF16)
    o_maskS = A.alloc(NBLK * 256 * 2)
    maskS = A.view(o_maskS, 0, 128, NBLK * 256, BF16)
    const_load("c_maskS", maskS, dmaskS)
    const_load("c_maskT", maskT, dmaskT)
    o_sel4 = A.alloc(512 * 4)
    sel4f = A.view(o_sel4, 0, 4, 512, F32)
    const_load("c_sel4f", sel4f, dconst[0:4, 400:912])
    sch.barrier()
    o_nm4 = A.alloc(4 * 4); o_Z4 = A.alloc(4 * 4)
    o_lnZ = A.alloc(4 * 4); o_nC = A.alloc(4 * 4)
    nm4 = A.view(o_nm4, 0, 128, 4, F32)
    Z4 = A.view(o_Z4, 0, 128, 4, F32)
    lnZ = A.view(o_lnZ, 0, 128, 4, F32)
    nC4 = A.view(o_nC, 0, 128, 4, F32)
    o_nCT = A.alloc(512)
    nCT = A.view(o_nCT, 0, 4, 128, F32)
    o_ncb = A.alloc(512 * 4)
    ncb = A.view(o_ncb, 0, 128, 512, F32)

    o_nm4b = A.alloc(4 * 4); o_Z4b = A.alloc(4 * 4)
    o_lnZb = A.alloc(4 * 4); o_nCb = A.alloc(4 * 4)
    o_nCTb = A.alloc(512); o_ncbb = A.alloc(512 * 4)
    o_cm4a = A.alloc(16 * 4); o_cm4b = A.alloc(16 * 4)
    o_stscr = A.alloc(512 * 4)
    o_stscr2 = A.alloc(512 * 4)
    nm4s = (nm4, A.view(o_nm4b, 0, 128, 4, F32))
    cm4s = (A.view(o_cm4a, 0, 128, 16, F32), A.view(o_cm4b, 0, 128, 16, F32))
    stscr_bufs = (A.view(o_stscr, 0, 128, 512, F32),
                  A.view(o_stscr2, 0, 128, 512, F32))
    Z4s = (Z4, A.view(o_Z4b, 0, 128, 4, F32))
    lnZs = (lnZ, A.view(o_lnZb, 0, 128, 4, F32))
    nC4s = (nC4, A.view(o_nCb, 0, 128, 4, F32))
    nCTs = (nCT, A.view(o_nCTb, 0, 4, 128, F32))
    ncbs = (ncb, A.view(o_ncbb, 0, 128, 512, F32))
    POSG = {0: 0, 2: 1, 1: 2, 3: 3}

    def emit_attn_S(h, qt, par):
        hc = h // 2
        NK = NKEYS[qt]
        ncb_nch = (NK + 511) // 512
        sptoks = tuple(f"ps{b}" for b in range(ncb_nch))
        nm4v, Z4v, lnZv, nC4v, nCTv, ncbv = (
            nm4s[par], Z4s[par], lnZs[par], nC4s[par], nCTs[par], ncbs[par])
        cm4v = cm4s[par]
        rot = (ncb_nch <= 2)
        for g in range(4):
            gp = g % 2
            base = 64 * gp
            b0 = (g % 2) * 2 if rot else 0
            gptoks = tuple(f"ps{b0 + b}" for b in range(ncb_nch))
            kbuf, ktok = (kpk, "kpack") if (h % 2) == gp else (ksw, "kswap")
            lhsq = Qg[base:base + 64,
                      (2 * h + g // 2) * NQ + qt * 128: (2 * h + g // 2) * NQ + qt * 128 + 128]
            for cb in range(ncb_nch):
                n = min(512, NK - cb * 512)
                sch.op("tensor", lambda b=b0 + cb, lq=lhsq, kv=kbuf, nn=n,
                       c=cb, bs=base, h2=hc:
                       nc.tensor.matmul(
                           PS(b, 0, nn), lq,
                           kv[bs:bs + 64, h2 * T + c * 512: h2 * T + c * 512 + nn],
                           start=True, stop=True),
                       reads=("qg", ktok), writes=(f"ps{b0+cb}",),
                       nodrain="C")
            c0 = b0 * 512
            for cb in range(ncb_nch - 1):
                n = min(512, NK - cb * 512)
                sch.op("vector", lambda b=b0 + cb, g2=g, c=cb, nn=n, v=cm4v:
                       nc.vector.tensor_reduce(
                           out=v[:, g2 * 4 + c:g2 * 4 + c + 1],
                           in_=PS(b, 0, nn), axis=AX.X, op=OP.max),
                       reads=(f"ps{b0+cb}",), writes=(f"cm{g}_{par}",))
            sch.op("vector", lambda q2=qt, nk=NK, cc=c0:
                   nc.vector.tensor_tensor(
                       out=psum_h[:, cc + nk - 256: cc + nk],
                       in0=psum_h[:, cc + nk - 256: cc + nk],
                       in1=maskS[:, q2 * 256:(q2 + 1) * 256], op=OP.add),
                   reads=(f"ps{b0+ncb_nch-1}",) + ((f"ps{b0+ncb_nch-2}",) if NK % 512 == 256 and ncb_nch > 1 else ()),
                   writes=(f"ps{b0+ncb_nch-1}",) + ((f"ps{b0+ncb_nch-2}",) if NK % 512 == 256 and ncb_nch > 1 else ()))
            lb = ncb_nch - 1
            ln_ = NK - lb * 512
            sch.op("vector", lambda b=b0 + lb, g2=g, c=lb, nn=ln_, v=cm4v:
                   nc.vector.tensor_reduce(
                       out=v[:, g2 * 4 + c:g2 * 4 + c + 1],
                       in_=PS(b, 0, nn), axis=AX.X, op=OP.max),
                   reads=(f"ps{b0+lb}",), writes=(f"cm{g}_{par}",))
            sch.op("vector", lambda g2=g, nch=ncb_nch, v=cm4v, o=nm4v:
                   nc.vector.tensor_reduce(
                       out=o[:, g2:g2 + 1], in_=v[:, g2 * 4:g2 * 4 + nch],
                       axis=AX.X, op=OP.max, negate=True),
                   reads=(f"cm{g}_{par}",), writes=(f"nm{g}_{par}",))
            sch.op("scalar", lambda g2=g, nk=NK, v=nm4v, zv=Z4v, cc=c0:
                   nc.scalar.activation(
                       out=psum_h[:, cc:cc + nk], in_=psum_h[:, cc:cc + nk],
                       func=AF.Exp, bias=v[:, g2:g2 + 1],
                       accum_out=zv[:, g2:g2 + 1]),
                   reads=gptoks + (f"nm{g}_{par}", "acttab"),
                   writes=gptoks + (f"Z{g}_{par}",))
        zt = tuple(f"Z{g}_{par}" for g in range(4))
        nmt = tuple(f"nm{g}_{par}" for g in range(4))
        sch.op("scalar", lambda zv=Z4v, lv=lnZv: nc.scalar.activation(
            out=lv, in_=zv, func=AF.Ln),
            reads=zt + ("acttab",), writes=(f"lnZ_{par}",))
        sch.op("vector", lambda nv=nC4v, mv=nm4v, lv=lnZv: nc.vector.tensor_tensor(
            out=nv, in0=mv, in1=lv, op=OP.subtract),
            reads=nmt + (f"lnZ_{par}",), writes=(f"nC_{par}",))

    def emit_negc_tensor(par):
        nC4v, nCTv, ncbv = nC4s[par], nCTs[par], ncbs[par]
        sch.op("tensor", lambda nv=nC4v: nc.tensor.matmul(
            PS(6, 0, 128, 0, 4), nv, I128f, start=True, stop=True),
            reads=(f"nC_{par}", "c_I128f"), writes=("ps6",), nodrain="E")
        sch.op("vector", lambda tv=nCTv: nc.vector.tensor_copy(
            tv, PS(6, 0, 128, 0, 4)),
            reads=("ps6",), writes=(f"nCT_{par}",))
        for g in range(4):
            pos = POSG[g]
            sch.op("tensor", lambda g2=g, p2=pos, tv=nCTv: nc.tensor.matmul(
                PS(6, p2 * 128, (p2 + 1) * 128),
                sel4f[:, g2 * 128:(g2 + 1) * 128], tv,
                start=True, stop=True),
                reads=(f"nCT_{par}", "c_sel4f"), writes=("ps6",), nodrain="E")
        sch.op("vector", lambda bv=ncbv: nc.vector.tensor_copy(bv, PS(6)),
               reads=("ps6",), writes=(f"ncbt_{par}",))

    def emit_attn_STAV(h, qt, par):
        hc = h // 2
        NK = NKEYS[qt]
        nkt = NK // 128
        ncbv = ncbs[par]
        emit_negc_tensor(par)
        for kt in range(nkt):
            pbA = 4 if kt % 2 == 0 else 0
            pbB = pbA + 1
            for g in range(4):
                gp = g % 2
                base = 64 * gp
                pb = pbA if gp == 0 else pbB
                col = (POSG[g] % 2) * 128
                kbuf, ktok = (kpk, "kpack") if (h % 2) == gp else (ksw, "kswap")
                sch.op("tensor", lambda b=pb, c2=col, k=kt, kv=kbuf, bs=base,
                       h2=hc, hh=h, q2=qt, g2=g:
                       nc.tensor.matmul(
                           PS(b, c2, c2 + 128),
                           kv[bs:bs + 64, h2 * T + k * 128: h2 * T + (k + 1) * 128],
                           Qg[bs:bs + 64, (2 * hh + g2 // 2) * NQ + q2 * 128:
                              (2 * hh + g2 // 2) * NQ + q2 * 128 + 128],
                           start=True, stop=True),
                       reads=("qg", ktok), writes=(f"ps{pb}",),
                       nodrain="D")
            span = psum_h[:, pbA * 512: pbA * 512 + 1024].rearrange(
                "p (b c) -> p b c", c=512)[:, :, 0:256]
            ptoks2 = (f"ps{pbA}", f"ps{pbB}")
            stscr = stscr_bufs[kt % 2]
            sttok = f"stscr{kt % 2}"
            sch.op("vector", lambda sp=span, bv=ncbv, sc=stscr:
                   nc.vector.tensor_tensor(
                       out=sc.rearrange("p (b c) -> p b c", c=256), in0=sp,
                       in1=bv.rearrange("p (b c) -> p b c", c=256), op=OP.add),
                   reads=ptoks2 + (f"ncbt_{par}",), writes=(sttok,))
            if kt >= nkt - 2:
                kt2 = kt - (nkt - 2)
                sch.op("vector", lambda q2=qt, k2=kt2, sc=stscr:
                       nc.vector.tensor_tensor(
                           out=sc, in0=sc,
                           in1=maskT[:, q2 * 1024 + k2 * 512:
                                     q2 * 1024 + (k2 + 1) * 512],
                           op=OP.add),
                       reads=(sttok, "c_maskT"), writes=(sttok,))
            sch.op("scalar", lambda k=kt, sc=stscr: nc.scalar.activation(
                out=PT[:, k * 512:(k + 1) * 512], in_=sc, func=AF.Exp),
                reads=(sttok, "acttab"), writes=(f"PT{kt}",))
        for gp2 in range(2):
            for gg in range(2):
                g = 2 * gp2 + gg
                colb = 64 * gg
                pos = POSG[g]
                for kt in range(nkt):
                    sch.op("tensor", lambda k=kt, p2=pos, cb2=colb, hh=h,
                           fr=(kt == 0), la=(kt == nkt - 1):
                           nc.tensor.matmul(
                               PS(7, 0, 128, cb2, cb2 + 64),
                               vtm[:, k * 512 + hh * 64: k * 512 + (hh + 1) * 64],
                               PT[:, k * 512 + p2 * 128: k * 512 + (p2 + 1) * 128],
                               start=fr, stop=la, tile_position=(0, cb2)),
                           reads=(f"PT{kt}", "vtm"), writes=("ps7",),
                           nodrain=('A' if kt != 0 else False))
            sch.op("vector", lambda gp3=gp2, hh=h, q2=qt: nc.vector.tensor_copy(
                yv[:, (2 * hh + gp3) * NQ + q2 * 128: (2 * hh + gp3) * NQ + q2 * 128 + 128],
                PS(7, 0, 128)),
                reads=("ps7",), writes=(f"y_{2*h+gp2}",))

    ITERS = [(h, qt) for h in range(HKV) for qt in range(NBLK)]
    for i, (h, qt) in enumerate(ITERS):
        emit_attn_S(h, qt, i % 2)
        emit_attn_STAV(h, qt, i % 2)

    sch.barrier()
    A.free(o_PT, 16 * 512 * 2)
    A.free(o_K, 4 * T * 2); A.free(o_Ksw, 4 * T * 2)
    A.free(o_V, 16 * KVD * 2)
    A.free(o_Qg, 16 * NQ * 2)
    A.free(o_nm4, 16); A.free(o_Z4, 16); A.free(o_lnZ, 16); A.free(o_nC, 16)
    A.free(o_nCT, 512); A.free(o_ncb, 2048)
    A.free(o_maskS, NBLK * 256 * 2); A.free(o_maskT, NBLK * 1024 * 2)
    A.free(o_sel4, 512 * 4)

    if "y" in DEBUG:
        sch.op("sync", lambda: nc.sync.dma_start(out=ddbg["y"], in_=yv),
               reads=tuple(f"y_{i}" for i in range(16)), writes=("dy",), dma="dbg2")
        sch.barrier()

    o_xown = A.alloc(FT * NQ * 4)
    x_own = A.view(o_xown, 0, 128, FT * NQ, F32)
    sch.op("sync", lambda: nc.sync.dma_start(
        out=x_own.rearrange("p (a b) -> p a b", b=NQ),
        in_=bass.AP(hxo, 0, [[NQ, 128], [128 * NQ, FT], [1, NQ]])),
        writes=("xown",), dma="xod")
    for fo in range(FT):
        r = fo % 2
        strip = emit_strip(r, dWo[fo * 128:(fo + 1) * 128, :], r)
        qpb = (2 + 2 * r, 3 + 2 * r)
        ptoks = (f"ps{qpb[0]}", f"ps{qpb[1]}")
        for fi in range(FT):
            for cb in range(2):
                sch.op("tensor", lambda b=qpb[cb], st=strip, f=fi, c=cb,
                       fr=(fi == 0), la=(fi == FT - 1):
                       nc.tensor.matmul(PS(b), st[:, f * 128:(f + 1) * 128],
                                        yv[:, f * NQ + c * 512: f * NQ + c * 512 + 512],
                                        start=fr, stop=la),
                       reads=(f"strip{r}", f"y_{fi}"), writes=(ptoks[cb],),
                       nodrain=('A' if fi != 0 else False))
        sch.op("vector", lambda b=qpb[0], f=fo: nc.vector.scalar_tensor_tensor(
            out=x_own[:, f * NQ:(f + 1) * NQ],
            in0=psum_h[:, b * 512:(b + 2) * 512],
            scalar=bo_t[:, f:f + 1], in1=x_own[:, f * NQ:(f + 1) * NQ],
            op0=OP.add, op1=OP.add),
            reads=ptoks + ("c_bo", "xown"), writes=("xown",))
    sch.barrier()
    A.free(o_y, 16 * NQ * 2)

    if "xattn" in DEBUG:
        sch.op("sync", lambda: nc.sync.dma_start(out=ddbg["xattn"], in_=x_own),
               reads=("xown",), writes=("dxa",), dma="dbg2")
        sch.barrier()

    A.free(o_wb[0], T * 2); A.free(o_wb[1], T * 2)
    A.free(o_strip[0], T * 2); A.free(o_strip[1], T * 2)
    o_ln2 = A.alloc(FT * NQ * 2)
    ln2v = A.view(o_ln2, 0, 128, FT * NQ, BF16)
    emit_ln("l2", ("sbuf", x_own, "xown"), NQ, ln2v, "ln2_")
    ln2toks = tuple(f"ln2_{i}" for i in range(FT))

    if "ln2" in DEBUG:
        sch.op("sync", lambda: nc.sync.dma_start(out=ddbg["ln2"], in_=ln2v),
               reads=ln2toks, writes=("dl2",), dma="dbg2")
        sch.barrier()

    o_wgr = A.alloc(FT * 8 * 2)
    wgr = A.view(o_wgr, 0, 128, FT * 8, BF16)
    sch.op("gpsimd", lambda: nc.gpsimd.dma_start(
        out=wgr.rearrange("p (a b) -> p a b", b=8),
        in_=bass.AP(hWg_r, 0, [[NEXP, 128], [128 * NEXP, FT], [1, NEXP]])),
        writes=("wgr",), dma="wgr")
    for fi in range(FT):
        for cb in range(2):
            sch.op("tensor", lambda f=fi, c=cb, fr=(fi == 0), la=(fi == FT - 1):
                   nc.tensor.matmul(PS(c, 0, 512, 0, 8),
                                    wgr[:, f * 8:(f + 1) * 8],
                                    ln2v[:, f * NQ + c * 512: f * NQ + c * 512 + 512],
                                    start=fr, stop=la),
                   reads=("wgr", f"ln2_{fi}"), writes=(f"ps{cb}",),
                   nodrain=('A' if fi != 0 else False))
    o_lg = A.alloc(NQ * 4)
    lg = A.view(o_lg, 0, 8, NQ, F32)
    sch.op("vector", lambda: nc.vector.tensor_copy(
        lg, psum_h[0:8, 0:1024]), reads=("ps0", "ps1"), writes=("lg",))
    for tt in range(8):
        sch.op("tensor", lambda t=tt: nc.tensor.matmul(
            PS(2, t * 8, (t + 1) * 8), lg[:, t * 128:(t + 1) * 128],
            I128f[0:8, 0:8], start=True, stop=True),
            reads=("lg", "c_I128f"), writes=("ps2",), nodrain="E")
    o_lgt = A.alloc(64 * 4); o_elg = A.alloc(64 * 4)
    o_m1 = A.alloc(8 * 4); o_m1n = A.alloc(8 * 4); o_m2 = A.alloc(8 * 4)
    o_zr = A.alloc(8 * 4); o_msk = A.alloc(64 * 4); o_lg2 = A.alloc(64 * 4)
    o_cmb = A.alloc(64 * 4)
    lgt = A.view(o_lgt, 0, 128, 64, F32)
    elg = A.view(o_elg, 0, 128, 64, F32)
    m1 = A.view(o_m1, 0, 128, 8, F32)
    m1n = A.view(o_m1n, 0, 128, 8, F32)
    m2v = A.view(o_m2, 0, 128, 8, F32)
    zr = A.view(o_zr, 0, 128, 8, F32)
    msk = A.view(o_msk, 0, 128, 64, F32)
    lg2 = A.view(o_lg2, 0, 128, 64, F32)
    cmb = A.view(o_cmb, 0, 128, 64, F32)
    sch.op("vector", lambda: nc.vector.tensor_copy(lgt, PS(2, 0, 64)),
           reads=("ps2",), writes=("lgt",))
    for tt in range(8):
        sl = slice(tt * 8, (tt + 1) * 8)
        sch.op("vector", lambda t=tt, s2=sl: nc.vector.tensor_reduce(
            out=m1[:, t:t + 1], in_=lgt[:, s2], axis=AX.X, op=OP.max),
            reads=("lgt",), writes=(f"m1_{tt}",))
        sch.op("vector", lambda t=tt: nc.vector.tensor_scalar(
            out=m1n[:, t:t + 1], in0=m1[:, t:t + 1], scalar1=-1.0,
            scalar2=None, op0=OP.mult),
            reads=(f"m1_{tt}",), writes=(f"m1n_{tt}",))
        sch.op("scalar", lambda t=tt, s2=sl: nc.scalar.activation(
            out=elg[:, s2], in_=lgt[:, s2], func=AF.Exp,
            bias=m1n[:, t:t + 1], accum_out=zr[:, t:t + 1]),
            reads=("lgt", f"m1n_{tt}", "acttab"), writes=(f"elg_{tt}", f"zr_{tt}"))
        sch.op("vector", lambda t=tt: nc.vector.reciprocal(
            out=zr[:, t:t + 1], in_=zr[:, t:t + 1]),
            reads=(f"zr_{tt}",), writes=(f"zr_{tt}",))
        sch.op("vector", lambda t=tt, s2=sl: nc.vector.tensor_scalar(
            out=msk[:, s2], in0=lgt[:, s2], scalar1=m1[:, t:t + 1],
            scalar2=None, op0=OP.is_ge),
            reads=("lgt", f"m1_{tt}"), writes=(f"msk_{tt}",))
        sch.op("vector", lambda t=tt, s2=sl: nc.vector.scalar_tensor_tensor(
            out=lg2[:, s2], in0=msk[:, s2], scalar=-30000.0,
            in1=lgt[:, s2], op0=OP.mult, op1=OP.add),
            reads=(f"msk_{tt}", "lgt"), writes=(f"lg2_{tt}",))
        sch.op("vector", lambda t=tt, s2=sl: nc.vector.tensor_reduce(
            out=m2v[:, t:t + 1], in_=lg2[:, s2], axis=AX.X, op=OP.max),
            reads=(f"lg2_{tt}",), writes=(f"m2_{tt}",))
        sch.op("vector", lambda t=tt, s2=sl: nc.vector.tensor_scalar(
            out=lg2[:, s2], in0=lg2[:, s2], scalar1=m2v[:, t:t + 1],
            scalar2=None, op0=OP.is_ge),
            reads=(f"lg2_{tt}", f"m2_{tt}"), writes=(f"lg2_{tt}",))
        sch.op("vector", lambda t=tt, s2=sl: nc.vector.tensor_tensor(
            out=msk[:, s2], in0=msk[:, s2], in1=lg2[:, s2], op=OP.add),
            reads=(f"msk_{tt}", f"lg2_{tt}"), writes=(f"msk_{tt}",))
        sch.op("vector", lambda t=tt, s2=sl: nc.vector.scalar_tensor_tensor(
            out=cmb[:, s2], in0=elg[:, s2], scalar=zr[:, t:t + 1],
            in1=msk[:, s2], op0=OP.mult, op1=OP.mult),
            reads=(f"elg_{tt}", f"zr_{tt}", f"msk_{tt}"), writes=(f"cmb_{tt}",))
    for tt in range(8):
        sch.op("tensor", lambda t=tt: nc.tensor.matmul(
            PS(4 + t // 4, (t % 4) * 128, (t % 4 + 1) * 128, 0, 8),
            cmb[:, t * 8:(t + 1) * 8], I128f, start=True, stop=True),
            reads=(f"cmb_{tt}", "c_I128f"), writes=(f"ps{4 + tt//4}",),
            nodrain="E")
    o_cfm = A.alloc(NQ * 4)
    cfm = A.view(o_cfm, 0, 8, NQ, F32)
    sch.op("vector", lambda: nc.vector.tensor_copy(
        cfm, psum_h[0:8, 4 * 512: 6 * 512]),
        reads=("ps4", "ps5"), writes=("cfm",))
    if "comb" in DEBUG:
        sch.op("sync", lambda: nc.sync.dma_start(out=ddbg["comb"], in_=cfm),
               reads=("cfm",), writes=("dcm",), dma="dbg2")
    o_sel8 = A.alloc(1024 * 4)
    sel8f = A.view(o_sel8, 0, 8, 1024, F32)
    const_load("c_sel8f", sel8f, dconst2)
    o_call = A.alloc(NEXP * NQ * 2)
    call = A.view(o_call, 0, 128, NEXP * NQ, BF16)
    for e in range(NEXP):
        for cb in range(2):
            sch.op("tensor", lambda e2=e, c=cb: nc.tensor.matmul(
                PS(6 + c), sel8f[:, e2 * 128:(e2 + 1) * 128],
                cfm[:, c * 512:(c + 1) * 512], start=True, stop=True),
                reads=("cfm", "c_sel8f"), writes=(f"ps{6+cb}",), nodrain="E")
        sch.op("vector", lambda e2=e: nc.vector.tensor_copy(
            call[:, e2 * NQ:(e2 + 1) * NQ], psum_h[:, 6 * 512: 8 * 512]),
            reads=("ps6", "ps7"), writes=(f"call{e}",))
    sch.barrier()

    sch.op("scalar", lambda: load_act_set(18), writes=("acttab2",))
    for _ in range(4):
        sch.op("scalar", lambda: nc.scalar.activation(
            out=warm, in_=warm, func=AF.Silu),
            reads=("acttab2", "c_fc"), writes=("warm",))
    sch.barrier()

    NCH = 11
    o_h = A.alloc(NCH * NQ * 2)
    hsb = A.view(o_h, 0, 128, NCH * NQ, BF16)
    o_wgc = [A.alloc(2 * T * 2), A.alloc(2 * T * 2)]
    o_wuc = [A.alloc(2 * T * 2), A.alloc(2 * T * 2)]
    o_wdc = [A.alloc(NCH * 128 * 2), A.alloc(NCH * 128 * 2)]
    o_sg = [A.alloc(NQ * 2), A.alloc(NQ * 2)]

    def ffn_col_ap(handle, base_off, FH, c, n2=1):
        return bass.AP(handle, base_off + c * 128,
                       [[FH, 128], [128 * FH, FT], [1, n2 * 128]])

    def ffn_wd_ap(handle, base_off, row0, fo):
        return bass.AP(handle, base_off + row0 * E + fo * 128,
                       [[E, 128], [128 * E, NCH], [1, 128]])

    def emit_ffn(tag, wg_src, wu_src, wd_src, cexp):
        for ci in range(NCH):
            r = ci % 2
            gb = 4 * r
            rb = (ci // 2) % 2
            half = ci % 2
            wgc = A.view(o_wgc[rb], 0, 128, 2 * T, BF16)
            wuc = A.view(o_wuc[rb], 0, 128, 2 * T, BF16)
            sgv = A.view(o_sg[r], 0, 128, NQ, BF16)
            if half == 0:
                nch2 = 2 if ci + 1 < NCH else 1
                sch.op("gpsimd", lambda v=wgc, c=ci, f=wg_src, n2=nch2:
                       nc.gpsimd.dma_start(
                           out=v.rearrange("p (a b) -> p a b", b=128)[:, :, 0:n2 * 128]
                           if False else
                           v.rearrange("p (a b) -> p a b", b=256)[:, :, 0:n2 * 128],
                           in_=f(c, n2)),
                       writes=(f"wgc{rb}",), dma=f"wgc{rb}")
                sch.op("gpsimd", lambda v=wuc, c=ci, f=wu_src, n2=nch2:
                       nc.gpsimd.dma_start(
                           out=v.rearrange("p (a b) -> p a b", b=256)[:, :, 0:n2 * 128],
                           in_=f(c, n2)),
                       writes=(f"wuc{rb}",), dma=f"wuc{rb}")
            for wtok, wv, b0 in ((f"wgc{rb}", wgc, gb), (f"wuc{rb}", wuc, gb + 2)):
                for fi in range(FT):
                    for cb in range(2):
                        sch.op("tensor", lambda b=b0 + cb, w=wv, f=fi, c=cb,
                               hf=half, fr=(fi == 0), la=(fi == FT - 1):
                               nc.tensor.matmul(
                                   PS(b), w[:, f * 256 + hf * 128: f * 256 + hf * 128 + 128],
                                   ln2v[:, f * NQ + c * 512: f * NQ + c * 512 + 512],
                                   start=fr, stop=la),
                               reads=(wtok, f"ln2_{fi}"), writes=(f"ps{b0+cb}",),
                               nodrain=('A' if fi != 0 else False))
            sch.op("scalar", lambda b=gb, s2=sgv: nc.scalar.activation(
                out=s2, in_=psum_h[:, b * 512:(b + 2) * 512], func=AF.Silu),
                reads=(f"ps{gb}", f"ps{gb+1}", "acttab2"), writes=(f"sg{r}",))
            sch.op("vector", lambda b=gb + 2, s2=sgv, c=ci:
                   nc.vector.tensor_tensor(
                       out=hsb[:, c * NQ:(c + 1) * NQ], in0=s2,
                       in1=psum_h[:, b * 512:(b + 2) * 512], op=OP.mult),
                   reads=(f"sg{r}", f"ps{gb+2}", f"ps{gb+3}"), writes=(f"h{ci}",))
            if cexp is not None:
                sch.op("vector", lambda c=ci, e=cexp: nc.vector.tensor_tensor(
                    out=hsb[:, c * NQ:(c + 1) * NQ],
                    in0=hsb[:, c * NQ:(c + 1) * NQ],
                    in1=call[:, e * NQ:(e + 1) * NQ], op=OP.mult),
                    reads=(f"h{ci}", f"call{e}"), writes=(f"h{ci}",))
        htoks = tuple(f"h{i}" for i in range(NCH))
        for fo in range(FT):
            r = fo % 2
            db = (2 + 2 * r, 3 + 2 * r)
            wdc = A.view(o_wdc[r], 0, 128, NCH * 128, BF16)
            sch.op("gpsimd", lambda v=wdc, f2=fo, f=wd_src: nc.gpsimd.dma_start(
                out=v.rearrange("p (a b) -> p a b", b=128), in_=f(f2)),
                writes=(f"wdc{r}",), dma=f"wdc{r}")
            for ci in range(NCH):
                for cb in range(2):
                    sch.op("tensor", lambda b=db[cb], w=wdc, c=ci, c2=cb,
                           fr=(ci == 0), la=(ci == NCH - 1):
                           nc.tensor.matmul(
                               PS(b), w[:, c * 128:(c + 1) * 128],
                               hsb[:, c * NQ + c2 * 512: c * NQ + c2 * 512 + 512],
                               start=fr, stop=la),
                           reads=(f"wdc{r}",) + htoks, writes=(f"ps{db[cb]}",),
                           nodrain=('A' if ci != 0 else False))
            sch.op("vector", lambda b=db[0], f=fo: nc.vector.tensor_tensor(
                out=x_own[:, f * NQ:(f + 1) * NQ],
                in0=psum_h[:, b * 512:(b + 2) * 512],
                in1=x_own[:, f * NQ:(f + 1) * NQ], op=OP.add),
                reads=(f"ps{db[0]}", f"ps{db[1]}", "xown"), writes=("xown",))

    emit_ffn("s0", lambda c, n2: ffn_col_ap(hWsg, 0, FH_S, c, n2),
             lambda c, n2: ffn_col_ap(hWsu, 0, FH_S, c, n2),
             lambda fo: ffn_wd_ap(hWsd, 0, 0, fo), None)
    emit_ffn("s1", lambda c, n2: ffn_col_ap(hWsg, 0, FH_S, NCH + c, n2),
             lambda c, n2: ffn_col_ap(hWsu, 0, FH_S, NCH + c, n2),
             lambda fo: ffn_wd_ap(hWsd, 0, NCH * 128, fo), None)
    for e in range(NEXP):
        emit_ffn(f"e{e}",
                 lambda c, n2, e2=e: ffn_col_ap(hWg, e2 * E * FH_E, FH_E, c, n2),
                 lambda c, n2, e2=e: ffn_col_ap(hWu, e2 * E * FH_E, FH_E, c, n2),
                 lambda fo, e2=e: ffn_wd_ap(hWd, e2 * FH_E * E, 0, fo), e)
    sch.barrier()

    sch.op("sync", lambda: nc.sync.dma_start(
        out=bass.AP(hout, 0, [[NQ, 128], [128 * NQ, FT], [1, NQ]]),
        in_=x_own.rearrange("p (a b) -> p a b", b=NQ)),
        reads=("xown",), writes=("outd",), dma="out")
    sch.barrier()
    sch.finalize()
    sch.plan()
    eng_sems = {}
    dma_sems = {}
    cms = []
    for e in ENGINES:
        cm = nc.semaphore(f"sem_{e}")
        eng_sems[e] = cm.__enter__()
        cms.append(cm)
    for k in sch.dma_keys:
        cm = nc.semaphore(f"dma_{k}")
        dma_sems[k] = cm.__enter__()
        cms.append(cm)
    my_sems = [s.num for s in eng_sems.values()] + [s.num for s in dma_sems.values()]
    lo, hi = min(my_sems), max(my_sems)
    assert set(my_sems) == set(range(lo, hi + 1)), "non-contiguous sems"

    nc.all_engine_barrier()
    nc.gpsimd.dma_reset(range(lo, hi + 1))
    nc.gpsimd.sem_clear(range(lo, hi + 1))
    nc.all_engine_barrier()

    eng_objs = dict(sync=nc.sync, gpsimd=nc.gpsimd, scalar=nc.scalar,
                    vector=nc.vector, tensor=nc.tensor)
    for e in ENGINES:
        sch.emit(e, eng_objs[e], eng_sems, dma_sems)

    nc.all_engine_barrier()
    nc.gpsimd.dma_reset(range(lo, hi + 1))
    nc.gpsimd.sem_clear(range(lo, hi + 1))
    nc.all_engine_barrier()
    return nc


def _host_tables(own_blocks):
    pos_own = (np.asarray(own_blocks)[:, None] * 128 + np.arange(128)[None, :]).ravel()
    j = np.arange(D // 2, dtype=np.float64)
    theta = 10000.0 ** (-2.0 * j / D)

    def tables(pos, scale):
        ang = (pos[None, :].astype(np.float64) + 1.0) * theta[:, None]
        C = np.concatenate([ang, ang], axis=0)
        cos = (np.cos(C) * scale).astype(np.float32)
        sin = (np.sin(C) * scale).astype(np.float32)
        return np.concatenate([cos, cos], 0), np.concatenate([sin, sin], 0)

    cqc, cqs = tables(pos_own, 0.125)
    ckc, cks = tables(np.arange(T), 1.0)
    return cqc, cqs, ckc, cks


def _host_masks(own_blocks):
    NEG = np.float32(-30000.0)
    q = np.arange(128)
    mS = np.zeros((128, NBLK * 256), np.float32)
    mT = np.zeros((128, NBLK * 1024), np.float32)
    for jslot in range(NBLK):
        g = own_blocks[jslot]
        nk = NKEYS[jslot]
        k = nk - 256 + np.arange(256)
        valid = k[None, :] <= (g * 128 + q)[:, None]
        mS[:, jslot * 256:(jslot + 1) * 256] = np.where(valid, 0.0, NEG)
        for kt2 in range(2):
            kk = nk - 256 + kt2 * 128 + np.arange(128)
            validT = kk[:, None] <= (g * 128 + q)[None, :]
            tile = np.where(validT, 0.0, NEG)
            for g4 in range(4):
                mT[:, jslot * 1024 + kt2 * 512 + g4 * 128:
                     jslot * 1024 + kt2 * 512 + (g4 + 1) * 128] = tile
    return mS.astype(ml_dtypes.bfloat16), mT.astype(ml_dtypes.bfloat16)


def _host_consts():
    ib = np.eye(128, dtype=ml_dtypes.bfloat16)
    raw = np.zeros((128, 4096), np.uint8)
    raw[:, 0:128 * 2] = ib.view(np.uint8).reshape(128, 256)
    i32 = np.eye(128, dtype=np.float32)
    raw[:, 256:256 + 512] = i32.view(np.uint8).reshape(128, 512)
    ones_b = np.ones((1, 128), dtype=ml_dtypes.bfloat16)
    raw[0:1, 768:768 + 256] = ones_b.view(np.uint8).reshape(1, 256)
    ones_f = np.ones((1, 128), dtype=np.float32)
    raw[0:1, 1024:1024 + 512] = ones_f.view(np.uint8).reshape(1, 512)
    onecol = np.ones((128, 1), dtype=ml_dtypes.bfloat16)
    raw[:, 1536:1538] = onecol.view(np.uint8).reshape(128, 2)
    fconst = np.zeros((128, 8), np.float32)
    fconst[:, 1] = EPS
    raw[:, 1568:1568 + 32] = fconst.view(np.uint8).reshape(128, 32)
    sel4 = np.zeros((4, 512), np.float32)
    for g in range(4):
        sel4[g, g * 128:(g + 1) * 128] = 1.0
    raw[0:4, 1600:1600 + 2048] = sel4.view(np.uint8).reshape(4, 2048)
    return raw.view(np.float32)


def _host_consts2():
    sel8 = np.zeros((8, 1024), np.float32)
    for e in range(8):
        sel8[e, e * 128:(e + 1) * 128] = 1.0
    return sel8


_PROG_CACHE = {}


def kernel(L0_attn_x, L0_attn_Wq, L0_attn_bq, L0_attn_Wk, L0_attn_bk,
           L0_attn_Wv, L0_attn_bv, L0_attn_Wo, L0_attn_bo,
           L0_ffn_W_g, L0_ffn_Wg, L0_ffn_Wu, L0_ffn_Wd,
           L0_ffn_Ws_gate, L0_ffn_Ws_up, L0_ffn_Ws_down):
    x = np.asarray(L0_attn_x, np.float32)
    consts = _host_consts()
    weights = dict(
        Wq=np.asarray(L0_attn_Wq, np.float32), Wk=np.asarray(L0_attn_Wk, np.float32),
        Wv=np.asarray(L0_attn_Wv, np.float32), Wo=np.asarray(L0_attn_Wo, np.float32),
        W_g=np.asarray(L0_ffn_W_g, np.float32),
        Wg=np.asarray(L0_ffn_Wg, np.float32), Wu=np.asarray(L0_ffn_Wu, np.float32),
        Wd=np.asarray(L0_ffn_Wd, np.float32),
        Ws_gate=np.asarray(L0_ffn_Ws_gate, np.float32),
        Ws_up=np.asarray(L0_ffn_Ws_up, np.float32),
        Ws_down=np.asarray(L0_ffn_Ws_down, np.float32),
    )
    bq = np.asarray(L0_attn_bq, np.float32).reshape(FT, 128).T.copy()
    bk = np.asarray(L0_attn_bk, np.float32).reshape(4, 128).T.copy()
    bv = np.asarray(L0_attn_bv, np.float32).reshape(4, 128).T.copy()
    bo = np.asarray(L0_attn_bo, np.float32).reshape(FT, 128).T.copy()

    consts2 = _host_consts2()
    in_maps = []
    metas = []
    for c in range(8):
        s = c // 2
        own = OWN_A if c % 2 == 0 else OWN_B
        x_fm = np.ascontiguousarray(x[s].T)
        tok = (np.asarray(own)[:, None] * 128 + np.arange(128)[None, :]).ravel()
        x_own_fm = np.ascontiguousarray(x_fm[:, tok])
        cqc, cqs, ckc, cks = _host_tables(own)
        mS, mT = _host_masks(own)
        m = dict(weights)
        m.update(x_fm=x_fm, x_own_fm=x_own_fm, bq_t=bq, bk_t=bk, bv_t=bv,
                 bo_t=bo, cq_cos=cqc, cq_sin=cqs, ck_cos=ckc, ck_sin=cks,
                 maskS=mS, maskT=mT, consts=consts, consts2=consts2)
        in_maps.append(m)
        metas.append((s, own))

    key = "prog"
    if key not in _PROG_CACHE:
        _PROG_CACHE[key] = build_program()
    nc = _PROG_CACHE[key]

    res = run_bass_kernel_spmd(nc, in_maps, core_ids=list(range(8)))

    out = np.empty((B, T, E), np.float32)
    for c in range(8):
        s, own = metas[c]
        tok = (np.asarray(own)[:, None] * 128 + np.arange(128)[None, :]).ravel()
        out[s, tok, :] = res.results[c]["out_fm"].T
    if DEBUG:
        kernel.debug = [{k: v for k, v in r.items() if k.startswith("dbg_")}
                        for r in res.results]
    return out
